# version 03
# baseline (speedup 1.0000x reference)
import numpy as np
from contextlib import ExitStack

import concourse.bass as bass
import concourse.mybir as mybir
from concourse.bass_utils import run_bass_kernel_spmd

F32 = mybir.dt.float32
F32R = mybir.dt.float32r
BF16 = mybir.dt.bfloat16
AF = mybir.ActivationFunctionType
ALU = mybir.AluOpType
AXX = mybir.AxisListType.X

B_, TF, D = 4, 2048, 2048
TH = TF // 2
DI, DXB, NST, DTR, KC = 4096, 1024, 16, 128, 4
NH = NST // 2
DFF = 4 * D
H, BH = DI // 16, DI // 64

TC = 256
NCH = TF // TC
FC = 512
NFC = TH // FC
ZC = 512
NZC = TH // ZC
NPF = DXB + BH * NH + H * NH
XO, BO, CO = 0, DXB, DXB + BH * NH
MT_X, MT_B, MT_C = DXB // 128, (BH * NH) // 128, (H * NH) // 128
KT_D = D // 128
MT_F = DFF // 128
MT_D = D // 128
CTI = DI // 128

EPS5, EPS6, EPSW = 1e-5, 1e-6, 1e-6
QB = 128.0

_NPHASE = 5


class Sem:
    def __init__(self, st, nc, name):
        self.h = st.enter_context(nc.semaphore(name))
        self.n = 0


def build_program():
    nc = bass.Bass("TRN2", target_bir_lowering=False)
    dt = nc.dram_tensor

    x0 = dt("x0", [D, TH], F32, kind="ExternalInput").ap()
    wm_sh = dt("wm_sh", [D // 4, NPF], F32, kind="ExternalInput").ap()
    wz_sh = dt("wz_sh", [D // 8, DI], F32, kind="ExternalInput").ap()
    wop_sh = dt("wop_sh", [DI // 8, D], F32, kind="ExternalInput").ap()
    wfc_sh = dt("wfc_sh", [D // 8, DFF], F32, kind="ExternalInput").ap()
    wpj_sh = dt("wpj_sh", [DFF // 8, D], F32, kind="ExternalInput").ap()
    wdtin_i = dt("wdtin", [D, DTR], F32, kind="ExternalInput").ap()
    wdtp_i = dt("wdtp", [DTR, DI], F32, kind="ExternalInput").ap()
    alog_i = dt("alog", [128, 256], F32, kind="ExternalInput").ap()
    pconv_i = dt("pconv", [64, KC * 16 * 128], F32R, kind="ExternalInput").ap()
    convb_i = dt("convb", [128, 32], F32, kind="ExternalInput").ap()
    dvec_i = dt("dvec", [128, 64], F32, kind="ExternalInput").ap()
    dtb_i = dt("dtb", [128, 32], F32, kind="ExternalInput").ap()
    p_sp_i = dt("p_sp", [128, 8 * 128], F32R, kind="ExternalInput").ap()
    p_w_i = dt("p_w", [128, 8 * 128], BF16, kind="ExternalInput").ap()
    p_b_i = dt("p_b", [128, 16 * 128], BF16, kind="ExternalInput").ap()
    p_r_i = dt("p_r", [128, 8 * 128], F32R, kind="ExternalInput").ap()
    ones_i = dt("ones_i", [128, 128], F32, kind="ExternalInput").ap()
    ident_i = dt("ident", [128, 128], F32, kind="ExternalInput").ap()
    out = dt("out", [D, TH], F32, kind="ExternalOutput").ap()

    b_wfc = dt("b_wfc", [2, 64, 128, 128], BF16).ap()
    g_wfc = dt("g_wfc", [16, 64, 128, 128], BF16).ap()
    b_wpj = dt("b_wpj", [8, 16, 128, 128], BF16).ap()
    g_wpj = dt("g_wpj", [64, 16, 128, 128], BF16).ap()
    b_wm = dt("b_wm", [4, 28, 128, 128], F32R).ap()
    g_wm = dt("g_wm", [16, 28, 128, 128], F32R).ap()
    b_wz = dt("b_wz", [2, 32, 128, 128], F32R).ap()
    g_wz = dt("g_wz", [16, 32, 128, 128], F32R).ap()
    b_wop = dt("b_wop", [4, 16, 128, 128], F32R).ap()
    g_wop = dt("g_wop", [32, 16, 128, 128], F32R).ap()
    g_wdtin = dt("g_wdtin", [16, 1, 128, 128], F32R).ap()
    g_wdtp = dt("g_wdtp", [1, 32, 128, 128], F32R).ap()
    b_gam = dt("b_gam", [1, 32], F32).ap()
    g_gam = dt("g_gam", [8, 32], F32).ap()
    x1me = dt("x1me", [D, TH], F32).ap()
    x1g = dt("x1g", [2, D, TH], F32).ap()
    ybuf = dt("ybuf", [2, DI, TH], F32).ap()
    ysum_b = dt("ysum", [DI, TH], F32).ap()
    x2buf = dt("x2buf", [D, TH], F32).ap()

    dbg = {}
    if _NPHASE == 1:
        dbg["xq"] = dt("dbg_xq", [D, FC], BF16, kind="ExternalOutput").ap()
        dbg["hq"] = dt("dbg_hq", [DFF, FC], BF16, kind="ExternalOutput").ap()
        dbg["rows"] = dt("dbg_rows", [1, 8 * FC], F32, kind="ExternalOutput").ap()
        dbg["rdup"] = dt("dbg_rdup", [128, 3 * FC], F32, kind="ExternalOutput").ap()
    if _NPHASE < 5:
        dbg["x1g"] = dt("dbg_x1g", [2 * D, TH], F32, kind="ExternalOutput").ap()
        dbg["msc"] = dt("dbg_msc", [2, 8 * FC], F32, kind="ExternalOutput").ap()
    if 3 <= _NPHASE < 5:
        dbg["spn"] = dt("dbg_spn", [DI, TC], F32, kind="ExternalOutput").ap()
        dbg["ypart"] = dt("dbg_ypart", [DI, TF], F32, kind="ExternalOutput").ap()
    if 4 <= _NPHASE < 5:
        dbg["x2"] = dt("dbg_x2", [D, TH], F32, kind="ExternalOutput").ap()

    st = ExitStack()
    with st:
        sb = nc.sbuf_tensor
        psal = nc.psum_tensor

        s_pe = Sem(st, nc, "s_pe")
        s_dv = Sem(st, nc, "s_dv")
        s_ac = Sem(st, nc, "s_ac")
        s_edv = Sem(st, nc, "s_edv")
        s_eac = Sem(st, nc, "s_eac")
        s_cst = Sem(st, nc, "s_cst")
        s_gw = Sem(st, nc, "s_gw")
        s_gs = Sem(st, nc, "s_gs")
        s_tw = Sem(st, nc, "s_tw")
        s_x = Sem(st, nc, "s_x")
        s_out = Sem(st, nc, "s_out")
        s_st = [Sem(st, nc, f"s_st{i}") for i in range(2)]
        s_ys = [Sem(st, nc, f"s_ys{i}") for i in range(2)]
        s_zy = [Sem(st, nc, f"s_zy{i}") for i in range(2)]
        s_dbg = Sem(st, nc, "s_dbg")
        WR_N = 2
        s_wsl = [Sem(st, nc, f"s_wsl{i}") for i in range(WR_N)]
        s_ccl = {}
        for name in ("gam", "wfc", "wpj", "wm", "wz", "wop", "x1", "y"):
            s_ccl[name] = Sem(st, nc, f"s_cc_{name}")

        _ectr = {"dv": 0, "ac": 0}

        def edge_dv(e, inst):
            _ectr["dv"] += 1
            inst.then_inc(s_edv.h, 1)
            e.wait_ge(s_edv.h, _ectr["dv"])

        def edge_ac(e, inst):
            _ectr["ac"] += 1
            inst.then_inc(s_eac.h, 1)
            e.wait_ge(s_eac.h, _ectr["ac"])

        ident = st.enter_context(sb("identt", [128, 128], F32))
        ones_f = st.enter_context(sb("ones_f", [128, 128], F32))
        ones_r = st.enter_context(sb("ones_r", [128, 128], F32R))
        ones_b = st.enter_context(sb("ones_b", [128, 128], BF16))
        gamv = st.enter_context(sb("gamv", [128, 6], F32))
        gsc = st.enter_context(sb("gsc", [1, 8], F32))
        cb_t = st.enter_context(sb("cb_t", [128, 32], F32))
        dv_t = st.enter_context(sb("dv_t", [128, 64], F32))
        dtbp = st.enter_context(sb("dtbp", [128, 32], F32))
        cst = st.enter_context(sb("cst", [1, 4], F32))
        cstp = st.enter_context(sb("cstp", [128, 4], F32))
        gpart_keep = st.enter_context(sb("gpart", [1, 32], F32))
        wr_slabs = [st.enter_context(sb(f"wrs{i}", [128, 16 * 128], F32R)) for i in range(WR_N)]
        wr_state = {"iss": 0, "done": []}

        def wr_mark(n, v):
            for _ in range(n):
                wr_state["done"].append(v)

        def new_ops():
            return {e: [] for e in ("sync", "tensor", "vector", "scalar", "gpsimd")}

        def run_block(o):
            with nc.Block() as blk:
                for e in ("sync", "tensor", "vector", "scalar", "gpsimd"):
                    if o[e]:
                        getattr(blk, e)(lambda eng, l=tuple(o[e]): [f(eng) for f in l])

        def wpanel(o, g_ap, m, k0, kt, dtype, wait_cc=None):
            slot = wr_state["iss"] % WR_N
            need_idx = wr_state["iss"] - WR_N
            need = wr_state["done"][need_idx] if need_idx >= 0 else 0
            wr_state["iss"] += 1
            sem = s_wsl[slot]

            def ld(e, slot=slot, need=need, m=m, k0=k0, kt=kt, wcc=wait_cc, sem=sem, g_ap=g_ap, dtype=dtype):
                if wcc is not None:
                    e.wait_ge(wcc[0].h, wcc[1])
                if need > 0:
                    e.wait_ge(s_pe.h, need)
                if dtype is BF16:
                    dst = wr_slabs[slot][:, :].bitcast(BF16)[:, 0:kt * 128]
                else:
                    dst = wr_slabs[slot][:, 0:kt * 128]
                e.dma_start(out=dst.rearrange("p (k c) -> p k c", k=kt),
                            in_=g_ap[k0:k0 + kt, m].transpose([1, 0, 2])).then_inc(sem.h, 16)

            o["sync"].append(ld)
            sem.n += 16
            if dtype is BF16:
                view = wr_slabs[slot][:, :].bitcast(BF16)[:, 0:kt * 128]
            else:
                view = wr_slabs[slot][:, 0:kt * 128]
            return view, sem, sem.n

        cc_done = {}
        with ExitStack() as stp:
            o = new_ops()
            gred = stp.enter_context(sb("gred", [128, 16], F32))
            gta = stp.enter_context(sb("gta", [128, 4096], F32))
            gtb = stp.enter_context(sb("gtb", [128, 2048], F32))
            g8 = stp.enter_context(sb("g8", [1, 256], F32))
            twork = stp.enter_context(sb("twork", [128, 8192], F32))
            tout_b = stp.enter_context(sb("tout_b", [128, 8192], BF16))
            tout_r = stp.enter_context(sb("tout_r", [128, 8192], F32R))
            ps_g = stp.enter_context(psal("ps_g", [128, 16], F32))
            ps_t = stp.enter_context(psal("ps_t", [128, 128], F32))

            def p0_sync(e):
                e.dma_start(out=ident[:, :], in_=ident_i).then_inc(s_cst.h, 16)
                e.dma_start(out=ones_f[:, :], in_=ones_i).then_inc(s_cst.h, 16)
                e.dma_start(out=cb_t[:, :], in_=convb_i).then_inc(s_cst.h, 16)
                e.dma_start(out=dv_t[:, :], in_=dvec_i).then_inc(s_cst.h, 16)
                e.dma_start(out=dtbp[:, :], in_=dtb_i).then_inc(s_cst.h, 16)

            o["sync"].append(p0_sync)
            s_cst.n += 5 * 16
            v_cst = s_cst.n

            def p0_cst0(e):
                e.memset(gred[:, :], 0.0)
                e.memset(cst[0:1, 0:1], EPS5)
                e.memset(cst[0:1, 1:2], EPS6)
                e.memset(cstp[:, 0:1], EPS5)
                e.memset(cstp[:, 1:2], EPS6)
                e.memset(cstp[:, 2:3], 1.0).then_inc(s_dv.h, 1)

            o["vector"].append(p0_cst0)
            s_dv.n += 1

            def p0_cstA(e):
                e.wait_ge(s_cst.h, v_cst)
                e.activation(out=ones_r[:, :], in_=ones_f[:, :], func=AF.Copy)
                e.activation(out=ones_b[:, :], in_=ones_f[:, :], func=AF.Copy).then_inc(s_ac.h, 1)

            o["scalar"].append(p0_cstA)
            s_ac.n += 1
            v_cstA = s_ac.n

            gjobs = [(wfc_sh[128 * i:128 * (i + 1), :], 8192, i) for i in range(2)] + \
                    [(wpj_sh[128 * i:128 * (i + 1), :], 2048, 4 + i) for i in range(8)]
            for src_ap, cols, slot in gjobs:
                free_at = s_dv.n

                def ld(e, src_ap=src_ap, cols=cols, free_at=free_at):
                    e.wait_ge(s_dv.h, free_at)
                    e.dma_start(out=twork[:, 0:cols], in_=src_ap).then_inc(s_gw.h, 16)
                o["sync"].append(ld)
                s_gw.n += 16
                v = s_gw.n

                def red(e, cols=cols, slot=slot, v=v):
                    e.wait_ge(s_gw.h, v)
                    src, n, first, ping = twork, cols, True, 0
                    while n > 1:
                        n2 = n // 2
                        dst = (gta, gtb)[ping]
                        ii = e.tensor_reduce(out=dst[:, 0:n2],
                                             in_=src[:, 0:n].rearrange("p (k c) -> p c k", k=2),
                                             axis=AXX, op=ALU.add, apply_absolute_value=first)
                        edge_dv(e, ii)
                        src, n, first, ping = dst, n2, False, 1 - ping
                    e.tensor_copy(out=gred[:, slot:slot + 1], in_=src[:, 0:1]).then_inc(s_dv.h, 1)
                o["vector"].append(red)
                s_dv.n += 1
            v_gred = s_dv.n

            def p0_gmrg(e):
                e.wait_ge(s_dv.h, v_gred)
                for i in range(4):
                    ii = e.tensor_add(out=gred[:, 4 + i:5 + i], in0=gred[:, 4 + i:5 + i],
                                      in1=gred[:, 8 + i:9 + i])
                e.tensor_add(out=gred[:, 12:13], in0=gred[:, 0:1], in1=gred[:, 1:2])
                edge_dv(e, ii)
                e.tensor_add(out=gred[:, 13:14], in0=gred[:, 4:5], in1=gred[:, 6:7])
                ii = e.tensor_add(out=gred[:, 14:15], in0=gred[:, 5:6], in1=gred[:, 7:8])
                edge_dv(e, ii)
                ii = e.tensor_add(out=gred[:, 13:14], in0=gred[:, 13:14], in1=gred[:, 14:15])
                edge_dv(e, ii)
                e.tensor_scalar(out=gred[:, 12:16], in0=gred[:, 12:16], scalar1=1.0 / (D * DFF),
                                scalar2=None, op0=ALU.mult).then_inc(s_dv.h, 1)

            o["vector"].append(p0_gmrg)
            s_dv.n += 1
            v_gmrg = s_dv.n

            def p0_gmm(e):
                e.wait_ge(s_dv.h, v_gmrg)
                e.wait_ge(s_ac.h, v_cstA)
                e.wait_ge(s_cst.h, 16)
                e.transpose(ps_t[0:4, 0:128], gred[:, 12:16], ident[:, :]).then_inc(s_pe.h, 1)

            o["tensor"].append(p0_gmm)
            s_pe.n += 1
            v_gmm = s_pe.n

            def p0_gcp0(e):
                e.wait_ge(s_pe.h, v_gmm)
                src, n, ping = ps_t, 128, 0
                while n > 1:
                    n2 = n // 2
                    dst = (gta, gtb)[ping]
                    ii = e.tensor_reduce(out=dst[0:4, 0:n2],
                                         in_=src[0:4, 0:n].rearrange("p (k c) -> p c k", k=2),
                                         axis=AXX, op=ALU.add)
                    edge_dv(e, ii)
                    src, n, ping = dst, n2, 1 - ping
                e.memset(gpart_keep[0:1, 0:32], 0.0).then_inc(s_dv.h, 1)

            o["vector"].append(p0_gcp0)
            s_dv.n += 1
            v_gsum = s_dv.n

            w_gst = s_out.n + 16

            def p0_gst(e):
                e.wait_ge(s_dv.h, v_gsum)
                e.dma_start(out=b_gam, in_=gpart_keep[0:1, 0:32]).then_inc(s_out.h, 16)
                e.wait_ge(s_out.h, w_gst)
                e.dma_start(out=b_gam[0:1, 12:16], in_=gta[0:4, 0:1]).then_inc(s_out.h, 16)
                e.wait_ge(s_out.h, w_gst + 16)
                e.collective_compute("AllGather", ALU.bypass, replica_groups=[list(range(8))],
                                     ins=[b_gam.opt()], outs=[g_gam.opt()]).then_inc(s_ccl["gam"].h, 1)

            o["gpsimd"].append(p0_gst)
            s_out.n += 32
            s_ccl["gam"].n += 1

            def p0_gld(e):
                e.wait_ge(s_ccl["gam"].h, 1)
                e.dma_start(out=g8[0:1, 0:256], in_=g_gam.rearrange("r c -> (r c)")).then_inc(s_gs.h, 16)

            o["sync"].append(p0_gld)
            s_gs.n += 16

            def p0_gfold(e):
                e.wait_ge(s_gs.h, 16)
                ii = e.tensor_reduce(out=gta[0:1, 0:128],
                                     in_=g8[0:1, 0:256].rearrange("o (k c) -> o c k", k=2),
                                     axis=AXX, op=ALU.add)
                edge_dv(e, ii)
                ii = e.tensor_reduce(out=gtb[0:1, 0:64],
                                     in_=gta[0:1, 0:128].rearrange("o (k c) -> o c k", k=2),
                                     axis=AXX, op=ALU.add)
                edge_dv(e, ii)
                ii = e.tensor_reduce(out=gta[0:1, 0:32],
                                     in_=gtb[0:1, 0:64].rearrange("o (k c) -> o c k", k=2),
                                     axis=AXX, op=ALU.add)
                edge_dv(e, ii)
                e.tensor_scalar(out=gta[0:1, 0:32], in0=gta[0:1, 0:32],
                                scalar1=float(1.0 - 2.0 ** -23), scalar2=None,
                                op0=ALU.mult).then_inc(s_dv.h, 1)

            o["vector"].append(p0_gfold)
            s_dv.n += 1
            v_gfold = s_dv.n

            def p0_gst2(e):
                e.wait_ge(s_dv.h, v_gfold)
                e.dma_start(out=b_gam, in_=gta[0:1, 0:32]).then_inc(s_gs.h, 16)

            o["gpsimd"].append(p0_gst2)
            s_gs.n += 16

            def p0_gld2(e):
                e.wait_ge(s_gs.h, 32)
                e.dma_start(out=gsc[:, 0:2], in_=b_gam[0:1, 12:14]).then_inc(s_gs.h, 16)

            o["sync"].append(p0_gld2)
            s_gs.n += 16

            def p0_gdup(e):
                e.wait_ge(s_gs.h, 48)
                e.matmul(ps_g[:, 0:2], ones_f[0:1, :], gsc[0:1, 0:2], start=True, stop=True).then_inc(s_pe.h, 1)

            o["tensor"].append(p0_gdup)
            s_pe.n += 1
            v_gdup = s_pe.n

            def p0_gcp(e):
                e.wait_ge(s_pe.h, v_gdup)
                ii = e.activation(out=gamv[:, 0:2], in_=ps_g[:, 0:2], func=AF.Ln, bias=cstp[:, 1:2])
                edge_ac(e, ii)
                e.activation(out=gamv[:, 0:2], in_=gamv[:, 0:2], func=AF.Exp, scale=-1.0).then_inc(s_ac.h, 1)

            o["scalar"].append(p0_gcp)
            s_ac.n += 1
            v_gamv_ac = s_ac.n

            def p0_gcp2(e):
                e.wait_ge(s_ac.h, v_gamv_ac)
                ii = e.tensor_scalar(out=gamv[:, 2:4], in0=ps_g[:, 0:2],
                                     scalar1=EPS6, scalar2=None, op0=ALU.add)
                edge_dv(e, ii)
                for _ in range(2):
                    ii = e.tensor_mul(out=gamv[:, 4:6], in0=gamv[:, 0:2], in1=gamv[:, 2:4])
                    edge_dv(e, ii)
                    ii = e.tensor_scalar(out=gamv[:, 4:6], in0=gamv[:, 4:6],
                                         scalar1=-1.0, scalar2=2.0, op0=ALU.mult, op1=ALU.add)
                    edge_dv(e, ii)
                    ii = e.tensor_mul(out=gamv[:, 0:2], in0=gamv[:, 0:2], in1=gamv[:, 4:6])
                    edge_dv(e, ii)
                e.tensor_copy(out=gred[:, 15:16], in_=gred[:, 15:16]).then_inc(s_dv.h, 1)

            o["vector"].append(p0_gcp2)
            s_dv.n += 1
            v_gamv = s_dv.n

            def cast_job(src_ap, dst_tiled, mt, tern_col):
                st_wait = s_out.n
                dv_wait = s_dv.n

                def ld(e, w=st_wait, wd=dv_wait, mt=mt, src_ap=src_ap):
                    e.wait_ge(s_out.h, w)
                    e.wait_ge(s_dv.h, wd)
                    e.dma_start(out=twork[:, 0:mt * 128], in_=src_ap).then_inc(s_tw.h, 16)
                o["sync"].append(ld)
                s_tw.n += 16
                v = s_tw.n

                if tern_col is not None:
                    def sgn(e, v=v, mt=mt):
                        e.wait_ge(s_tw.h, v)
                        e.activation(out=tout_b[:, 0:mt * 128], in_=twork[:, 0:mt * 128],
                                     func=AF.Sign).then_inc(s_ac.h, 1)
                    o["scalar"].append(sgn)
                    s_ac.n += 1
                    va = s_ac.n

                    def msk1(e, v=v, va=va, c=tern_col, mt=mt):
                        e.wait_ge(s_tw.h, v)
                        e.wait_ge(s_dv.h, v_gamv)
                        e.wait_ge(s_ac.h, va)
                        e.tensor_scalar(out=twork[:, 0:mt * 128], in0=twork[:, 0:mt * 128],
                                        scalar1=gamv[:, c:c + 1], scalar2=None, op0=ALU.mult).then_inc(s_dv.h, 1)
                    o["vector"].append(msk1)
                    s_dv.n += 1
                    vm1 = s_dv.n

                    def mabs(e, vm1=vm1, mt=mt):
                        e.wait_ge(s_dv.h, vm1)
                        e.activation(out=twork[:, 0:mt * 128], in_=twork[:, 0:mt * 128],
                                     func=AF.Abs).then_inc(s_ac.h, 1)
                    o["scalar"].append(mabs)
                    s_ac.n += 1
                    va2 = s_ac.n

                    def msk2(e, va2=va2, mt=mt):
                        e.wait_ge(s_ac.h, va2)
                        ii = e.tensor_scalar(out=twork[:, 0:mt * 128], in0=twork[:, 0:mt * 128],
                                             scalar1=0.5, scalar2=None, op0=ALU.is_gt)
                        edge_dv(e, ii)
                        e.tensor_mul(out=tout_b[:, 0:mt * 128], in0=twork[:, 0:mt * 128],
                                     in1=tout_b[:, 0:mt * 128]).then_inc(s_dv.h, 1)
                    o["vector"].append(msk2)
                    s_dv.n += 1
                    vd, buf = s_dv.n, tout_b
                else:
                    def cv(e, v=v, mt=mt):
                        e.wait_ge(s_tw.h, v)
                        e.tensor_copy(out=tout_r[:, 0:mt * 128], in_=twork[:, 0:mt * 128]).then_inc(s_dv.h, 1)
                    o["vector"].append(cv)
                    s_dv.n += 1
                    vd, buf = s_dv.n, tout_r

                def stv(e, vd=vd, buf=buf, mt=mt, dst=dst_tiled):
                    e.wait_ge(s_dv.h, vd)
                    e.dma_start(out=dst.transpose([1, 0, 2]),
                                in_=buf[:, 0:mt * 128].rearrange("p (m c) -> p m c", m=mt)).then_inc(s_out.h, 16)
                o["gpsimd"].append(stv)
                s_out.n += 16

            for i in range(2):
                cast_job(wfc_sh[128 * i:128 * (i + 1), :], b_wfc[i], 64, 0)
            for i in range(8):
                cast_job(wpj_sh[128 * i:128 * (i + 1), :], b_wpj[i], 16, 1)
            for i in range(4):
                cast_job(wm_sh[128 * i:128 * (i + 1), :], b_wm[i], 28, None)
            for i in range(2):
                cast_job(wz_sh[128 * i:128 * (i + 1), :], b_wz[i], 32, None)
            for i in range(4):
                cast_job(wop_sh[128 * i:128 * (i + 1), :], b_wop[i], 16, None)
            for i in range(16):
                cast_job(wdtin_i[128 * i:128 * (i + 1), :], g_wdtin[i], 1, None)
            cast_job(wdtp_i, g_wdtp[0], 32, None)

            v_prep_st = s_out.n

            def p0_cc(e):
                e.wait_ge(s_out.h, v_prep_st)
                for name, i_ap, o_ap in [("wfc", b_wfc, g_wfc), ("wpj", b_wpj, g_wpj),
                                         ("wm", b_wm, g_wm), ("wz", b_wz, g_wz), ("wop", b_wop, g_wop)]:
                    rg = [[0, 2, 4, 6], [1, 3, 5, 7]] if name == "wm" else [list(range(8))]
                    e.collective_compute("AllGather", ALU.bypass, replica_groups=rg,
                                         ins=[i_ap.opt()], outs=[o_ap.opt()]).then_inc(s_ccl[name].h, 1)

            o["gpsimd"].append(p0_cc)
            for name in ("wfc", "wpj", "wm", "wz", "wop"):
                s_ccl[name].n += 1
                cc_done[name] = (s_ccl[name], 1)

            run_block(o)

        def emit_ffn(src_hbm, dst_hbm, tag, gates=()):
            with ExitStack() as stf:
                o = new_ops()
                if gates:
                    def fgate(e, gg=tuple(gates)):
                        for sh, vv in gg:
                            if vv:
                                e.wait_ge(sh, vv)
                    for _eng in ("sync", "tensor", "vector", "scalar"):
                        o[_eng].append(fgate)
                xsrc = stf.enter_context(sb(f"{tag}_xsrc", [128, KT_D * FC], F32))
                xq = stf.enter_context(sb(f"{tag}_xq", [128, KT_D * FC], BF16))
                hsl = stf.enter_context(sb(f"{tag}_h", [128, MT_F * FC], BF16))
                sqx = stf.enter_context(sb(f"{tag}_sqx", [128, 4 * FC], F32R))
                sqh = stf.enter_context(sb(f"{tag}_sqh", [128, 8 * FC], BF16))
                rows = stf.enter_context(sb(f"{tag}_rows", [1, 8 * FC], F32))
                rdup = stf.enter_context(sb(f"{tag}_rdup", [128, 3 * FC], F32))
                tre = stf.enter_context(sb(f"{tag}_tre", [128, 2 * FC], F32))
                mxt = stf.enter_context(sb(f"{tag}_mxt", [128, 32], F32))
                evs = stf.enter_context(sb(f"{tag}_evs", [128, 2 * FC], F32))
                p_mm = [stf.enter_context(psal(f"{tag}_pmm{i}", [128, FC], F32)) for i in range(2)]
                p_s1 = stf.enter_context(psal(f"{tag}_ps1", [1, FC], F32))
                p_s2 = stf.enter_context(psal(f"{tag}_ps2", [1, FC], F32))
                p_du = stf.enter_context(psal(f"{tag}_pdu", [128, FC], F32))
                p_tp = [stf.enter_context(psal(f"{tag}_ptp{i}", [128, 128], F32)) for i in range(2)]

                RS = {k: i for i, k in enumerate(["m_", "v_", "g_", "s_", "a_", "mx", "t0", "t1"])}

                def row(name):
                    i = RS[name]
                    return rows[0:1, i * FC:(i + 1) * FC]

                NTP = FC // 128
                stats_free = {"v": 0}

                def grouped_stats(slab, nkt, grp, sq_slab, bf, wait_ac, wait_ld):
                    ngrp = nkt // grp
                    for g in range(ngrp):
                        v_pe_prev = s_pe.n

                        def sq_a(e, g=g, v_pe_prev=v_pe_prev, wait_ac=wait_ac, wait_ld=wait_ld,
                                 grp=grp, slab=slab, sq_slab=sq_slab):
                            if wait_ac:
                                e.wait_ge(s_ac.h, wait_ac)
                            if wait_ld:
                                e.wait_ge(s_x.h, wait_ld)
                            if g > 0:
                                e.wait_ge(s_pe.h, v_pe_prev)
                            for i in range(grp):
                                k = g * grp + i
                                ii = e.activation(out=sq_slab[:, i * FC:(i + 1) * FC],
                                                  in_=slab[:, k * FC:(k + 1) * FC], func=AF.Square)
                            ii.then_inc(s_ac.h, 1)
                        o["scalar"].append(sq_a)
                        s_ac.n += 1
                        v_sq = s_ac.n
                        v_free = stats_free["v"]

                        def mmst(e, g=g, v_sq=v_sq, v_free=v_free, wait_ld=wait_ld, grp=grp,
                                 bf=bf, nkt=nkt, slab=slab, sq_slab=sq_slab):
                            if wait_ld:
                                e.wait_ge(s_x.h, wait_ld)
                            e.wait_ge(s_ac.h, v_sq)
                            if g == 0 and v_free:
                                e.wait_ge(s_dv.h, v_free)
                            oc1 = ones_b if bf else ones_f
                            oc2 = ones_b if bf else ones_r
                            for i in range(grp):
                                k = g * grp + i
                                e.matmul(p_s1[0:1, :], oc1[:, 0:1], slab[:, k * FC:(k + 1) * FC],
                                         start=(k == 0), stop=(k == nkt - 1), skip_group_check=True)
                                ii = e.matmul(p_s2[0:1, :], oc2[:, 0:1], sq_slab[:, i * FC:(i + 1) * FC],
                                              start=(k == 0), stop=(k == nkt - 1), skip_group_check=True)
                            ii.then_inc(s_pe.h, 1)
                        o["tensor"].append(mmst)
                        s_pe.n += 1
                    return s_pe.n

                def rowdup(slot, r_ap, wait_dv):
                    def md(e, wait_dv=wait_dv, r_ap=r_ap):
                        e.wait_ge(s_dv.h, wait_dv)
                        e.wait_ge(s_dv.h, md.vprev)
                        e.matmul(p_du[:, :], ones_f[0:1, :], r_ap, start=True, stop=True).then_inc(s_pe.h, 1)
                    md.vprev = s_dv.n
                    o["tensor"].append(md)
                    s_pe.n += 1
                    v_md = s_pe.n

                    def mcp(e, slot=slot, v_md=v_md):
                        e.wait_ge(s_pe.h, v_md)
                        e.tensor_copy(out=rdup[:, slot * FC:(slot + 1) * FC], in_=p_du[:, :]).then_inc(s_dv.h, 1)
                    o["vector"].append(mcp)
                    s_dv.n += 1
                    return s_dv.n

                def ln_rows(v_stats, nfeat, eps_a, eps_b):
                    def rows1(e, v_stats=v_stats, nfeat=nfeat):
                        e.wait_ge(s_pe.h, v_stats)
                        ii = e.tensor_scalar(out=row("m_"), in0=p_s1[0:1, :], scalar1=1.0 / nfeat,
                                             scalar2=None, op0=ALU.mult)
                        e.tensor_scalar(out=row("t0"), in0=p_s2[0:1, :], scalar1=1.0 / nfeat,
                                        scalar2=None, op0=ALU.mult)
                        edge_dv(e, ii)
                        ii = e.tensor_mul(out=row("t1"), in0=row("m_"), in1=row("m_"))
                        edge_dv(e, ii)
                        e.tensor_sub(out=row("v_"), in0=row("t0"), in1=row("t1")).then_inc(s_dv.h, 1)
                    o["vector"].append(rows1)
                    s_dv.n += 1
                    v_r1 = s_dv.n
                    stats_free["v"] = v_r1

                    def rows2(e, eps_a=eps_a):
                        e.wait_ge(s_dv.h, v_r1)
                        e.activation(out=row("t0"), in_=row("v_"), func=AF.Ln,
                                     bias=(cst[0:1, 0:1] if eps_a == EPS5 else cst[0:1, 1:2])).then_inc(s_ac.h, 1)
                    o["scalar"].append(rows2)
                    s_ac.n += 1
                    v_r2 = s_ac.n

                    if eps_b is not None:
                        def rows3(e, eps_a=eps_a):
                            e.wait_ge(s_ac.h, v_r2)
                            ii = e.tensor_scalar(out=row("t1"), in0=row("v_"), scalar1=eps_a,
                                                 scalar2=None, op0=ALU.add)
                            edge_dv(e, ii)
                            ii = e.reciprocal(out=row("t1"), in_=row("t1"))
                            edge_dv(e, ii)
                            e.tensor_mul(out=row("t1"), in0=row("t1"), in1=row("v_")).then_inc(s_dv.h, 1)
                        o["vector"].append(rows3)
                        s_dv.n += 1
                        v_r3 = s_dv.n

                        def rows4(e, eps_b=eps_b):
                            e.wait_ge(s_dv.h, v_r3)
                            e.activation(out=row("t1"), in_=row("t1"), func=AF.Ln,
                                         bias=(cst[0:1, 0:1] if eps_b == EPS5 else cst[0:1, 1:2])
                                         ).then_inc(s_ac.h, 1)
                        o["scalar"].append(rows4)
                        s_ac.n += 1
                        v_r4 = s_ac.n

                        def rows5(e):
                            e.wait_ge(s_ac.h, v_r4)
                            e.tensor_add(out=row("t0"), in0=row("t0"), in1=row("t1")).then_inc(s_dv.h, 1)
                        o["vector"].append(rows5)
                        s_dv.n += 1
                        v_r5 = s_dv.n
                    else:
                        v_r5 = None

                    def rows6(e, v_r5=v_r5, v_r2=v_r2):
                        if v_r5 is not None:
                            e.wait_ge(s_dv.h, v_r5)
                        else:
                            e.wait_ge(s_ac.h, v_r2)
                        e.activation(out=row("g_"), in_=row("t0"), func=AF.Exp, scale=-0.5).then_inc(s_ac.h, 1)
                    o["scalar"].append(rows6)
                    s_ac.n += 1
                    return v_r1, s_ac.n

                def maxabs_rows(slab, nkt, wait_sem, wait_val, v_m, v_g):
                    def redmm(e, nkt=nkt, wait_val=wait_val, wait_sem=wait_sem, slab=slab):
                        e.wait_ge(wait_sem.h, wait_val)
                        sv = slab[:, 0:nkt * FC].rearrange("p (k c) -> p c k", k=nkt)
                        e.tensor_reduce(out=tre[:, 0:FC], in_=sv, axis=AXX, op=ALU.max)
                        e.tensor_reduce(out=tre[:, FC:2 * FC], in_=sv, axis=AXX, op=ALU.min).then_inc(s_dv.h, 1)
                    o["vector"].append(redmm)
                    s_dv.n += 1
                    v_tree = s_dv.n

                    red_vals = []
                    for i in range(2 * NTP):
                        wait_v = v_tree if i < 2 else red_vals[i - 2]
                        half, blk = i % 2, i // 2

                        def tp(e, half=half, blk=blk, i=i, wait_v=wait_v):
                            e.wait_ge(s_dv.h, wait_v)
                            e.transpose(p_tp[i % 2][:, :], tre[:, half * FC + blk * 128: half * FC + (blk + 1) * 128],
                                        ident[:, :]).then_inc(s_pe.h, 1)
                        o["tensor"].append(tp)
                        s_pe.n += 1
                        v_tp = s_pe.n

                        def rd(e, half=half, i=i, v_tp=v_tp):
                            e.wait_ge(s_pe.h, v_tp)
                            e.tensor_reduce(out=mxt[:, i:i + 1], in_=p_tp[i % 2][:, :], axis=AXX,
                                            op=(ALU.min if half else ALU.max)).then_inc(s_dv.h, 1)
                        o["vector"].append(rd)
                        s_dv.n += 1
                        red_vals.append(s_dv.n)

                    cp_vals = []
                    for i in range(2 * NTP):
                        wait_v = red_vals[-1] if i < 2 else cp_vals[i - 2]
                        half, blk = i % 2, i // 2

                        def tpb(e, i=i, wait_v=wait_v):
                            e.wait_ge(s_dv.h, wait_v)
                            e.transpose(p_tp[i % 2][0:1, 0:128], mxt[:, i:i + 1], ident[:, :]).then_inc(s_pe.h, 1)
                        o["tensor"].append(tpb)
                        s_pe.n += 1
                        v_tpb = s_pe.n

                        def cpb(e, half=half, blk=blk, i=i, v_tpb=v_tpb, v_g=v_g):
                            e.wait_ge(s_pe.h, v_tpb)
                            if i < 2:
                                e.wait_ge(s_ac.h, v_g)
                            dstrow = row("t0") if half == 0 else row("t1")
                            e.tensor_copy(out=dstrow[0:1, blk * 128:(blk + 1) * 128],
                                          in_=p_tp[i % 2][0:1, 0:128]).then_inc(s_dv.h, 1)
                        o["vector"].append(cpb)
                        s_dv.n += 1
                        cp_vals.append(s_dv.n)
                    v_cp = s_dv.n

                    def mxr(e, v_m=v_m, v_cp=v_cp):
                        e.wait_ge(s_dv.h, v_cp)
                        e.wait_ge(s_dv.h, v_m)
                        e.tensor_sub(out=row("t0"), in0=row("t0"), in1=row("m_"))
                        ii = e.tensor_sub(out=row("t1"), in0=row("m_"), in1=row("t1"))
                        edge_dv(e, ii)
                        e.tensor_tensor(out=row("mx"), in0=row("t0"), in1=row("t1"), op=ALU.max).then_inc(s_dv.h, 1)
                    o["vector"].append(mxr)
                    s_dv.n += 1
                    return s_dv.n

                def squant_rows(v_mx, v_g, gam_col, extra):
                    def srow(e, v_mx=v_mx, v_g=v_g, gam_col=gam_col, extra=extra):
                        e.wait_ge(s_dv.h, v_mx)
                        e.wait_ge(s_ac.h, v_g)
                        e.reciprocal(out=row("s_"), in_=row("mx"))
                        ii = e.tensor_mul(out=row("a_"), in0=row("g_"), in1=row("mx"))
                        edge_dv(e, ii)
                        e.tensor_scalar(out=row("s_"), in0=row("s_"), scalar1=QB, scalar2=None, op0=ALU.mult)
                        ii = e.tensor_scalar(out=row("a_"), in0=row("a_"),
                                             scalar1=gsc[0:1, gam_col:gam_col + 1],
                                             scalar2=extra, op0=ALU.mult, op1=ALU.mult)
                        edge_dv(e, ii)
                        e.tensor_mul(out=row("t0"), in0=row("m_"), in1=row("s_")).then_inc(s_dv.h, 1)
                    o["vector"].append(srow)
                    s_dv.n += 1
                    return s_dv.n

                def quantize(slab, nkt, out_q, v_sd, v_msd):
                    def qz(e, nkt=nkt, v_sd=v_sd, v_msd=v_msd, slab=slab, out_q=out_q):
                        e.wait_ge(s_dv.h, max(v_sd, v_msd))
                        for k in range(nkt):
                            ii = e.tensor_mul(out=out_q[:, k * FC:(k + 1) * FC],
                                              in0=slab[:, k * FC:(k + 1) * FC], in1=rdup[:, 0:FC])
                        edge_dv(e, ii)
                        for k in range(nkt):
                            ii = e.tensor_sub(out=out_q[:, k * FC:(k + 1) * FC],
                                              in0=out_q[:, k * FC:(k + 1) * FC], in1=rdup[:, FC:2 * FC])
                        edge_dv(e, ii)
                        e.tensor_scalar(out=out_q[:, 0:nkt * FC], in0=out_q[:, 0:nkt * FC],
                                        scalar1=-QB, scalar2=QB, op0=ALU.max, op1=ALU.min).then_inc(s_dv.h, 1)
                    o["vector"].append(qz)
                    s_dv.n += 1
                    return s_dv.n

                for ch in range(NFC):
                    t0c = ch * FC
                    v_free = s_dv.n
                    prev_st = [s_st[0].n, s_st[1].n]

                    def ldx(e, t0c=t0c, v_free=v_free,
                            wdbg=(s_dbg.n if (tag == "f1" and ch == 1 and "xq" in dbg) else 0)):
                        e.wait_ge(s_dv.h, v_free)
                        if wdbg:
                            e.wait_ge(s_dbg.h, wdbg)
                        e.dma_start(out=xsrc[:, :].rearrange("p (k t) -> p k t", k=KT_D),
                                    in_=src_hbm[:, t0c:t0c + FC].rearrange("(k p) t -> p k t", p=128)
                                    ).then_inc(s_x.h, 16)
                    o["sync"].append(ldx)
                    s_x.n += 16
                    v_ldx = s_x.n

                    v_stats = grouped_stats(xsrc, KT_D, 4, sqx, False, 0, v_ldx)
                    v_r1, v_g = ln_rows(v_stats, D, EPS5, EPS6)
                    v_mx = maxabs_rows(xsrc, KT_D, s_x, v_ldx, v_r1, v_g)
                    v_sq = squant_rows(v_mx, v_g, 0, 1.0 / QB)
                    v_sd = rowdup(0, row("s_"), v_sq)
                    v_msd = rowdup(1, row("t0"), v_sq)
                    v_ad = rowdup(2, row("a_"), v_sq)
                    v_xq = quantize(xsrc, KT_D, xq, v_sd, v_msd)

                    evac_vals = []
                    gl_vals = []
                    for m in range(MT_F):
                        view, wsem, v_w = wpanel(o, g_wfc, m, 0, KT_D, BF16,
                                                 wait_cc=(cc_done.get("wfc") if (tag == "f1" and ch == 0 and m == 0) else None))
                        wait_ev = evac_vals[m - 2] if m >= 2 else 0

                        def mm(e, view=view, wsem=wsem, v_w=v_w, m=m, wait_ev=wait_ev, v_xq=v_xq):
                            e.wait_ge(wsem.h, v_w)
                            e.wait_ge(s_dv.h, v_xq)
                            if wait_ev:
                                e.wait_ge(s_dv.h, wait_ev)
                            pm = p_mm[m % 2]
                            for k in range(KT_D):
                                ii = e.matmul(pm[:, :], view[:, k * 128:(k + 1) * 128],
                                              xq[:, k * FC:(k + 1) * FC], start=(k == 0), stop=(k == KT_D - 1))
                            ii.then_inc(s_pe.h, 1)
                        o["tensor"].append(mm)
                        s_pe.n += 1
                        wr_mark(1, s_pe.n)
                        v_mm = s_pe.n

                        wait_gl = gl_vals[m - 2] if m >= 2 else 0

                        def ev(e, m=m, v_mm=v_mm, v_ad=v_ad, wait_gl=wait_gl, prev_st=tuple(prev_st)):
                            e.wait_ge(s_pe.h, v_mm)
                            e.wait_ge(s_dv.h, v_ad)
                            if wait_gl:
                                e.wait_ge(s_ac.h, wait_gl)
                            if m < 2 and prev_st[m % 2]:
                                e.wait_ge(s_st[m % 2].h, prev_st[m % 2])
                            e.tensor_mul(out=evs[:, (m % 2) * FC:(m % 2 + 1) * FC], in0=p_mm[m % 2][:, :],
                                         in1=rdup[:, 2 * FC:3 * FC]).then_inc(s_dv.h, 1)
                        o["vector"].append(ev)
                        s_dv.n += 1
                        evac_vals.append(s_dv.n)
                        v_ev = s_dv.n

                        def gl(e, m=m, v_ev=v_ev):
                            e.wait_ge(s_dv.h, v_ev)
                            e.activation(out=hsl[:, m * FC:(m + 1) * FC],
                                         in_=evs[:, (m % 2) * FC:(m % 2 + 1) * FC], func=AF.Gelu).then_inc(s_ac.h, 1)
                        o["scalar"].append(gl)
                        s_ac.n += 1
                        gl_vals.append(s_ac.n)
                    v_gelu = s_ac.n

                    v_hstats = grouped_stats(hsl, MT_F, 8, sqh, True, v_gelu, 0)
                    v_hr1, v_hg = ln_rows(v_hstats, DFF, EPS6, None)
                    v_hmx = maxabs_rows(hsl, MT_F, s_ac, v_gelu, v_hr1, v_hg)
                    v_hs = squant_rows(v_hmx, v_hg, 1, 0.5 / QB)
                    v_hsd = rowdup(0, row("s_"), v_hs)
                    v_hmsd = rowdup(1, row("t0"), v_hs)
                    v_had = rowdup(2, row("a_"), v_hs)
                    v_hq = quantize(hsl, MT_F, hsl, v_hsd, v_hmsd)

                    if tag == "f1" and ch == 0 and "xq" in dbg:
                        def dqd(e, v_hq=v_hq, v_xq=v_xq):
                            e.wait_ge(s_dv.h, max(v_hq, v_xq))
                            e.dma_start(out=dbg["xq"].rearrange("(k p) t -> p k t", p=128),
                                        in_=xq[:, :].rearrange("p (k t) -> p k t", k=KT_D)).then_inc(s_dbg.h, 16)
                            e.dma_start(out=dbg["hq"].rearrange("(k p) t -> p k t", p=128),
                                        in_=hsl[:, :].rearrange("p (k t) -> p k t", k=MT_F)).then_inc(s_dbg.h, 16)
                            e.dma_start(out=dbg["rows"], in_=rows[0:1, :]).then_inc(s_dbg.h, 16)
                            e.dma_start(out=dbg["rdup"], in_=rdup[:, :]).then_inc(s_dbg.h, 16)
                        o["gpsimd"].append(dqd)
                        s_dbg.n += 64

                    ev2_vals = []
                    st_vals = [0, 0]
                    for m in range(MT_D):
                        wait_ev = ev2_vals[m - 2] if m >= 2 else 0
                        for q in range(4):
                            view, wsem, v_w = wpanel(o, g_wpj, m, 16 * q, 16, BF16,
                                                     wait_cc=(cc_done.get("wpj") if (tag == "f1" and ch == 0 and m == 0 and q == 0) else None))

                            def mm2q(e, m=m, q=q, view=view, wsem=wsem, v_w=v_w,
                                     wait_ev=wait_ev, v_hq=v_hq):
                                e.wait_ge(wsem.h, v_w)
                                if q == 0:
                                    e.wait_ge(s_dv.h, v_hq)
                                    if wait_ev:
                                        e.wait_ge(s_dv.h, wait_ev)
                                pm = p_mm[m % 2]
                                for k in range(16):
                                    ii = e.matmul(pm[:, :], view[:, k * 128:(k + 1) * 128],
                                                  hsl[:, (16 * q + k) * FC:(16 * q + k + 1) * FC],
                                                  start=(q == 0 and k == 0), stop=(q == 3 and k == 15))
                                ii.then_inc(s_pe.h, 1)
                            o["tensor"].append(mm2q)
                            s_pe.n += 1
                            wr_mark(1, s_pe.n)
                        v_mm2 = s_pe.n
                        wait_st = st_vals[m % 2]

                        def ev_p(e, m=m, v_mm2=v_mm2, v_had=v_had, wait_st=wait_st):
                            e.wait_ge(s_pe.h, v_mm2)
                            e.wait_ge(s_dv.h, v_had)
                            if wait_st:
                                e.wait_ge(s_st[m % 2].h, wait_st)
                            sl = evs[:, (m % 2) * FC:(m % 2 + 1) * FC]
                            ii = e.tensor_mul(out=sl, in0=p_mm[m % 2][:, :], in1=rdup[:, 2 * FC:3 * FC])
                            edge_dv(e, ii)
                            e.tensor_add(out=sl, in0=sl, in1=xsrc[:, m * FC:(m + 1) * FC]).then_inc(s_dv.h, 1)
                        o["vector"].append(ev_p)
                        s_dv.n += 1
                        ev2_vals.append(s_dv.n)
                        v_evp = s_dv.n

                        def stx(e, m=m, t0c=t0c, v_evp=v_evp):
                            e.wait_ge(s_dv.h, v_evp)
                            e.dma_start(out=dst_hbm[128 * m:128 * (m + 1), t0c:t0c + FC],
                                        in_=evs[:, (m % 2) * FC:(m % 2 + 1) * FC]).then_inc(s_st[m % 2].h, 16)
                        o["gpsimd"].append(stx)
                        s_st[m % 2].n += 16
                        st_vals[m % 2] = s_st[m % 2].n

                if tag == "f1" and "msc" in dbg:
                    v_last = s_dv.n

                    def dmp(e, v_last=v_last):
                        e.wait_ge(s_dv.h, v_last)
                        e.dma_start(out=dbg["msc"][0:1, :], in_=rows[0:1, :]).then_inc(s_out.h, 16)
                        e.dma_start(out=dbg["msc"][1:2, 0:2], in_=gsc[0:1, 0:2]).then_inc(s_out.h, 16)
                        e.dma_start(out=dbg["msc"][1:2, 8:10], in_=gamv[0:1, 0:2]).then_inc(s_out.h, 16)
                        e.dma_start(out=dbg["msc"][1:2, 10:12], in_=gpart_keep[0:1, 0:2]).then_inc(s_out.h, 16)
                    o["gpsimd"].append(dmp)
                    s_out.n += 64
                run_block(o)

        if _NPHASE >= 1:
            emit_ffn(x0, x1me, "f1")
            o = new_ops()
            vst0, vst1 = s_st[0].n, s_st[1].n

            def ag_x1(e):
                e.wait_ge(s_st[0].h, vst0)
                e.wait_ge(s_st[1].h, vst1)
                e.collective_compute("AllGather", ALU.bypass,
                                     replica_groups=[[2 * b, 2 * b + 1] for b in range(4)],
                                     ins=[x1me.opt()], outs=[x1g.opt()]).then_inc(s_ccl["x1"].h, 1)

            o["gpsimd"].append(ag_x1)
            s_ccl["x1"].n += 1
            cc_done["x1"] = (s_ccl["x1"], 1)

            if "x1g" in dbg:
                def dump_x1(e):
                    e.wait_ge(s_ccl["x1"].h, 1)
                    e.dma_start(out=dbg["x1g"], in_=x1g.rearrange("a d t -> (a d) t")).then_inc(s_out.h, 16)
                o["gpsimd"].append(dump_x1)
                s_out.n += 16
            run_block(o)

        def emit_mamba():
            with ExitStack() as stm:
                o = new_ops()

                def mgate(e, vo=s_out.n, v0=s_st[0].n, v1=s_st[1].n):
                    if vo:
                        e.wait_ge(s_out.h, vo)
                    if v0:
                        e.wait_ge(s_st[0].h, v0)
                    if v1:
                        e.wait_ge(s_st[1].h, v1)
                for _eng in ("sync", "tensor", "vector", "scalar"):
                    o[_eng].append(mgate)
                x1c = stm.enter_context(sb("m_x1c", [128, KT_D * TC], F32))
                sqm = stm.enter_context(sb("m_sqm", [128, 4 * TC], F32R))
                rows = stm.enter_context(sb("m_rows", [1, 4 * TC], F32))
                rdm = stm.enter_context(sb("m_rdm", [128, 2 * TC], F32))
                xs2 = [stm.enter_context(sb(f"m_xs{j}", [64, TC + 3], F32R)) for j in range(16)]
                pconv = stm.enter_context(sb("m_pconv", [64, KC * 16 * 128], F32R))
                bsrc = stm.enter_context(sb("m_bsrc", [128, MT_B * TC], BF16))
                csrc = stm.enter_context(sb("m_csrc", [128, MT_C * TC], BF16))
                rsl = stm.enter_context(sb("m_rsl", [128, TC], F32R))
                spn = stm.enter_context(sb("m_spn", [128, CTI * TC], F32R))
                wsl = stm.enter_context(sb("m_wsl", [128, 2 * TC], BF16))
                rg1 = stm.enter_context(sb("m_rg1", [128, 2 * TC], F32))
                rg2 = stm.enter_context(sb("m_rg2", [128, 2 * TC], F32))
                rg3 = stm.enter_context(sb("m_rg3", [128, 2 * TC], F32))
                xar = stm.enter_context(sb("m_xar", [128, 2 * TC], F32))
                dxe = stm.enter_context(sb("m_dxe", [128, 2 * TC], F32))
                Esl = stm.enter_context(sb("m_E", [128, 2 * TC], F32))
                wdr = stm.enter_context(sb("m_wd", [128, 2 * TC], F32))
                usl = stm.enter_context(sb("m_u", [128, 2 * TC], F32))
                Ssl = [stm.enter_context(sb(f"m_S{i}", [128, 8 * TC], F32)) for i in range(2)]
                SCr = stm.enter_context(sb("m_SC", [128, 2 * TC], F32R))
                carry = stm.enter_context(sb("m_carry", [128, 256], F32))
                yer = stm.enter_context(sb("m_ye", [128, 2 * TC], F32))
                masc = stm.enter_context(sb("m_asc", [128, 256], F32))
                p_sp = stm.enter_context(sb("m_psp", [128, 8 * 128], F32R))
                p_w = stm.enter_context(sb("m_pw", [128, 8 * 128], BF16))
                p_b = stm.enter_context(sb("m_pb", [128, 16 * 128], BF16))
                p_r = stm.enter_context(sb("m_pr", [128, 8 * 128], F32R))
                pb_ = [stm.enter_context(psal(f"m_pb{i}", [128, TC], F32)) for i in range(8)]

                def mld(e):
                    e.dma_start(out=masc[:, :], in_=alog_i).then_inc(s_cst.h, 16)
                    e.dma_start(out=pconv[:, :], in_=pconv_i).then_inc(s_cst.h, 16)
                    e.dma_start(out=p_sp[:, :], in_=p_sp_i).then_inc(s_cst.h, 16)
                    e.dma_start(out=p_r[:, :], in_=p_r_i).then_inc(s_cst.h, 16)
                    e.dma_start(out=p_w[:, :], in_=p_w_i).then_inc(s_cst.h, 16)
                    e.dma_start(out=p_b[:, :], in_=p_b_i).then_inc(s_cst.h, 16)
                o["sync"].append(mld)
                s_cst.n += 96
                v_mld = s_cst.n

                def mcst(e):
                    e.wait_ge(s_cst.h, v_mld)
                    ii = e.activation(out=masc[:, :], in_=masc[:, :], func=AF.Exp)
                    edge_ac(e, ii)
                    e.activation(out=masc[:, :], in_=masc[:, :], func=AF.Copy, scale=-1.0).then_inc(s_ac.h, 1)
                o["scalar"].append(mcst)
                s_ac.n += 1
                v_masc = s_ac.n

                def mpat(e):
                    e.wait_ge(s_cst.h, v_mld)
                    e.memset(carry[:, :], 0.0).then_inc(s_dv.h, 1)
                o["vector"].append(mpat)
                s_dv.n += 1
                v_mpat = s_dv.n

                carry_vals = [0] * 32
                silu_vals = [0] * 32
                wtt_vals = [0] * 32
                mmw7 = [0] * 32
                v_convmm_prev = 0
                v_ipmm_prev = 0
                v_scmm_prev = 0
                v_yev_prev = 0
                v_lnc_prev = 0

                for ch in range(NCH):
                    half, tc0 = ch // 4, (ch % 4) * TC

                    v_cm = v_convmm_prev

                    def halo(e, ch=ch, v_cm=v_cm):
                        if v_cm:
                            e.wait_ge(s_pe.h, v_cm)
                        if ch == 0:
                            for j in range(16):
                                ii = e.memset(xs2[j][:, 0:3].bitcast(F32), 0.0)
                        else:
                            for j in range(16):
                                ii = e.tensor_copy(out=xs2[j][:, 0:3], in_=xs2[j][:, TC:TC + 3])
                        ii.then_inc(s_dv.h, 1)
                    o["vector"].append(halo)
                    s_dv.n += 1
                    v_halo = s_dv.n

                    v_lp = v_lnc_prev
                    v_ip = v_ipmm_prev

                    ch_first_cc = cc_done.get("x1") if ch == 0 else None

                    def ldx1(e, half=half, tc0=tc0, v_lp=v_lp, v_ip=v_ip, cfc=ch_first_cc):
                        if v_lp:
                            e.wait_ge(s_dv.h, v_lp)
                        if v_ip:
                            e.wait_ge(s_pe.h, v_ip)
                        if cfc is not None:
                            e.wait_ge(cfc[0].h, cfc[1])
                        e.dma_start(out=x1c[:, :].rearrange("p (k t) -> p k t", k=KT_D),
                                    in_=x1g[half][:, tc0:tc0 + TC].rearrange("(k p) t -> p k t", p=128)
                                    ).then_inc(s_x.h, 16)
                    o["sync"].append(ldx1)
                    s_x.n += 16
                    v_lx1 = s_x.n

                    vstat_mm = 0
                    for g4 in range(4):
                        v_pe_prev = s_pe.n

                        def msq(e, g4=g4, v_pe_prev=v_pe_prev, v_lx1=v_lx1):
                            e.wait_ge(s_x.h, v_lx1)
                            if g4 > 0:
                                e.wait_ge(s_pe.h, v_pe_prev)
                            for i in range(4):
                                k = 4 * g4 + i
                                ii = e.activation(out=sqm[:, i * TC:(i + 1) * TC],
                                                  in_=x1c[:, k * TC:(k + 1) * TC], func=AF.Square)
                            ii.then_inc(s_ac.h, 1)
                        o["scalar"].append(msq)
                        s_ac.n += 1
                        v_msq = s_ac.n
                        v_rfree = v_lnc_prev

                        def mmst(e, g4=g4, v_msq=v_msq, v_lx1=v_lx1, mf=mmst_free[0], af=scanac_free[0]):
                            e.wait_ge(s_x.h, v_lx1)
                            e.wait_ge(s_ac.h, v_msq)
                            if g4 == 0 and mf:
                                e.wait_ge(s_dv.h, mf)
                            if g4 == 0 and af:
                                e.wait_ge(s_ac.h, af)
                            for i in range(4):
                                k = 4 * g4 + i
                                e.matmul(pb_[2][0:1, :], ones_f[:, 0:1], x1c[:, k * TC:(k + 1) * TC],
                                         start=(k == 0), stop=(k == KT_D - 1), skip_group_check=True)
                                ii = e.matmul(pb_[3][0:1, :], ones_r[:, 0:1], sqm[:, i * TC:(i + 1) * TC],
                                              start=(k == 0), stop=(k == KT_D - 1), skip_group_check=True)
                            ii.then_inc(s_pe.h, 1)
                        o["tensor"].append(mmst)
                        s_pe.n += 1
                    v_mst = s_pe.n

                    def mrows(e, v_mst=v_mst):
                        e.wait_ge(s_pe.h, v_mst)
                        ii = e.tensor_scalar(out=rows[0:1, 0:TC], in0=pb_[2][0:1, :], scalar1=1.0 / D,
                                             scalar2=None, op0=ALU.mult)
                        e.tensor_scalar(out=rows[0:1, TC:2 * TC], in0=pb_[3][0:1, :], scalar1=1.0 / D,
                                        scalar2=None, op0=ALU.mult)
                        edge_dv(e, ii)
                        ii = e.tensor_mul(out=rows[0:1, 2 * TC:3 * TC], in0=rows[0:1, 0:TC], in1=rows[0:1, 0:TC])
                        edge_dv(e, ii)
                        e.tensor_sub(out=rows[0:1, TC:2 * TC], in0=rows[0:1, TC:2 * TC],
                                     in1=rows[0:1, 2 * TC:3 * TC]).then_inc(s_dv.h, 1)
                    o["vector"].append(mrows)
                    s_dv.n += 1
                    v_mr1 = s_dv.n
                    mmst_free[0] = v_mr1

                    def mrow2(e, v_mr1=v_mr1):
                        e.wait_ge(s_dv.h, v_mr1)
                        ii = e.activation(out=rows[0:1, 2 * TC:3 * TC], in_=rows[0:1, TC:2 * TC], func=AF.Ln,
                                          bias=cst[0:1, 0:1])
                        edge_ac(e, ii)
                        e.activation(out=rows[0:1, 3 * TC:4 * TC], in_=rows[0:1, 2 * TC:3 * TC],
                                     func=AF.Exp, scale=-0.5).then_inc(s_ac.h, 1)
                    o["scalar"].append(mrow2)
                    s_ac.n += 1
                    v_mg = s_ac.n

                    v_mdupmm = s_pe.n

                    def mdup(e, v_mr1=v_mr1, v_mg=v_mg, mdw=mdup_free[0], af=scanac_free[0]):
                        e.wait_ge(s_dv.h, v_mr1)
                        if mdw:
                            e.wait_ge(s_dv.h, mdw)
                        if af:
                            e.wait_ge(s_ac.h, af)
                        e.matmul(pb_[4][:, :], ones_f[0:1, :], rows[0:1, 0:TC], start=True, stop=True).then_inc(s_pe.h, 1)
                    o["tensor"].append(mdup)
                    s_pe.n += 1
                    v_md1 = s_pe.n

                    def mcp1(e, v_md1=v_md1):
                        e.wait_ge(s_pe.h, v_md1)
                        e.tensor_copy(out=rdm[:, 0:TC], in_=pb_[4][:, :]).then_inc(s_dv.h, 1)
                    o["vector"].append(mcp1)
                    s_dv.n += 1
                    v_mc1 = s_dv.n

                    def mdup2(e, v_mg=v_mg, v_mc1=v_mc1):
                        e.wait_ge(s_ac.h, v_mg)
                        e.wait_ge(s_dv.h, v_mc1)
                        e.matmul(pb_[4][:, :], ones_f[0:1, :], rows[0:1, 3 * TC:4 * TC], start=True, stop=True).then_inc(s_pe.h, 1)
                    o["tensor"].append(mdup2)
                    s_pe.n += 1
                    v_md2 = s_pe.n

                    def mcp2(e, v_md2=v_md2):
                        e.wait_ge(s_pe.h, v_md2)
                        e.tensor_copy(out=rdm[:, TC:2 * TC], in_=pb_[4][:, :]).then_inc(s_dv.h, 1)
                    o["vector"].append(mcp2)
                    s_dv.n += 1
                    v_mc2 = s_dv.n
                    mdup_free[0] = v_mc2

                    lnc = x1c[:, :].bitcast(F32R)

                    def mln(e, v_mc2=v_mc2):
                        e.wait_ge(s_dv.h, v_mc2)
                        for k in range(KT_D):
                            ii = e.tensor_sub(out=lnc[:, k * TC:(k + 1) * TC], in0=x1c[:, k * TC:(k + 1) * TC],
                                              in1=rdm[:, 0:TC])
                        edge_dv(e, ii)
                        for k in range(KT_D):
                            ii = e.tensor_mul(out=lnc[:, k * TC:(k + 1) * TC],
                                              in0=lnc[:, k * TC:(k + 1) * TC], in1=rdm[:, TC:2 * TC])
                        ii.then_inc(s_dv.h, 1)
                    o["vector"].append(mln)
                    s_dv.n += 1
                    v_lnc = s_dv.n
                    v_lnc_prev = v_lnc

                    v_sc_prev = v_scmm_prev
                    v_scdv_prev = scandv_free[0]
                    evac_ac = {}
                    for m in range(28):
                        view, wsem, v_w = wpanel(o, g_wm, m, 0, 16, F32R,
                                                 wait_cc=(cc_done.get("wm") if (ch == 0 and m == 0) else None))
                        wait_ev = evac_ac.get(m - 2, 0)

                        def ipmm(e, view=view, wsem=wsem, v_w=v_w, m=m, wait_ev=wait_ev,
                                 v_lnc=v_lnc, v_sc_prev=v_sc_prev, v_scdv_prev=v_scdv_prev):
                            e.wait_ge(wsem.h, v_w)
                            e.wait_ge(s_dv.h, v_lnc)
                            if m < 2 and v_sc_prev:
                                e.wait_ge(s_pe.h, v_sc_prev)
                            if m < 2 and v_scdv_prev:
                                e.wait_ge(s_dv.h, v_scdv_prev)
                            if wait_ev:
                                e.wait_ge(s_ac.h, wait_ev)
                            pm = pb_[m % 2]
                            for k in range(KT_D):
                                ii = e.matmul(pm[:, :], view[:, k * 128:(k + 1) * 128],
                                              lnc[:, k * TC:(k + 1) * TC], start=(k == 0), stop=(k == KT_D - 1))
                            ii.then_inc(s_pe.h, 1)
                        o["tensor"].append(ipmm)
                        s_pe.n += 1
                        wr_mark(1, s_pe.n)
                        v_ipm = s_pe.n

                        if m < 8:
                            def evx(e, m=m, v_ipm=v_ipm, v_halo=v_halo):
                                e.wait_ge(s_pe.h, v_ipm)
                                e.wait_ge(s_dv.h, v_halo)
                                e.activation(out=xs2[2 * m][:, 3:3 + TC], in_=pb_[m % 2][0:64, :], func=AF.Copy)
                                e.activation(out=xs2[2 * m + 1][:, 3:3 + TC], in_=pb_[m % 2][64:128, :],
                                             func=AF.Copy).then_inc(s_ac.h, 1)
                            o["scalar"].append(evx)
                        elif m < 12:
                            def evx(e, m=m, v_ipm=v_ipm, v_sc_prev=v_sc_prev):
                                e.wait_ge(s_pe.h, v_ipm)
                                if v_sc_prev:
                                    e.wait_ge(s_pe.h, v_sc_prev)
                                e.activation(out=bsrc[:, (m - 8) * TC:(m - 7) * TC], in_=pb_[m % 2][:, :],
                                             func=AF.Copy).then_inc(s_ac.h, 1)
                            o["scalar"].append(evx)
                        else:
                            def evx(e, m=m, v_ipm=v_ipm, v_sc_prev=v_sc_prev):
                                e.wait_ge(s_pe.h, v_ipm)
                                if v_sc_prev:
                                    e.wait_ge(s_pe.h, v_sc_prev)
                                e.activation(out=csrc[:, (m - 12) * TC:(m - 11) * TC], in_=pb_[m % 2][:, :],
                                             func=AF.Copy).then_inc(s_ac.h, 1)
                            o["scalar"].append(evx)
                        s_ac.n += 1
                        evac_ac[m] = s_ac.n
                    v_bc = s_ac.n

                    view, wsem, v_w = wpanel(o, g_wdtin, 0, 0, 16, F32R)
                    wait_ev = evac_ac.get(26, 0)

                    def dt1mm(e, view=view, wsem=wsem, v_w=v_w, wait_ev=wait_ev, v_lnc=v_lnc):
                        e.wait_ge(wsem.h, v_w)
                        e.wait_ge(s_dv.h, v_lnc)
                        e.wait_ge(s_ac.h, wait_ev)
                        for k in range(KT_D):
                            ii = e.matmul(pb_[0][:, :], view[:, k * 128:(k + 1) * 128],
                                          lnc[:, k * TC:(k + 1) * TC], start=(k == 0), stop=(k == KT_D - 1))
                        ii.then_inc(s_pe.h, 1)
                    o["tensor"].append(dt1mm)
                    s_pe.n += 1
                    wr_mark(1, s_pe.n)
                    v_dt1 = s_pe.n

                    def dt1ev(e, v_dt1=v_dt1):
                        e.wait_ge(s_pe.h, v_dt1)
                        e.activation(out=rsl[:, :], in_=pb_[0][:, :], func=AF.Copy).then_inc(s_ac.h, 1)
                    o["scalar"].append(dt1ev)
                    s_ac.n += 1
                    v_rsl = s_ac.n

                    sp_vals = [0] * 32
                    relu_vals = [0] * 32
                    for ct in range(CTI):
                        view, wsem, v_w = wpanel(o, g_wdtp, ct, 0, 1, F32R)
                        wait_rd = relu_vals[ct - 2] if ct >= 2 else 0

                        def dt2mm(e, view=view, wsem=wsem, v_w=v_w, ct=ct, v_rsl=v_rsl, wait_rd=wait_rd):
                            e.wait_ge(wsem.h, v_w)
                            e.wait_ge(s_ac.h, max(v_rsl, wait_rd))
                            e.matmul(pb_[ct % 2][:, :], view[:, 0:128], rsl[:, :],
                                     start=True, stop=True).then_inc(s_pe.h, 1)
                        o["tensor"].append(dt2mm)
                        s_pe.n += 1
                        wr_mark(1, s_pe.n)
                        v_d2 = s_pe.n

                        sp_wdv = sp_vals[ct - 2] if ct >= 2 else 0
                        sp_wdbg = s_dbg.n if ch == 1 else 0

                        def spchain(e, ct=ct, v_d2=v_d2, wdv=sp_wdv):
                            e.wait_ge(s_pe.h, v_d2)
                            if wdv:
                                e.wait_ge(s_dv.h, wdv)
                            sl = ct % 2
                            ii = e.activation(out=rg1[:, sl * TC:(sl + 1) * TC], in_=pb_[ct % 2][:, :],
                                              func=AF.Abs, bias=dtbp[:, ct:ct + 1])
                            edge_ac(e, ii)
                            ii = e.activation(out=rg2[:, sl * TC:(sl + 1) * TC],
                                              in_=rg1[:, sl * TC:(sl + 1) * TC], func=AF.Exp, scale=-1.0)
                            edge_ac(e, ii)
                            e.activation(out=rg2[:, sl * TC:(sl + 1) * TC],
                                         in_=rg2[:, sl * TC:(sl + 1) * TC], func=AF.Ln, bias=cstp[:, 2:3])
                            e.activation(out=rg3[:, sl * TC:(sl + 1) * TC], in_=pb_[ct % 2][:, :],
                                         func=AF.Relu, bias=dtbp[:, ct:ct + 1]).then_inc(s_ac.h, 1)
                        o["scalar"].append(spchain)
                        s_ac.n += 1
                        relu_vals[ct] = s_ac.n
                        v_spc = s_ac.n

                        def spadd(e, ct=ct, v_spc=v_spc, wdbg=sp_wdbg):
                            e.wait_ge(s_ac.h, v_spc)
                            if wdbg:
                                e.wait_ge(s_dbg.h, wdbg)
                            sl = ct % 2
                            e.tensor_add(out=spn[:, ct * TC:(ct + 1) * TC], in0=rg2[:, sl * TC:(sl + 1) * TC],
                                         in1=rg3[:, sl * TC:(sl + 1) * TC]).then_inc(s_dv.h, 1)
                        o["vector"].append(spadd)
                        s_dv.n += 1
                        sp_vals[ct] = s_dv.n

                    v_dt2ac = s_ac.n

                    def emit_cv(g, ch=ch):
                        if g < 2:
                            wbank = ("ac", v_dt2ac)
                        elif g % 2 == 0:
                            wbank = ("dv", sc_vals[8 * (g - 2) + 7])
                        else:
                            wbank = ("dv", yev_vals[g - 2])

                        def cvmm(e, g=g, wbank=wbank, v_bc=v_bc):
                            e.wait_ge(s_ac.h, v_bc)
                            if wbank[0] == "ac":
                                e.wait_ge(s_ac.h, wbank[1])
                            else:
                                e.wait_ge(s_dv.h, wbank[1])
                            b0 = 32 * (g % 2)
                            for k in range(KC):
                                gg = k * 16 + g // 2
                                ii = e.matmul(pb_[g % 2][:, :],
                                              pconv[b0:b0 + 32, gg * 128:(gg + 1) * 128],
                                              xs2[g // 2][b0:b0 + 32, k:k + TC],
                                              start=(k == 0), stop=(k == KC - 1))
                            ii.then_inc(s_pe.h, 1)
                        o["tensor"].append(cvmm)
                        s_pe.n += 1
                        v_cv = s_pe.n

                        def silu(e, g=g, v_cv=v_cv, wdv=wtt_vals[g - 2]):
                            e.wait_ge(s_pe.h, v_cv)
                            if wdv:
                                e.wait_ge(s_dv.h, wdv)
                            e.activation(out=xar[:, (g % 2) * TC:(g % 2 + 1) * TC], in_=pb_[g % 2][:, :],
                                         func=AF.Silu, bias=cb_t[:, g:g + 1]).then_inc(s_ac.h, 1)
                        o["scalar"].append(silu)
                        s_ac.n += 1
                        silu_vals[g] = s_ac.n
                        v_si = s_ac.n

                        def wtt(e, g=g, hf=ch // 4, v_si=v_si, w_sp=sp_vals[g], wmw=mmw7[g - 2]):
                            e.wait_ge(s_ac.h, v_si)
                            e.wait_ge(s_dv.h, w_sp)
                            if wmw:
                                e.wait_ge(s_pe.h, wmw)
                            ii = e.tensor_mul(out=wsl[:, (g % 2) * TC:(g % 2 + 1) * TC],
                                              in0=spn[:, g * TC:(g + 1) * TC].bitcast(F32),
                                              in1=xar[:, (g % 2) * TC:(g % 2 + 1) * TC])
                            edge_dv(e, ii)
                            e.tensor_scalar(out=dxe[:, (g % 2) * TC:(g % 2 + 1) * TC],
                                            in0=xar[:, (g % 2) * TC:(g % 2 + 1) * TC],
                                            scalar1=dv_t[:, 32 * hf + g:32 * hf + g + 1],
                                            scalar2=None, op0=ALU.mult).then_inc(s_dv.h, 1)
                        o["vector"].append(wtt)
                        s_dv.n += 1
                        wtt_vals[g] = s_dv.n

                    emit_cv(0)
                    emit_cv(1)

                    E_vals = [0] * 256
                    wd_vals = [0] * 256
                    u_vals = [0] * 256
                    scan_vals = [0] * 256
                    sc_vals = [0] * 256
                    yev_vals = [0] * 32
                    new_carry = [0] * 32
                    for q in range(256):
                        g, qq = q // 8, q % 8
                        if qq == 0 and g > 0:
                            pass
                        w_free = E_vals[q - 2] if q >= 2 else 0

                        def mmS(e, q=q, g=g, w_free=w_free, w_sp=sp_vals[g], v_mpat=v_mpat):
                            e.wait_ge(s_dv.h, max(w_sp, v_mpat))
                            if w_free:
                                e.wait_ge(s_ac.h, w_free)
                            e.matmul(pb_[2 + q % 2][:, :], p_sp[:, (q % 8) * 128:(q % 8 + 1) * 128],
                                     spn[:, g * TC:(g + 1) * TC], start=True, stop=True).then_inc(s_pe.h, 1)
                        o["tensor"].append(mmS)
                        s_pe.n += 1
                        v_mmS = s_pe.n

                        def mE(e, q=q, v_mmS=v_mmS, v_masc=v_masc, wsc=scan_vals[q - 2] if q >= 2 else 0):
                            e.wait_ge(s_pe.h, v_mmS)
                            e.wait_ge(s_ac.h, v_masc)
                            if wsc:
                                e.wait_ge(s_dv.h, wsc)
                            e.activation(out=Esl[:, (q % 2) * TC:(q % 2 + 1) * TC], in_=pb_[2 + q % 2][:, :],
                                         func=AF.Exp, scale=masc[:, q:q + 1]).then_inc(s_ac.h, 1)
                        o["scalar"].append(mE)
                        s_ac.n += 1
                        E_vals[q] = s_ac.n

                        w_free = wd_vals[q - 2] if q >= 2 else 0

                        def mmW(e, q=q, g=g, w_free=w_free, w_w=wtt_vals[g]):
                            e.wait_ge(s_dv.h, w_w)
                            if w_free:
                                e.wait_ge(s_ac.h, w_free)
                            e.matmul(pb_[4 + q % 2][:, :], p_w[:, (q % 8) * 128:(q % 8 + 1) * 128],
                                     wsl[:, (g % 2) * TC:(g % 2 + 1) * TC],
                                     start=True, stop=True).then_inc(s_pe.h, 1)
                        o["tensor"].append(mmW)
                        s_pe.n += 1
                        v_mmW = s_pe.n
                        if qq == 7:
                            mmw7[g] = s_pe.n

                        def mwd(e, q=q, v_mmW=v_mmW, wu=u_vals[q - 2] if q >= 2 else 0):
                            e.wait_ge(s_pe.h, v_mmW)
                            if wu:
                                e.wait_ge(s_dv.h, wu)
                            e.activation(out=wdr[:, (q % 2) * TC:(q % 2 + 1) * TC], in_=pb_[4 + q % 2][:, :],
                                         func=AF.Copy).then_inc(s_ac.h, 1)
                        o["scalar"].append(mwd)
                        s_ac.n += 1
                        wd_vals[q] = s_ac.n

                        w_free = u_vals[q - 2] if q >= 2 else 0

                        def mmB(e, q=q, w_free=w_free, v_bc=v_bc, v_mpat=v_mpat):
                            e.wait_ge(s_ac.h, v_bc)
                            e.wait_ge(s_dv.h, v_mpat)
                            if w_free:
                                e.wait_ge(s_dv.h, w_free)
                            e.matmul(pb_[6 + q % 2][:, :], p_b[:, ((q // 4) % 16) * 128:((q // 4) % 16 + 1) * 128],
                                     bsrc[:, (q // 64) * TC:(q // 64 + 1) * TC], start=True, stop=True).then_inc(s_pe.h, 1)
                        o["tensor"].append(mmB)
                        s_pe.n += 1
                        v_mmB = s_pe.n

                        def mu(e, q=q, v_mmB=v_mmB, wwd=wd_vals[q]):
                            e.wait_ge(s_pe.h, v_mmB)
                            e.wait_ge(s_ac.h, wwd)
                            e.tensor_mul(out=usl[:, (q % 2) * TC:(q % 2 + 1) * TC], in0=pb_[6 + q % 2][:, :],
                                         in1=wdr[:, (q % 2) * TC:(q % 2 + 1) * TC]).then_inc(s_dv.h, 1)
                        o["vector"].append(mu)
                        s_dv.n += 1
                        u_vals[q] = s_dv.n

                        def mscan(e, q=q, g=g, qq=qq, ch=ch, wu=u_vals[q], wE=E_vals[q],
                                  wca=carry_vals[g], wsl2=new_carry[g - 2] if g >= 2 else 0,
                                  wsc2=sc_vals[q - 16] if q >= 16 else 0):
                            e.wait_ge(s_dv.h, wu)
                            e.wait_ge(s_ac.h, wE)
                            if wca:
                                e.wait_ge(s_dv.h, wca)
                            if wsl2:
                                e.wait_ge(s_dv.h, wsl2)
                            init = 0.0 if ch == 0 else carry[:, q:q + 1]
                            e.tensor_tensor_scan(out=Ssl[g % 2][:, qq * TC:(qq + 1) * TC],
                                                 data0=Esl[:, (q % 2) * TC:(q % 2 + 1) * TC],
                                                 data1=usl[:, (q % 2) * TC:(q % 2 + 1) * TC],
                                                 initial=init, op0=ALU.mult, op1=ALU.add).then_inc(s_dv.h, 1)
                        o["vector"].append(mscan)
                        s_dv.n += 1
                        scan_vals[q] = s_dv.n

                        w_free = sc_vals[q - 1] if q >= 1 else 0
                        if qq == 0 and g == 0:
                            w_cvac = silu_vals[0]
                        elif qq == 0 and g % 2 == 1 and g <= 30:
                            w_cvac = silu_vals[g + 1]
                        else:
                            w_cvac = 0

                        def mmC(e, q=q, w_free=w_free, v_bc=v_bc, v_mpat=v_mpat, w_cvac=w_cvac):
                            e.wait_ge(s_ac.h, v_bc)
                            e.wait_ge(s_dv.h, v_mpat)
                            if w_cvac:
                                e.wait_ge(s_ac.h, w_cvac)
                            if w_free:
                                e.wait_ge(s_dv.h, w_free)
                            e.matmul(pb_[0][:, :], p_b[:, (q % 16) * 128:(q % 16 + 1) * 128],
                                     csrc[:, (q // 16) * TC:(q // 16 + 1) * TC], start=True, stop=True).then_inc(s_pe.h, 1)
                        o["tensor"].append(mmC)
                        s_pe.n += 1
                        v_mmC = s_pe.n

                        def msc(e, q=q, g=g, qq=qq, v_mmC=v_mmC, wscan=scan_vals[q]):
                            e.wait_ge(s_pe.h, v_mmC)
                            e.wait_ge(s_dv.h, wscan)
                            e.tensor_mul(out=SCr[:, (q % 2) * TC:(q % 2 + 1) * TC],
                                         in0=pb_[0][:, :], in1=Ssl[g % 2][:, qq * TC:(qq + 1) * TC]
                                         ).then_inc(s_dv.h, 1)
                        o["vector"].append(msc)
                        s_dv.n += 1
                        sc_vals[q] = s_dv.n

                        def mmY(e, q=q, qq=qq, wsc=sc_vals[q],
                                wyev=yev_vals[g - 1] if (qq == 0 and g >= 1) else 0,
                                w_cvac=silu_vals[g + 1] if (qq == 0 and g % 2 == 0 and g <= 30) else 0):
                            e.wait_ge(s_dv.h, wsc)
                            if wyev:
                                e.wait_ge(s_dv.h, wyev)
                            if w_cvac:
                                e.wait_ge(s_ac.h, w_cvac)
                            e.matmul(pb_[1][:, :], p_r[:, (q % 8) * 128:(q % 8 + 1) * 128],
                                     SCr[:, (q % 2) * TC:(q % 2 + 1) * TC],
                                     start=(qq == 0), stop=(qq == 7)).then_inc(s_pe.h, 1)
                        o["tensor"].append(mmY)
                        s_pe.n += 1
                        v_mmY = s_pe.n

                        if qq == 7:
                            def ycar(e, g=g, ch=ch, v_mmY=v_mmY, wye=yest[g % 2]):
                                e.wait_ge(s_pe.h, v_mmY)
                                if wye:
                                    e.wait_ge(s_ys[g % 2].h, wye)
                                ii = e.tensor_add(out=yer[:, (g % 2) * TC:(g % 2 + 1) * TC],
                                                  in0=pb_[1][:, :], in1=dxe[:, (g % 2) * TC:(g % 2 + 1) * TC])
                                if ch < NCH - 1:
                                    ii = e.tensor_copy(out=carry[:, 8 * g:8 * g + 8],
                                                       in_=Ssl[g % 2][:, :].rearrange("p (a b) -> p a b", a=8)[:, :, TC - 1:TC].squeeze(2))
                                ii.then_inc(s_dv.h, 1)
                            o["vector"].append(ycar)
                            s_dv.n += 1
                            yev_vals[g] = s_dv.n
                            new_carry[g] = s_dv.n

                            def yst(e, g=g, ch=ch, half=half, tc0=tc0, wy=yev_vals[g]):
                                e.wait_ge(s_dv.h, wy)
                                e.dma_start(out=ybuf[half][128 * g:128 * (g + 1), tc0:tc0 + TC],
                                            in_=yer[:, (g % 2) * TC:(g % 2 + 1) * TC]).then_inc(s_ys[g % 2].h, 16)
                            o["gpsimd"].append(yst)
                            s_ys[g % 2].n += 16
                            yest[g % 2] = s_ys[g % 2].n

                            if g < 30:
                                emit_cv(g + 2)

                    carry_vals = list(new_carry)
                    v_scmm_prev = s_pe.n
                    v_convmm_prev = s_pe.n
                    v_ipmm_prev = s_pe.n
                    v_yev_prev = yev_vals[31]
                    scandv_free[0] = s_dv.n
                    scanac_free[0] = s_ac.n

                    if ch == 0 and "spn" in dbg:
                        v_dmp = s_dv.n

                        def dmps(e, v_dmp=v_dmp):
                            e.wait_ge(s_dv.h, v_dmp)
                            e.dma_start(out=dbg["spn"].rearrange("(g p) t -> p g t", p=128),
                                        in_=spn[:, :].bitcast(F32).rearrange("p (g t) -> p g t", g=CTI)
                                        ).then_inc(s_dbg.h, 16)
                        o["gpsimd"].append(dmps)
                        s_dbg.n += 16

                if "ypart" in dbg:
                    def dmpy(e, w0=s_ys[0].n, w1=s_ys[1].n):
                        e.wait_ge(s_ys[0].h, w0)
                        e.wait_ge(s_ys[1].h, w1)
                        e.dma_start(out=dbg["ypart"][:, 0:TH], in_=ybuf[0]).then_inc(s_out.h, 16)
                        e.dma_start(out=dbg["ypart"][:, TH:TF], in_=ybuf[1]).then_inc(s_out.h, 16)
                    o["gpsimd"].append(dmpy)
                    s_out.n += 32

                run_block(o)

        if _NPHASE >= 3:
            mmst_free = [0]
            mdup_free = [0]
            yest = [0, 0]
            scandv_free = [0]
            scanac_free = [0]
            emit_mamba()
            o = new_ops()
            def rs_y(e, w0=s_ys[0].n, w1=s_ys[1].n):
                e.wait_ge(s_ys[0].h, w0)
                e.wait_ge(s_ys[1].h, w1)
                e.collective_compute("ReduceScatter", ALU.add,
                                     replica_groups=[[2 * b, 2 * b + 1] for b in range(4)],
                                     ins=[ybuf.opt()], outs=[ysum_b.opt()]).then_inc(s_ccl["y"].h, 1)

            o["gpsimd"].append(rs_y)
            s_ccl["y"].n += 1
            cc_done["y"] = (s_ccl["y"], 1)
            run_block(o)

        def emit_zgate():
            with ExitStack() as stz:
                o = new_ops()

                def zgate0(e, vo=s_out.n, v0=s_st[0].n, v1=s_st[1].n,
                           y0=s_ys[0].n, y1=s_ys[1].n, vd=s_dbg.n):
                    for sh, vv in ((s_out.h, vo), (s_st[0].h, v0), (s_st[1].h, v1),
                                   (s_ys[0].h, y0), (s_ys[1].h, y1), (s_dbg.h, vd)):
                        if vv:
                            e.wait_ge(sh, vv)
                for _eng in ("sync", "tensor", "vector", "scalar"):
                    o[_eng].append(zgate0)

                zx = stz.enter_context(sb("z_x", [128, KT_D * ZC], F32))
                zln = stz.enter_context(sb("z_ln", [128, KT_D * ZC], F32R))
                zsq = stz.enter_context(sb("z_sq", [128, 4 * ZC], F32R))
                zrows = stz.enter_context(sb("z_rows", [1, 4 * ZC], F32))
                zrdm = stz.enter_context(sb("z_rdm", [128, 2 * ZC], F32))
                zsil = stz.enter_context(sb("z_sil", [128, 2 * ZC], F32))
                ysl = stz.enter_context(sb("z_y", [128, 2 * ZC], F32))
                gsl = stz.enter_context(sb("z_g", [128, CTI * ZC], F32R))
                x2s = stz.enter_context(sb("z_x2", [128, 2 * ZC], F32))
                pz = [stz.enter_context(psal(f"z_pz{i}", [128, ZC], F32)) for i in range(2)]
                po = [stz.enter_context(psal(f"z_po{i}", [128, ZC], F32)) for i in range(2)]
                pzs1 = stz.enter_context(psal("z_ps1", [1, ZC], F32))
                pzs2 = stz.enter_context(psal("z_ps2", [1, ZC], F32))
                pzdu = stz.enter_context(psal("z_pdu", [128, ZC], F32))

                v_zlnrd_prev = 0
                v_x2_prev = 0
                v_gslrd_prev = 0
                v_zr1_prev = 0
                zdu_prev = 0
                evz_sl = [0, 0]
                gm_sl = [0, 0]
                x2_sl = [0, 0]
                st2 = [s_st[0].n, s_st[1].n]

                for oc in range(NZC):
                    oc0 = oc * ZC

                    def ldzx(e, oc0=oc0, wx2=v_x2_prev):
                        if wx2:
                            e.wait_ge(s_dv.h, wx2)
                        e.dma_start(out=zx[:, :].rearrange("p (k t) -> p k t", k=KT_D),
                                    in_=x1me[:, oc0:oc0 + ZC].rearrange("(k p) t -> p k t", p=128)
                                    ).then_inc(s_x.h, 16)
                    o["sync"].append(ldzx)
                    s_x.n += 16
                    v_lzx = s_x.n

                    for g4 in range(4):
                        v_pe_prev = s_pe.n

                        def zsqa(e, g4=g4, v_pe_prev=v_pe_prev, v_lzx=v_lzx):
                            e.wait_ge(s_x.h, v_lzx)
                            if g4 > 0:
                                e.wait_ge(s_pe.h, v_pe_prev)
                            for i in range(4):
                                k = 4 * g4 + i
                                ii = e.activation(out=zsq[:, i * ZC:(i + 1) * ZC],
                                                  in_=zx[:, k * ZC:(k + 1) * ZC], func=AF.Square)
                            ii.then_inc(s_ac.h, 1)
                        o["scalar"].append(zsqa)
                        s_ac.n += 1
                        v_zsq = s_ac.n

                        def zmmst(e, g4=g4, v_zsq=v_zsq, v_lzx=v_lzx,
                                  wrp=(v_zr1_prev if g4 == 0 else 0)):
                            e.wait_ge(s_x.h, v_lzx)
                            e.wait_ge(s_ac.h, v_zsq)
                            if wrp:
                                e.wait_ge(s_dv.h, wrp)
                            for i in range(4):
                                k = 4 * g4 + i
                                e.matmul(pzs1[0:1, :], ones_f[:, 0:1], zx[:, k * ZC:(k + 1) * ZC],
                                         start=(k == 0), stop=(k == KT_D - 1), skip_group_check=True)
                                ii = e.matmul(pzs2[0:1, :], ones_r[:, 0:1], zsq[:, i * ZC:(i + 1) * ZC],
                                              start=(k == 0), stop=(k == KT_D - 1), skip_group_check=True)
                            ii.then_inc(s_pe.h, 1)
                        o["tensor"].append(zmmst)
                        s_pe.n += 1
                    v_zstat = s_pe.n

                    def zrows1(e, v_zstat=v_zstat):
                        e.wait_ge(s_pe.h, v_zstat)
                        ii = e.tensor_scalar(out=zrows[0:1, 0:ZC], in0=pzs1[0:1, :],
                                             scalar1=1.0 / D, scalar2=None, op0=ALU.mult)
                        e.tensor_scalar(out=zrows[0:1, ZC:2 * ZC], in0=pzs2[0:1, :],
                                        scalar1=1.0 / D, scalar2=None, op0=ALU.mult)
                        edge_dv(e, ii)
                        ii = e.tensor_mul(out=zrows[0:1, 2 * ZC:3 * ZC], in0=zrows[0:1, 0:ZC],
                                          in1=zrows[0:1, 0:ZC])
                        edge_dv(e, ii)
                        e.tensor_sub(out=zrows[0:1, ZC:2 * ZC], in0=zrows[0:1, ZC:2 * ZC],
                                     in1=zrows[0:1, 2 * ZC:3 * ZC]).then_inc(s_dv.h, 1)
                    o["vector"].append(zrows1)
                    s_dv.n += 1
                    v_zr1 = s_dv.n
                    v_zr1_prev = v_zr1

                    def zrow2(e, v_zr1=v_zr1):
                        e.wait_ge(s_dv.h, v_zr1)
                        ii = e.activation(out=zrows[0:1, 2 * ZC:3 * ZC], in_=zrows[0:1, ZC:2 * ZC],
                                          func=AF.Ln, bias=cst[0:1, 0:1])
                        edge_ac(e, ii)
                        e.activation(out=zrows[0:1, 3 * ZC:4 * ZC], in_=zrows[0:1, 2 * ZC:3 * ZC],
                                     func=AF.Exp, scale=-0.5).then_inc(s_ac.h, 1)
                    o["scalar"].append(zrow2)
                    s_ac.n += 1
                    v_zg = s_ac.n

                    def zdup(e, v_zr1=v_zr1, wdup=zdu_prev):
                        e.wait_ge(s_dv.h, v_zr1)
                        if wdup:
                            e.wait_ge(s_dv.h, wdup)
                        e.matmul(pzdu[:, :], ones_f[0:1, :], zrows[0:1, 0:ZC],
                                 start=True, stop=True).then_inc(s_pe.h, 1)
                    o["tensor"].append(zdup)
                    s_pe.n += 1
                    v_zd1 = s_pe.n

                    def zcp1(e, v_zd1=v_zd1):
                        e.wait_ge(s_pe.h, v_zd1)
                        e.tensor_copy(out=zrdm[:, 0:ZC], in_=pzdu[:, :]).then_inc(s_dv.h, 1)
                    o["vector"].append(zcp1)
                    s_dv.n += 1
                    v_zc1 = s_dv.n

                    def zdup2(e, v_zg=v_zg, v_zc1=v_zc1):
                        e.wait_ge(s_ac.h, v_zg)
                        e.wait_ge(s_dv.h, v_zc1)
                        e.matmul(pzdu[:, :], ones_f[0:1, :], zrows[0:1, 3 * ZC:4 * ZC],
                                 start=True, stop=True).then_inc(s_pe.h, 1)
                    o["tensor"].append(zdup2)
                    s_pe.n += 1
                    v_zd2 = s_pe.n

                    def zcp2(e, v_zd2=v_zd2):
                        e.wait_ge(s_pe.h, v_zd2)
                        e.tensor_copy(out=zrdm[:, ZC:2 * ZC], in_=pzdu[:, :]).then_inc(s_dv.h, 1)
                    o["vector"].append(zcp2)
                    s_dv.n += 1
                    v_zc2 = s_dv.n
                    zdu_prev = v_zc2

                    def zlnc(e, v_zc2=v_zc2, wrd=v_zlnrd_prev):
                        e.wait_ge(s_dv.h, v_zc2)
                        if wrd:
                            e.wait_ge(s_pe.h, wrd)
                        for k in range(KT_D):
                            ii = e.tensor_sub(out=zln[:, k * ZC:(k + 1) * ZC],
                                              in0=zx[:, k * ZC:(k + 1) * ZC], in1=zrdm[:, 0:ZC])
                        edge_dv(e, ii)
                        for k in range(KT_D):
                            ii = e.tensor_mul(out=zln[:, k * ZC:(k + 1) * ZC],
                                              in0=zln[:, k * ZC:(k + 1) * ZC], in1=zrdm[:, ZC:2 * ZC])
                        ii.then_inc(s_dv.h, 1)
                    o["vector"].append(zlnc)
                    s_dv.n += 1
                    v_zln = s_dv.n

                    for m in range(CTI):
                        view, wsem, v_w = wpanel(o, g_wz, m, 0, KT_D, F32R,
                                                 wait_cc=(cc_done.get("wz") if (oc == 0 and m == 0) else None))

                        def zmm(e, m=m, view=view, wsem=wsem, v_w=v_w, v_zln=v_zln, wev=evz_sl[m % 2]):
                            e.wait_ge(wsem.h, v_w)
                            e.wait_ge(s_dv.h, v_zln)
                            if wev:
                                e.wait_ge(s_ac.h, wev)
                            for k in range(KT_D):
                                ii = e.matmul(pz[m % 2][:, :], view[:, k * 128:(k + 1) * 128],
                                              zln[:, k * ZC:(k + 1) * ZC],
                                              start=(k == 0), stop=(k == KT_D - 1))
                            ii.then_inc(s_pe.h, 1)
                        o["tensor"].append(zmm)
                        s_pe.n += 1
                        wr_mark(1, s_pe.n)
                        v_zmm = s_pe.n

                        def ldy(e, m=m, oc0=oc0, wgm=gm_sl[m % 2],
                                wcc=(cc_done.get("y") if (oc == 0 and m == 0) else None)):
                            if wcc is not None:
                                e.wait_ge(wcc[0].h, wcc[1])
                            if wgm:
                                e.wait_ge(s_dv.h, wgm)
                            e.dma_start(out=ysl[:, (m % 2) * ZC:(m % 2 + 1) * ZC],
                                        in_=ysum_b[128 * m:128 * (m + 1), oc0:oc0 + ZC]
                                        ).then_inc(s_zy[m % 2].h, 16)
                        o["sync"].append(ldy)
                        s_zy[m % 2].n += 16
                        v_ly = s_zy[m % 2].n

                        def evz(e, m=m, v_zmm=v_zmm, wgm=gm_sl[m % 2]):
                            e.wait_ge(s_pe.h, v_zmm)
                            if wgm:
                                e.wait_ge(s_dv.h, wgm)
                            e.activation(out=zsil[:, (m % 2) * ZC:(m % 2 + 1) * ZC],
                                         in_=pz[m % 2][:, :], func=AF.Silu).then_inc(s_ac.h, 1)
                        o["scalar"].append(evz)
                        s_ac.n += 1
                        evz_sl[m % 2] = s_ac.n
                        v_evz = s_ac.n

                        def gmul(e, m=m, v_ly=v_ly, v_evz=v_evz,
                                 wgf=(v_gslrd_prev if oc > 0 else 0)):
                            e.wait_ge(s_zy[m % 2].h, v_ly)
                            e.wait_ge(s_ac.h, v_evz)
                            if wgf:
                                e.wait_ge(s_pe.h, wgf)
                            e.tensor_mul(out=gsl[:, m * ZC:(m + 1) * ZC],
                                         in0=ysl[:, (m % 2) * ZC:(m % 2 + 1) * ZC],
                                         in1=zsil[:, (m % 2) * ZC:(m % 2 + 1) * ZC]
                                         ).then_inc(s_dv.h, 1)
                        o["vector"].append(gmul)
                        s_dv.n += 1
                        gm_sl[m % 2] = s_dv.n
                    v_zlnrd_prev = s_pe.n
                    v_gsl_done = s_dv.n

                    for mo in range(MT_D):
                        pans = []
                        for kh in range(2):
                            view, wsem, v_w = wpanel(o, g_wop, mo, 16 * kh, 16, F32R,
                                                     wait_cc=(cc_done.get("wop")
                                                              if (oc == 0 and mo == 0 and kh == 0) else None))
                            pans.append((view, wsem, v_w))

                        def omm(e, pans=tuple(pans), mo=mo, v_gd=v_gsl_done, wx2=x2_sl[mo % 2]):
                            for _, wsem, v_w in pans:
                                e.wait_ge(wsem.h, v_w)
                            e.wait_ge(s_dv.h, v_gd)
                            if wx2:
                                e.wait_ge(s_dv.h, wx2)
                            for kh in range(2):
                                view = pans[kh][0]
                                for k in range(16):
                                    ii = e.matmul(po[mo % 2][:, :], view[:, k * 128:(k + 1) * 128],
                                                  gsl[:, (16 * kh + k) * ZC:(16 * kh + k + 1) * ZC],
                                                  start=(kh == 0 and k == 0), stop=(kh == 1 and k == 15))
                            ii.then_inc(s_pe.h, 1)
                        o["tensor"].append(omm)
                        s_pe.n += 1
                        wr_mark(2, s_pe.n)
                        v_omm = s_pe.n

                        def x2add(e, mo=mo, v_omm=v_omm, wst=st2[mo % 2]):
                            e.wait_ge(s_pe.h, v_omm)
                            if wst:
                                e.wait_ge(s_st[mo % 2].h, wst)
                            e.tensor_add(out=x2s[:, (mo % 2) * ZC:(mo % 2 + 1) * ZC],
                                         in0=po[mo % 2][:, :],
                                         in1=zx[:, mo * ZC:(mo + 1) * ZC]).then_inc(s_dv.h, 1)
                        o["vector"].append(x2add)
                        s_dv.n += 1
                        x2_sl[mo % 2] = s_dv.n
                        v_x2a = s_dv.n

                        def x2st(e, mo=mo, oc0=oc0, v_x2a=v_x2a):
                            e.wait_ge(s_dv.h, v_x2a)
                            e.dma_start(out=x2buf[128 * mo:128 * (mo + 1), oc0:oc0 + ZC],
                                        in_=x2s[:, (mo % 2) * ZC:(mo % 2 + 1) * ZC]
                                        ).then_inc(s_st[mo % 2].h, 16)
                        o["gpsimd"].append(x2st)
                        s_st[mo % 2].n += 16
                        st2[mo % 2] = s_st[mo % 2].n

                    v_gslrd_prev = s_pe.n
                    v_x2_prev = s_dv.n

                if "x2" in dbg:
                    def dmpx2(e, w0=s_st[0].n, w1=s_st[1].n):
                        e.wait_ge(s_st[0].h, w0)
                        e.wait_ge(s_st[1].h, w1)
                        e.dma_start(out=dbg["x2"], in_=x2buf).then_inc(s_dbg.h, 16)
                    o["gpsimd"].append(dmpx2)
                    s_dbg.n += 16
                run_block(o)

        if _NPHASE >= 4:
            emit_zgate()

        if _NPHASE >= 5:
            emit_ffn(x2buf, out, "f2",
                     gates=[(s_out.h, s_out.n), (s_st[0].h, s_st[0].n), (s_st[1].h, s_st[1].n),
                            (s_ys[0].h, s_ys[0].n), (s_ys[1].h, s_ys[1].n),
                            (s_zy[0].h, s_zy[0].n), (s_zy[1].h, s_zy[1].n), (s_dbg.h, s_dbg.n)])

        o = new_ops()
        if _NPHASE < 5:
            zro = st.enter_context(sb("zro", [128, TH], F32))

            def fin_v(e):
                e.memset(zro[:, :], 0.0).then_inc(s_dv.h, 1)
            o["vector"].append(fin_v)
            s_dv.n += 1
            vz = s_dv.n
            v0 = s_out.n

            def fin_g(e):
                e.wait_ge(s_dv.h, vz)
                for k in range(MT_D):
                    e.dma_start(out=out[128 * k:128 * (k + 1), :], in_=zro[:, :]).then_inc(s_out.h, 16)
                e.wait_ge(s_out.h, v0 + 16 * MT_D)
            o["gpsimd"].append(fin_g)
            s_out.n += 16 * MT_D
        else:
            def fin_g(e, v0=s_out.n, w0=s_st[0].n, w1=s_st[1].n):
                e.wait_ge(s_st[0].h, w0)
                e.wait_ge(s_st[1].h, w1)
                if v0:
                    e.wait_ge(s_out.h, v0)
            o["gpsimd"].append(fin_g)
        run_block(o)

    return nc


def _host_layout(inputs):
    import ml_dtypes
    _bf16 = ml_dtypes.bfloat16
    x_in = np.asarray(inputs["L0_attn_hidden_states"], np.float32)
    ipw = np.asarray(inputs["L0_attn_in_proj_w"], np.float32)
    dtin = np.asarray(inputs["L0_attn_dt_in_proj_w"], np.float32)
    dtp = np.asarray(inputs["L0_attn_dt_proj_w"], np.float32)
    dtb_v = np.asarray(inputs["L0_attn_dt_proj_b"], np.float32)
    cw = np.asarray(inputs["L0_attn_conv_w"], np.float32)
    cb = np.asarray(inputs["L0_attn_conv_b"], np.float32)
    alog_v = np.asarray(inputs["L0_attn_A_log"], np.float32)
    dv = np.asarray(inputs["L0_attn_D"], np.float32)
    opw = np.asarray(inputs["L0_attn_out_proj_w"], np.float32)
    wfc = np.asarray(inputs["L0_ffn_w_fc"], np.float32)
    wpj = np.asarray(inputs["L0_ffn_w_proj"], np.float32)

    wfcT = np.ascontiguousarray(wfc.T)
    wpjT = np.ascontiguousarray(wpj.T)
    wzT = np.ascontiguousarray(ipw[:DI, :].T)
    wopT = np.ascontiguousarray(opw.T)
    wdtinT = np.ascontiguousarray(dtin.T)
    wdtpT = np.ascontiguousarray(dtp.T)

    wmv = []
    for j in (0, 1):
        cols = np.empty([D, NPF], np.float32)
        cols[:, XO:XO + DXB] = ipw[DI:DI + DXB, :].T
        bidx = np.array([DI + DXB + bh * 16 + 8 * j + nl for bh in range(BH) for nl in range(NH)])
        cols[:, BO:BO + BH * NH] = ipw[bidx, :].T
        cidx = np.array([DI + 2 * DXB + h * 16 + 8 * j + nl for h in range(H) for nl in range(NH)])
        cols[:, CO:CO + H * NH] = ipw[cidx, :].T
        wmv.append(np.ascontiguousarray(cols))

    alogs = []
    p = np.arange(128)
    for j in (0, 1):
        a = np.empty([128, 256], np.float32)
        for q in range(256):
            a[:, q] = alog_v[16 * q + p // 8, 8 * j + p % 8]
        alogs.append(np.ascontiguousarray(a))

    def tile32(v):
        return np.ascontiguousarray(v.reshape(32, 128).T)

    dvmask = []
    for j in range(2):
        dm = np.zeros([128, 64], np.float32)
        dm[:, 32 * j:32 * j + 32] = tile32(dv)
        dvmask.append(dm)

    pc = np.zeros([64, KC * 16 * 128], np.float32)
    for k in range(KC):
        for ct in range(32):
            c = 128 * ct + p
            srcloc = (p // 64) * 16 + p % 16
            pc[32 * (ct % 2) + srcloc, (k * 16 + ct // 2) * 128 + p] = cw[c, k]

    psp = np.zeros([128, 8 * 128], np.float32)
    pwm = np.zeros([128, 8 * 128], np.float32)
    for kk in range(8):
        psp[16 * kk + p // 8, kk * 128 + p] = 1.0
        pwm[16 * kk + p // 8, kk * 128 + p] = 1.0
    pb = np.zeros([128, 16 * 128], np.float32)
    for kk in range(16):
        pb[8 * kk + p % 8, kk * 128 + p] = 1.0
    pr = np.zeros([128, 8 * 128], np.float32)
    for kk in range(8):
        pr[p, kk * 128 + 16 * kk + p // 8] = 1.0

    ident = np.eye(128, dtype=np.float32)
    onesm = np.ones([128, 128], np.float32)

    maps = []
    for core in range(8):
        b, j = core // 2, core % 2
        x0c = np.ascontiguousarray(x_in[b, j * TH:(j + 1) * TH, :].T)
        m = {
            "x0": x0c,
            "wm_sh": np.ascontiguousarray(wmv[j][512 * b:512 * (b + 1), :]),
            "wz_sh": np.ascontiguousarray(wzT[256 * core:256 * (core + 1), :]),
            "wop_sh": np.ascontiguousarray(wopT[512 * core:512 * (core + 1), :]),
            "wfc_sh": np.ascontiguousarray(wfcT[256 * core:256 * (core + 1), :]),
            "wpj_sh": np.ascontiguousarray(wpjT[1024 * core:1024 * (core + 1), :]),
            "wdtin": wdtinT, "wdtp": wdtpT,
            "alog": alogs[j],
            "pconv": pc, "convb": tile32(cb), "dvec": dvmask[j], "dtb": tile32(dtb_v),
            "p_sp": psp, "p_w": pwm.astype(_bf16), "p_b": pb.astype(_bf16), "p_r": pr,
            "ones_i": onesm, "ident": ident,
        }
        maps.append(m)
    return maps


def kernel(**inputs):
    maps = _host_layout(inputs)
    nc = build_program()
    res = run_bass_kernel_spmd(nc, maps, core_ids=list(range(8)))
    outa = np.empty([B_, TF, D], np.float32)
    for core in range(8):
        b, j = core // 2, core % 2
        outa[b, j * TH:(j + 1) * TH, :] = np.asarray(res.results[core]["out"]).T
    return outa


# version 04 (final)
# speedup vs baseline: 1.0435x; 1.0435x over previous
import numpy as np
from contextlib import ExitStack

import concourse.bass as bass
import concourse.mybir as mybir
from concourse.bass_utils import run_bass_kernel_spmd

F32 = mybir.dt.float32
F32R = mybir.dt.float32r
BF16 = mybir.dt.bfloat16
AF = mybir.ActivationFunctionType
ALU = mybir.AluOpType
AXX = mybir.AxisListType.X

B_, TF, D = 4, 2048, 2048
TH = TF // 2
DI, DXB, NST, DTR, KC = 4096, 1024, 16, 128, 4
NH = NST // 2
DFF = 4 * D
H, BH = DI // 16, DI // 64

TC = 256
NCH = TF // TC
FC = 512
NFC = TH // FC
ZC = 512
NZC = TH // ZC
NPF = DXB + BH * NH + H * NH
XO, BO, CO = 0, DXB, DXB + BH * NH
MT_X, MT_B, MT_C = DXB // 128, (BH * NH) // 128, (H * NH) // 128
KT_D = D // 128
MT_F = DFF // 128
MT_D = D // 128
CTI = DI // 128

EPS5, EPS6, EPSW = 1e-5, 1e-6, 1e-6
QB = 128.0

_NPHASE = 5


class Sem:
    def __init__(self, st, nc, name):
        self.h = st.enter_context(nc.semaphore(name))
        self.n = 0


def build_program():
    nc = bass.Bass("TRN2", target_bir_lowering=False)
    dt = nc.dram_tensor

    x0 = dt("x0", [D, TH], F32, kind="ExternalInput").ap()
    wm_sh = dt("wm_sh", [D // 4, NPF], F32, kind="ExternalInput").ap()
    wz_sh = dt("wz_sh", [D // 8, DI], F32, kind="ExternalInput").ap()
    wop_sh = dt("wop_sh", [DI // 8, D], F32, kind="ExternalInput").ap()
    wfc_sh = dt("wfc_sh", [D // 8, DFF], F32, kind="ExternalInput").ap()
    wpj_sh = dt("wpj_sh", [DFF // 8, D], F32, kind="ExternalInput").ap()
    wdtin_i = dt("wdtin", [D, DTR], F32, kind="ExternalInput").ap()
    wdtp_i = dt("wdtp", [DTR, DI], F32, kind="ExternalInput").ap()
    alog_i = dt("alog", [128, 256], F32, kind="ExternalInput").ap()
    pconv_i = dt("pconv", [64, KC * 16 * 128], F32R, kind="ExternalInput").ap()
    convb_i = dt("convb", [128, 32], F32, kind="ExternalInput").ap()
    dvec_i = dt("dvec", [128, 64], F32, kind="ExternalInput").ap()
    dtb_i = dt("dtb", [128, 32], F32, kind="ExternalInput").ap()
    p_sp_i = dt("p_sp", [128, 8 * 128], F32R, kind="ExternalInput").ap()
    p_w_i = dt("p_w", [128, 8 * 128], BF16, kind="ExternalInput").ap()
    p_b_i = dt("p_b", [128, 16 * 128], BF16, kind="ExternalInput").ap()
    p_r_i = dt("p_r", [128, 8 * 128], F32R, kind="ExternalInput").ap()
    ones_i = dt("ones_i", [128, 128], F32, kind="ExternalInput").ap()
    ident_i = dt("ident", [128, 128], F32, kind="ExternalInput").ap()
    out = dt("out", [D, TH], F32, kind="ExternalOutput").ap()

    b_wfc = dt("b_wfc", [2, 64, 128, 128], BF16).ap()
    g_wfc = dt("g_wfc", [16, 64, 128, 128], BF16).ap()
    b_wpj = dt("b_wpj", [8, 16, 128, 128], BF16).ap()
    g_wpj = dt("g_wpj", [64, 16, 128, 128], BF16).ap()
    b_wm = dt("b_wm", [4, 28, 128, 128], F32R).ap()
    g_wm = dt("g_wm", [16, 28, 128, 128], F32R).ap()
    b_wz = dt("b_wz", [2, 32, 128, 128], F32R).ap()
    g_wz = dt("g_wz", [16, 32, 128, 128], F32R).ap()
    b_wop = dt("b_wop", [4, 16, 128, 128], F32R).ap()
    g_wop = dt("g_wop", [32, 16, 128, 128], F32R).ap()
    g_wdtin = dt("g_wdtin", [16, 1, 128, 128], F32R).ap()
    g_wdtp = dt("g_wdtp", [1, 32, 128, 128], F32R).ap()
    b_gam = dt("b_gam", [1, 32], F32).ap()
    g_gam = dt("g_gam", [8, 32], F32).ap()
    x1me = dt("x1me", [D, TH], F32).ap()
    x1g = dt("x1g", [2, D, TH], F32).ap()
    ybuf = dt("ybuf", [2, DI, TH], F32).ap()
    ysum_b = dt("ysum", [DI, TH], F32).ap()
    x2buf = dt("x2buf", [D, TH], F32).ap()

    dbg = {}
    if _NPHASE == 1:
        dbg["xq"] = dt("dbg_xq", [D, FC], BF16, kind="ExternalOutput").ap()
        dbg["hq"] = dt("dbg_hq", [DFF, FC], BF16, kind="ExternalOutput").ap()
        dbg["rows"] = dt("dbg_rows", [1, 8 * FC], F32, kind="ExternalOutput").ap()
        dbg["rdup"] = dt("dbg_rdup", [128, 3 * FC], F32, kind="ExternalOutput").ap()
    if _NPHASE < 5:
        dbg["x1g"] = dt("dbg_x1g", [2 * D, TH], F32, kind="ExternalOutput").ap()
        dbg["msc"] = dt("dbg_msc", [2, 8 * FC], F32, kind="ExternalOutput").ap()
    if 3 <= _NPHASE < 5:
        dbg["spn"] = dt("dbg_spn", [DI, TC], F32, kind="ExternalOutput").ap()
        dbg["ypart"] = dt("dbg_ypart", [DI, TF], F32, kind="ExternalOutput").ap()
    if 4 <= _NPHASE < 5:
        dbg["x2"] = dt("dbg_x2", [D, TH], F32, kind="ExternalOutput").ap()

    st = ExitStack()
    with st:
        sb = nc.sbuf_tensor
        psal = nc.psum_tensor

        s_pe = Sem(st, nc, "s_pe")
        s_dv = Sem(st, nc, "s_dv")
        s_ac = Sem(st, nc, "s_ac")
        s_edv = Sem(st, nc, "s_edv")
        s_eac = Sem(st, nc, "s_eac")
        s_cst = Sem(st, nc, "s_cst")
        s_gw = Sem(st, nc, "s_gw")
        s_gs = Sem(st, nc, "s_gs")
        s_tw = Sem(st, nc, "s_tw")
        s_x = Sem(st, nc, "s_x")
        s_out = Sem(st, nc, "s_out")
        s_st = [Sem(st, nc, f"s_st{i}") for i in range(2)]
        s_ys = [Sem(st, nc, f"s_ys{i}") for i in range(2)]
        s_zy = [Sem(st, nc, f"s_zy{i}") for i in range(2)]
        s_dbg = Sem(st, nc, "s_dbg")
        WR_N = 4
        s_wsl = [Sem(st, nc, f"s_wsl{i}") for i in range(WR_N)]
        s_ccl = {}
        for name in ("gam", "wfc", "wpj", "wm", "wz", "wop", "x1", "y"):
            s_ccl[name] = Sem(st, nc, f"s_cc_{name}")

        _ectr = {"dv": 0, "ac": 0}

        def edge_dv(e, inst):
            _ectr["dv"] += 1
            inst.then_inc(s_edv.h, 1)
            e.wait_ge(s_edv.h, _ectr["dv"])

        def edge_ac(e, inst):
            _ectr["ac"] += 1
            inst.then_inc(s_eac.h, 1)
            e.wait_ge(s_eac.h, _ectr["ac"])

        ident = st.enter_context(sb("identt", [128, 128], F32))
        ones_f = st.enter_context(sb("ones_f", [128, 128], F32))
        ones_r = st.enter_context(sb("ones_r", [128, 128], F32R))
        ones_b = st.enter_context(sb("ones_b", [128, 128], BF16))
        gamv = st.enter_context(sb("gamv", [128, 6], F32))
        gsc = st.enter_context(sb("gsc", [1, 8], F32))
        cb_t = st.enter_context(sb("cb_t", [128, 32], F32))
        dv_t = st.enter_context(sb("dv_t", [128, 64], F32))
        dtbp = st.enter_context(sb("dtbp", [128, 32], F32))
        cst = st.enter_context(sb("cst", [1, 4], F32))
        cstp = st.enter_context(sb("cstp", [128, 4], F32))
        gpart_keep = st.enter_context(sb("gpart", [1, 32], F32))
        wr_slabs = [st.enter_context(sb(f"wrs{i}", [128, 16 * 128], F32R)) for i in range(WR_N)]
        wr_state = {"iss": 0, "done": []}

        def wr_mark(n, v):
            for _ in range(n):
                wr_state["done"].append(v)

        def new_ops():
            return {e: [] for e in ("sync", "tensor", "vector", "scalar", "gpsimd")}

        def run_block(o):
            with nc.Block() as blk:
                for e in ("sync", "tensor", "vector", "scalar", "gpsimd"):
                    if o[e]:
                        getattr(blk, e)(lambda eng, l=tuple(o[e]): [f(eng) for f in l])

        def wpanel(o, g_ap, m, k0, kt, dtype, wait_cc=None):
            slot = wr_state["iss"] % WR_N
            need_idx = wr_state["iss"] - WR_N
            need = wr_state["done"][need_idx] if need_idx >= 0 else 0
            wr_state["iss"] += 1
            sem = s_wsl[slot]

            def ld(e, slot=slot, need=need, m=m, k0=k0, kt=kt, wcc=wait_cc, sem=sem, g_ap=g_ap, dtype=dtype):
                if wcc is not None:
                    e.wait_ge(wcc[0].h, wcc[1])
                if need > 0:
                    e.wait_ge(s_pe.h, need)
                if dtype is BF16:
                    dst = wr_slabs[slot][:, :].bitcast(BF16)[:, 0:kt * 128]
                else:
                    dst = wr_slabs[slot][:, 0:kt * 128]
                e.dma_start(out=dst.rearrange("p (k c) -> p k c", k=kt),
                            in_=g_ap[k0:k0 + kt, m].transpose([1, 0, 2])).then_inc(sem.h, 16)

            o["sync"].append(ld)
            sem.n += 16
            if dtype is BF16:
                view = wr_slabs[slot][:, :].bitcast(BF16)[:, 0:kt * 128]
            else:
                view = wr_slabs[slot][:, 0:kt * 128]
            return view, sem, sem.n

        cc_done = {}
        with ExitStack() as stp:
            o = new_ops()
            gred = stp.enter_context(sb("gred", [128, 16], F32))
            gta = stp.enter_context(sb("gta", [128, 4096], F32))
            gtb = stp.enter_context(sb("gtb", [128, 2048], F32))
            g8 = stp.enter_context(sb("g8", [1, 256], F32))
            twork = stp.enter_context(sb("twork", [128, 8192], F32))
            tout_b = stp.enter_context(sb("tout_b", [128, 8192], BF16))
            tout_r = stp.enter_context(sb("tout_r", [128, 8192], F32R))
            ps_g = stp.enter_context(psal("ps_g", [128, 16], F32))
            ps_t = stp.enter_context(psal("ps_t", [128, 128], F32))

            def p0_sync(e):
                e.dma_start(out=ident[:, :], in_=ident_i).then_inc(s_cst.h, 16)
                e.dma_start(out=ones_f[:, :], in_=ones_i).then_inc(s_cst.h, 16)
                e.dma_start(out=cb_t[:, :], in_=convb_i).then_inc(s_cst.h, 16)
                e.dma_start(out=dv_t[:, :], in_=dvec_i).then_inc(s_cst.h, 16)
                e.dma_start(out=dtbp[:, :], in_=dtb_i).then_inc(s_cst.h, 16)

            o["sync"].append(p0_sync)
            s_cst.n += 5 * 16
            v_cst = s_cst.n

            def p0_cst0(e):
                e.memset(gred[:, :], 0.0)
                e.memset(cst[0:1, 0:1], EPS5)
                e.memset(cst[0:1, 1:2], EPS6)
                e.memset(cstp[:, 0:1], EPS5)
                e.memset(cstp[:, 1:2], EPS6)
                e.memset(cstp[:, 2:3], 1.0).then_inc(s_dv.h, 1)

            o["vector"].append(p0_cst0)
            s_dv.n += 1

            def p0_cstA(e):
                e.wait_ge(s_cst.h, v_cst)
                e.activation(out=ones_r[:, :], in_=ones_f[:, :], func=AF.Copy)
                e.activation(out=ones_b[:, :], in_=ones_f[:, :], func=AF.Copy).then_inc(s_ac.h, 1)

            o["scalar"].append(p0_cstA)
            s_ac.n += 1
            v_cstA = s_ac.n

            gjobs = [(wfc_sh[128 * i:128 * (i + 1), :], 8192, i) for i in range(2)] + \
                    [(wpj_sh[128 * i:128 * (i + 1), :], 2048, 4 + i) for i in range(8)]
            for src_ap, cols, slot in gjobs:
                free_at = s_dv.n

                def ld(e, src_ap=src_ap, cols=cols, free_at=free_at):
                    e.wait_ge(s_dv.h, free_at)
                    e.dma_start(out=twork[:, 0:cols], in_=src_ap).then_inc(s_gw.h, 16)
                o["sync"].append(ld)
                s_gw.n += 16
                v = s_gw.n

                def red(e, cols=cols, slot=slot, v=v):
                    e.wait_ge(s_gw.h, v)
                    src, n, first, ping = twork, cols, True, 0
                    while n > 1:
                        n2 = n // 2
                        dst = (gta, gtb)[ping]
                        ii = e.tensor_reduce(out=dst[:, 0:n2],
                                             in_=src[:, 0:n].rearrange("p (k c) -> p c k", k=2),
                                             axis=AXX, op=ALU.add, apply_absolute_value=first)
                        edge_dv(e, ii)
                        src, n, first, ping = dst, n2, False, 1 - ping
                    e.tensor_copy(out=gred[:, slot:slot + 1], in_=src[:, 0:1]).then_inc(s_dv.h, 1)
                o["vector"].append(red)
                s_dv.n += 1
            v_gred = s_dv.n

            def p0_gmrg(e):
                e.wait_ge(s_dv.h, v_gred)
                for i in range(4):
                    ii = e.tensor_add(out=gred[:, 4 + i:5 + i], in0=gred[:, 4 + i:5 + i],
                                      in1=gred[:, 8 + i:9 + i])
                e.tensor_add(out=gred[:, 12:13], in0=gred[:, 0:1], in1=gred[:, 1:2])
                edge_dv(e, ii)
                e.tensor_add(out=gred[:, 13:14], in0=gred[:, 4:5], in1=gred[:, 6:7])
                ii = e.tensor_add(out=gred[:, 14:15], in0=gred[:, 5:6], in1=gred[:, 7:8])
                edge_dv(e, ii)
                ii = e.tensor_add(out=gred[:, 13:14], in0=gred[:, 13:14], in1=gred[:, 14:15])
                edge_dv(e, ii)
                e.tensor_scalar(out=gred[:, 12:16], in0=gred[:, 12:16], scalar1=1.0 / (D * DFF),
                                scalar2=None, op0=ALU.mult).then_inc(s_dv.h, 1)

            o["vector"].append(p0_gmrg)
            s_dv.n += 1
            v_gmrg = s_dv.n

            def p0_gmm(e):
                e.wait_ge(s_dv.h, v_gmrg)
                e.wait_ge(s_ac.h, v_cstA)
                e.wait_ge(s_cst.h, 16)
                e.transpose(ps_t[0:4, 0:128], gred[:, 12:16], ident[:, :]).then_inc(s_pe.h, 1)

            o["tensor"].append(p0_gmm)
            s_pe.n += 1
            v_gmm = s_pe.n

            def p0_gcp0(e):
                e.wait_ge(s_pe.h, v_gmm)
                src, n, ping = ps_t, 128, 0
                while n > 1:
                    n2 = n // 2
                    dst = (gta, gtb)[ping]
                    ii = e.tensor_reduce(out=dst[0:4, 0:n2],
                                         in_=src[0:4, 0:n].rearrange("p (k c) -> p c k", k=2),
                                         axis=AXX, op=ALU.add)
                    edge_dv(e, ii)
                    src, n, ping = dst, n2, 1 - ping
                e.memset(gpart_keep[0:1, 0:32], 0.0).then_inc(s_dv.h, 1)

            o["vector"].append(p0_gcp0)
            s_dv.n += 1
            v_gsum = s_dv.n

            w_gst = s_out.n + 16

            def p0_gst(e):
                e.wait_ge(s_dv.h, v_gsum)
                e.dma_start(out=b_gam, in_=gpart_keep[0:1, 0:32]).then_inc(s_out.h, 16)
                e.wait_ge(s_out.h, w_gst)
                e.dma_start(out=b_gam[0:1, 12:16], in_=gta[0:4, 0:1]).then_inc(s_out.h, 16)
                e.wait_ge(s_out.h, w_gst + 16)
                e.collective_compute("AllGather", ALU.bypass, replica_groups=[list(range(8))],
                                     ins=[b_gam.opt()], outs=[g_gam.opt()]).then_inc(s_ccl["gam"].h, 1)

            o["gpsimd"].append(p0_gst)
            s_out.n += 32
            s_ccl["gam"].n += 1

            def p0_gld(e):
                e.wait_ge(s_ccl["gam"].h, 1)
                e.dma_start(out=g8[0:1, 0:256], in_=g_gam.rearrange("r c -> (r c)")).then_inc(s_gs.h, 16)

            o["sync"].append(p0_gld)
            s_gs.n += 16

            def p0_gfold(e):
                e.wait_ge(s_gs.h, 16)
                ii = e.tensor_reduce(out=gta[0:1, 0:128],
                                     in_=g8[0:1, 0:256].rearrange("o (k c) -> o c k", k=2),
                                     axis=AXX, op=ALU.add)
                edge_dv(e, ii)
                ii = e.tensor_reduce(out=gtb[0:1, 0:64],
                                     in_=gta[0:1, 0:128].rearrange("o (k c) -> o c k", k=2),
                                     axis=AXX, op=ALU.add)
                edge_dv(e, ii)
                ii = e.tensor_reduce(out=gta[0:1, 0:32],
                                     in_=gtb[0:1, 0:64].rearrange("o (k c) -> o c k", k=2),
                                     axis=AXX, op=ALU.add)
                edge_dv(e, ii)
                e.tensor_scalar(out=gta[0:1, 0:32], in0=gta[0:1, 0:32],
                                scalar1=float(1.0 - 2.0 ** -23), scalar2=None,
                                op0=ALU.mult).then_inc(s_dv.h, 1)

            o["vector"].append(p0_gfold)
            s_dv.n += 1
            v_gfold = s_dv.n

            def p0_gst2(e):
                e.wait_ge(s_dv.h, v_gfold)
                e.dma_start(out=b_gam, in_=gta[0:1, 0:32]).then_inc(s_gs.h, 16)

            o["gpsimd"].append(p0_gst2)
            s_gs.n += 16

            def p0_gld2(e):
                e.wait_ge(s_gs.h, 32)
                e.dma_start(out=gsc[:, 0:2], in_=b_gam[0:1, 12:14]).then_inc(s_gs.h, 16)

            o["sync"].append(p0_gld2)
            s_gs.n += 16

            def p0_gdup(e):
                e.wait_ge(s_gs.h, 48)
                e.matmul(ps_g[:, 0:2], ones_f[0:1, :], gsc[0:1, 0:2], start=True, stop=True).then_inc(s_pe.h, 1)

            o["tensor"].append(p0_gdup)
            s_pe.n += 1
            v_gdup = s_pe.n

            def p0_gcp(e):
                e.wait_ge(s_pe.h, v_gdup)
                ii = e.activation(out=gamv[:, 0:2], in_=ps_g[:, 0:2], func=AF.Ln, bias=cstp[:, 1:2])
                edge_ac(e, ii)
                e.activation(out=gamv[:, 0:2], in_=gamv[:, 0:2], func=AF.Exp, scale=-1.0).then_inc(s_ac.h, 1)

            o["scalar"].append(p0_gcp)
            s_ac.n += 1
            v_gamv_ac = s_ac.n

            def p0_gcp2(e):
                e.wait_ge(s_ac.h, v_gamv_ac)
                ii = e.tensor_scalar(out=gamv[:, 2:4], in0=ps_g[:, 0:2],
                                     scalar1=EPS6, scalar2=None, op0=ALU.add)
                edge_dv(e, ii)
                for _ in range(2):
                    ii = e.tensor_mul(out=gamv[:, 4:6], in0=gamv[:, 0:2], in1=gamv[:, 2:4])
                    edge_dv(e, ii)
                    ii = e.tensor_scalar(out=gamv[:, 4:6], in0=gamv[:, 4:6],
                                         scalar1=-1.0, scalar2=2.0, op0=ALU.mult, op1=ALU.add)
                    edge_dv(e, ii)
                    ii = e.tensor_mul(out=gamv[:, 0:2], in0=gamv[:, 0:2], in1=gamv[:, 4:6])
                    edge_dv(e, ii)
                e.tensor_copy(out=gred[:, 15:16], in_=gred[:, 15:16]).then_inc(s_dv.h, 1)

            o["vector"].append(p0_gcp2)
            s_dv.n += 1
            v_gamv = s_dv.n

            def cast_job(src_ap, dst_tiled, mt, tern_col):
                st_wait = s_out.n
                dv_wait = s_dv.n

                def ld(e, w=st_wait, wd=dv_wait, mt=mt, src_ap=src_ap):
                    e.wait_ge(s_out.h, w)
                    e.wait_ge(s_dv.h, wd)
                    e.dma_start(out=twork[:, 0:mt * 128], in_=src_ap).then_inc(s_tw.h, 16)
                o["sync"].append(ld)
                s_tw.n += 16
                v = s_tw.n

                if tern_col is not None:
                    def sgn(e, v=v, mt=mt):
                        e.wait_ge(s_tw.h, v)
                        e.activation(out=tout_b[:, 0:mt * 128], in_=twork[:, 0:mt * 128],
                                     func=AF.Sign).then_inc(s_ac.h, 1)
                    o["scalar"].append(sgn)
                    s_ac.n += 1
                    va = s_ac.n

                    def msk1(e, v=v, va=va, c=tern_col, mt=mt):
                        e.wait_ge(s_tw.h, v)
                        e.wait_ge(s_dv.h, v_gamv)
                        e.wait_ge(s_ac.h, va)
                        e.tensor_scalar(out=twork[:, 0:mt * 128], in0=twork[:, 0:mt * 128],
                                        scalar1=gamv[:, c:c + 1], scalar2=None, op0=ALU.mult).then_inc(s_dv.h, 1)
                    o["vector"].append(msk1)
                    s_dv.n += 1
                    vm1 = s_dv.n

                    def mabs(e, vm1=vm1, mt=mt):
                        e.wait_ge(s_dv.h, vm1)
                        e.activation(out=twork[:, 0:mt * 128], in_=twork[:, 0:mt * 128],
                                     func=AF.Abs).then_inc(s_ac.h, 1)
                    o["scalar"].append(mabs)
                    s_ac.n += 1
                    va2 = s_ac.n

                    def msk2(e, va2=va2, mt=mt):
                        e.wait_ge(s_ac.h, va2)
                        ii = e.tensor_scalar(out=twork[:, 0:mt * 128], in0=twork[:, 0:mt * 128],
                                             scalar1=0.5, scalar2=None, op0=ALU.is_gt)
                        edge_dv(e, ii)
                        e.tensor_mul(out=tout_b[:, 0:mt * 128], in0=twork[:, 0:mt * 128],
                                     in1=tout_b[:, 0:mt * 128]).then_inc(s_dv.h, 1)
                    o["vector"].append(msk2)
                    s_dv.n += 1
                    vd, buf = s_dv.n, tout_b
                else:
                    def cv(e, v=v, mt=mt):
                        e.wait_ge(s_tw.h, v)
                        e.tensor_copy(out=tout_r[:, 0:mt * 128], in_=twork[:, 0:mt * 128]).then_inc(s_dv.h, 1)
                    o["vector"].append(cv)
                    s_dv.n += 1
                    vd, buf = s_dv.n, tout_r

                def stv(e, vd=vd, buf=buf, mt=mt, dst=dst_tiled):
                    e.wait_ge(s_dv.h, vd)
                    e.dma_start(out=dst.transpose([1, 0, 2]),
                                in_=buf[:, 0:mt * 128].rearrange("p (m c) -> p m c", m=mt)).then_inc(s_out.h, 16)
                o["gpsimd"].append(stv)
                s_out.n += 16

            for i in range(2):
                cast_job(wfc_sh[128 * i:128 * (i + 1), :], b_wfc[i], 64, 0)
            for i in range(8):
                cast_job(wpj_sh[128 * i:128 * (i + 1), :], b_wpj[i], 16, 1)
            for i in range(4):
                cast_job(wm_sh[128 * i:128 * (i + 1), :], b_wm[i], 28, None)
            for i in range(2):
                cast_job(wz_sh[128 * i:128 * (i + 1), :], b_wz[i], 32, None)
            for i in range(4):
                cast_job(wop_sh[128 * i:128 * (i + 1), :], b_wop[i], 16, None)
            for i in range(16):
                cast_job(wdtin_i[128 * i:128 * (i + 1), :], g_wdtin[i], 1, None)
            cast_job(wdtp_i, g_wdtp[0], 32, None)

            v_prep_st = s_out.n

            def p0_cc(e):
                e.wait_ge(s_out.h, v_prep_st)
                for name, i_ap, o_ap in [("wfc", b_wfc, g_wfc), ("wpj", b_wpj, g_wpj),
                                         ("wm", b_wm, g_wm), ("wz", b_wz, g_wz), ("wop", b_wop, g_wop)]:
                    rg = [[0, 2, 4, 6], [1, 3, 5, 7]] if name == "wm" else [list(range(8))]
                    e.collective_compute("AllGather", ALU.bypass, replica_groups=rg,
                                         ins=[i_ap.opt()], outs=[o_ap.opt()]).then_inc(s_ccl[name].h, 1)

            o["gpsimd"].append(p0_cc)
            for name in ("wfc", "wpj", "wm", "wz", "wop"):
                s_ccl[name].n += 1
                cc_done[name] = (s_ccl[name], 1)

            run_block(o)

        def emit_ffn(src_hbm, dst_hbm, tag, gates=()):
            with ExitStack() as stf:
                o = new_ops()
                if gates:
                    def fgate(e, gg=tuple(gates)):
                        for sh, vv in gg:
                            if vv:
                                e.wait_ge(sh, vv)
                    for _eng in ("sync", "tensor", "vector", "scalar"):
                        o[_eng].append(fgate)
                xsrc = stf.enter_context(sb(f"{tag}_xsrc", [128, KT_D * FC], F32))
                xq = stf.enter_context(sb(f"{tag}_xq", [128, KT_D * FC], BF16))
                hsl = stf.enter_context(sb(f"{tag}_h", [128, MT_F * FC], BF16))
                sqx = stf.enter_context(sb(f"{tag}_sqx", [128, 4 * FC], F32R))
                sqh = stf.enter_context(sb(f"{tag}_sqh", [128, 8 * FC], BF16))
                rows = stf.enter_context(sb(f"{tag}_rows", [1, 8 * FC], F32))
                rdup = stf.enter_context(sb(f"{tag}_rdup", [128, 3 * FC], F32))
                tre = stf.enter_context(sb(f"{tag}_tre", [128, 2 * FC], F32))
                mxt = stf.enter_context(sb(f"{tag}_mxt", [128, 32], F32))
                evs = stf.enter_context(sb(f"{tag}_evs", [128, 2 * FC], F32))
                p_mm = [stf.enter_context(psal(f"{tag}_pmm{i}", [128, FC], F32)) for i in range(2)]
                p_s1 = stf.enter_context(psal(f"{tag}_ps1", [1, FC], F32))
                p_s2 = stf.enter_context(psal(f"{tag}_ps2", [1, FC], F32))
                p_du = stf.enter_context(psal(f"{tag}_pdu", [128, FC], F32))
                p_tp = [stf.enter_context(psal(f"{tag}_ptp{i}", [128, 128], F32)) for i in range(2)]

                RS = {k: i for i, k in enumerate(["m_", "v_", "g_", "s_", "a_", "mx", "t0", "t1"])}

                def row(name):
                    i = RS[name]
                    return rows[0:1, i * FC:(i + 1) * FC]

                NTP = FC // 128
                stats_free = {"v": 0}

                def grouped_stats(slab, nkt, grp, sq_slab, bf, wait_ac, wait_ld):
                    ngrp = nkt // grp
                    for g in range(ngrp):
                        v_pe_prev = s_pe.n

                        def sq_a(e, g=g, v_pe_prev=v_pe_prev, wait_ac=wait_ac, wait_ld=wait_ld,
                                 grp=grp, slab=slab, sq_slab=sq_slab):
                            if wait_ac:
                                e.wait_ge(s_ac.h, wait_ac)
                            if wait_ld:
                                e.wait_ge(s_x.h, wait_ld)
                            if g > 0:
                                e.wait_ge(s_pe.h, v_pe_prev)
                            for i in range(grp):
                                k = g * grp + i
                                ii = e.activation(out=sq_slab[:, i * FC:(i + 1) * FC],
                                                  in_=slab[:, k * FC:(k + 1) * FC], func=AF.Square)
                            ii.then_inc(s_ac.h, 1)
                        o["scalar"].append(sq_a)
                        s_ac.n += 1
                        v_sq = s_ac.n
                        v_free = stats_free["v"]

                        def mmst(e, g=g, v_sq=v_sq, v_free=v_free, wait_ld=wait_ld, grp=grp,
                                 bf=bf, nkt=nkt, slab=slab, sq_slab=sq_slab):
                            if wait_ld:
                                e.wait_ge(s_x.h, wait_ld)
                            e.wait_ge(s_ac.h, v_sq)
                            if g == 0 and v_free:
                                e.wait_ge(s_dv.h, v_free)
                            oc1 = ones_b if bf else ones_f
                            oc2 = ones_b if bf else ones_r
                            for i in range(grp):
                                k = g * grp + i
                                e.matmul(p_s1[0:1, :], oc1[:, 0:1], slab[:, k * FC:(k + 1) * FC],
                                         start=(k == 0), stop=(k == nkt - 1), skip_group_check=True)
                                ii = e.matmul(p_s2[0:1, :], oc2[:, 0:1], sq_slab[:, i * FC:(i + 1) * FC],
                                              start=(k == 0), stop=(k == nkt - 1), skip_group_check=True)
                            ii.then_inc(s_pe.h, 1)
                        o["tensor"].append(mmst)
                        s_pe.n += 1
                    return s_pe.n

                def rowdup(slot, r_ap, wait_dv):
                    def md(e, wait_dv=wait_dv, r_ap=r_ap):
                        e.wait_ge(s_dv.h, wait_dv)
                        e.wait_ge(s_dv.h, md.vprev)
                        e.matmul(p_du[:, :], ones_f[0:1, :], r_ap, start=True, stop=True).then_inc(s_pe.h, 1)
                    md.vprev = s_dv.n
                    o["tensor"].append(md)
                    s_pe.n += 1
                    v_md = s_pe.n

                    def mcp(e, slot=slot, v_md=v_md):
                        e.wait_ge(s_pe.h, v_md)
                        e.tensor_copy(out=rdup[:, slot * FC:(slot + 1) * FC], in_=p_du[:, :]).then_inc(s_dv.h, 1)
                    o["vector"].append(mcp)
                    s_dv.n += 1
                    return s_dv.n

                def ln_rows(v_stats, nfeat, eps_a, eps_b):
                    def rows1(e, v_stats=v_stats, nfeat=nfeat):
                        e.wait_ge(s_pe.h, v_stats)
                        ii = e.tensor_scalar(out=row("m_"), in0=p_s1[0:1, :], scalar1=1.0 / nfeat,
                                             scalar2=None, op0=ALU.mult)
                        e.tensor_scalar(out=row("t0"), in0=p_s2[0:1, :], scalar1=1.0 / nfeat,
                                        scalar2=None, op0=ALU.mult)
                        edge_dv(e, ii)
                        ii = e.tensor_mul(out=row("t1"), in0=row("m_"), in1=row("m_"))
                        edge_dv(e, ii)
                        e.tensor_sub(out=row("v_"), in0=row("t0"), in1=row("t1")).then_inc(s_dv.h, 1)
                    o["vector"].append(rows1)
                    s_dv.n += 1
                    v_r1 = s_dv.n
                    stats_free["v"] = v_r1

                    def rows2(e, eps_a=eps_a):
                        e.wait_ge(s_dv.h, v_r1)
                        e.activation(out=row("t0"), in_=row("v_"), func=AF.Ln,
                                     bias=(cst[0:1, 0:1] if eps_a == EPS5 else cst[0:1, 1:2])).then_inc(s_ac.h, 1)
                    o["scalar"].append(rows2)
                    s_ac.n += 1
                    v_r2 = s_ac.n

                    if eps_b is not None:
                        def rows3(e, eps_a=eps_a):
                            e.wait_ge(s_ac.h, v_r2)
                            ii = e.tensor_scalar(out=row("t1"), in0=row("v_"), scalar1=eps_a,
                                                 scalar2=None, op0=ALU.add)
                            edge_dv(e, ii)
                            ii = e.reciprocal(out=row("t1"), in_=row("t1"))
                            edge_dv(e, ii)
                            e.tensor_mul(out=row("t1"), in0=row("t1"), in1=row("v_")).then_inc(s_dv.h, 1)
                        o["vector"].append(rows3)
                        s_dv.n += 1
                        v_r3 = s_dv.n

                        def rows4(e, eps_b=eps_b):
                            e.wait_ge(s_dv.h, v_r3)
                            e.activation(out=row("t1"), in_=row("t1"), func=AF.Ln,
                                         bias=(cst[0:1, 0:1] if eps_b == EPS5 else cst[0:1, 1:2])
                                         ).then_inc(s_ac.h, 1)
                        o["scalar"].append(rows4)
                        s_ac.n += 1
                        v_r4 = s_ac.n

                        def rows5(e):
                            e.wait_ge(s_ac.h, v_r4)
                            e.tensor_add(out=row("t0"), in0=row("t0"), in1=row("t1")).then_inc(s_dv.h, 1)
                        o["vector"].append(rows5)
                        s_dv.n += 1
                        v_r5 = s_dv.n
                    else:
                        v_r5 = None

                    def rows6(e, v_r5=v_r5, v_r2=v_r2):
                        if v_r5 is not None:
                            e.wait_ge(s_dv.h, v_r5)
                        else:
                            e.wait_ge(s_ac.h, v_r2)
                        e.activation(out=row("g_"), in_=row("t0"), func=AF.Exp, scale=-0.5).then_inc(s_ac.h, 1)
                    o["scalar"].append(rows6)
                    s_ac.n += 1
                    return v_r1, s_ac.n

                def maxabs_rows(slab, nkt, wait_sem, wait_val, v_m, v_g):
                    def redmm(e, nkt=nkt, wait_val=wait_val, wait_sem=wait_sem, slab=slab):
                        e.wait_ge(wait_sem.h, wait_val)
                        sv = slab[:, 0:nkt * FC].rearrange("p (k c) -> p c k", k=nkt)
                        e.tensor_reduce(out=tre[:, 0:FC], in_=sv, axis=AXX, op=ALU.max)
                        e.tensor_reduce(out=tre[:, FC:2 * FC], in_=sv, axis=AXX, op=ALU.min).then_inc(s_dv.h, 1)
                    o["vector"].append(redmm)
                    s_dv.n += 1
                    v_tree = s_dv.n

                    red_vals = []
                    for i in range(2 * NTP):
                        wait_v = v_tree if i < 2 else red_vals[i - 2]
                        half, blk = i % 2, i // 2

                        def tp(e, half=half, blk=blk, i=i, wait_v=wait_v):
                            e.wait_ge(s_dv.h, wait_v)
                            e.transpose(p_tp[i % 2][:, :], tre[:, half * FC + blk * 128: half * FC + (blk + 1) * 128],
                                        ident[:, :]).then_inc(s_pe.h, 1)
                        o["tensor"].append(tp)
                        s_pe.n += 1
                        v_tp = s_pe.n

                        def rd(e, half=half, i=i, v_tp=v_tp):
                            e.wait_ge(s_pe.h, v_tp)
                            e.tensor_reduce(out=mxt[:, i:i + 1], in_=p_tp[i % 2][:, :], axis=AXX,
                                            op=(ALU.min if half else ALU.max)).then_inc(s_dv.h, 1)
                        o["vector"].append(rd)
                        s_dv.n += 1
                        red_vals.append(s_dv.n)

                    cp_vals = []
                    for i in range(2 * NTP):
                        wait_v = red_vals[-1] if i < 2 else cp_vals[i - 2]
                        half, blk = i % 2, i // 2

                        def tpb(e, i=i, wait_v=wait_v):
                            e.wait_ge(s_dv.h, wait_v)
                            e.transpose(p_tp[i % 2][0:1, 0:128], mxt[:, i:i + 1], ident[:, :]).then_inc(s_pe.h, 1)
                        o["tensor"].append(tpb)
                        s_pe.n += 1
                        v_tpb = s_pe.n

                        def cpb(e, half=half, blk=blk, i=i, v_tpb=v_tpb, v_g=v_g):
                            e.wait_ge(s_pe.h, v_tpb)
                            if i < 2:
                                e.wait_ge(s_ac.h, v_g)
                            dstrow = row("t0") if half == 0 else row("t1")
                            e.tensor_copy(out=dstrow[0:1, blk * 128:(blk + 1) * 128],
                                          in_=p_tp[i % 2][0:1, 0:128]).then_inc(s_dv.h, 1)
                        o["vector"].append(cpb)
                        s_dv.n += 1
                        cp_vals.append(s_dv.n)
                    v_cp = s_dv.n

                    def mxr(e, v_m=v_m, v_cp=v_cp):
                        e.wait_ge(s_dv.h, v_cp)
                        e.wait_ge(s_dv.h, v_m)
                        e.tensor_sub(out=row("t0"), in0=row("t0"), in1=row("m_"))
                        ii = e.tensor_sub(out=row("t1"), in0=row("m_"), in1=row("t1"))
                        edge_dv(e, ii)
                        e.tensor_tensor(out=row("mx"), in0=row("t0"), in1=row("t1"), op=ALU.max).then_inc(s_dv.h, 1)
                    o["vector"].append(mxr)
                    s_dv.n += 1
                    return s_dv.n

                def squant_rows(v_mx, v_g, gam_col, extra):
                    def srow(e, v_mx=v_mx, v_g=v_g, gam_col=gam_col, extra=extra):
                        e.wait_ge(s_dv.h, v_mx)
                        e.wait_ge(s_ac.h, v_g)
                        e.reciprocal(out=row("s_"), in_=row("mx"))
                        ii = e.tensor_mul(out=row("a_"), in0=row("g_"), in1=row("mx"))
                        edge_dv(e, ii)
                        e.tensor_scalar(out=row("s_"), in0=row("s_"), scalar1=QB, scalar2=None, op0=ALU.mult)
                        ii = e.tensor_scalar(out=row("a_"), in0=row("a_"),
                                             scalar1=gsc[0:1, gam_col:gam_col + 1],
                                             scalar2=extra, op0=ALU.mult, op1=ALU.mult)
                        edge_dv(e, ii)
                        e.tensor_mul(out=row("t0"), in0=row("m_"), in1=row("s_")).then_inc(s_dv.h, 1)
                    o["vector"].append(srow)
                    s_dv.n += 1
                    return s_dv.n

                def quantize(slab, nkt, out_q, v_sd, v_msd):
                    def qz(e, nkt=nkt, v_sd=v_sd, v_msd=v_msd, slab=slab, out_q=out_q):
                        e.wait_ge(s_dv.h, max(v_sd, v_msd))
                        for k in range(nkt):
                            ii = e.tensor_mul(out=out_q[:, k * FC:(k + 1) * FC],
                                              in0=slab[:, k * FC:(k + 1) * FC], in1=rdup[:, 0:FC])
                        edge_dv(e, ii)
                        for k in range(nkt):
                            ii = e.tensor_sub(out=out_q[:, k * FC:(k + 1) * FC],
                                              in0=out_q[:, k * FC:(k + 1) * FC], in1=rdup[:, FC:2 * FC])
                        edge_dv(e, ii)
                        e.tensor_scalar(out=out_q[:, 0:nkt * FC], in0=out_q[:, 0:nkt * FC],
                                        scalar1=-QB, scalar2=QB, op0=ALU.max, op1=ALU.min).then_inc(s_dv.h, 1)
                    o["vector"].append(qz)
                    s_dv.n += 1
                    return s_dv.n

                for ch in range(NFC):
                    t0c = ch * FC
                    v_free = s_dv.n
                    prev_st = [s_st[0].n, s_st[1].n]

                    def ldx(e, t0c=t0c, v_free=v_free,
                            wdbg=(s_dbg.n if (tag == "f1" and ch == 1 and "xq" in dbg) else 0)):
                        e.wait_ge(s_dv.h, v_free)
                        if wdbg:
                            e.wait_ge(s_dbg.h, wdbg)
                        e.dma_start(out=xsrc[:, :].rearrange("p (k t) -> p k t", k=KT_D),
                                    in_=src_hbm[:, t0c:t0c + FC].rearrange("(k p) t -> p k t", p=128)
                                    ).then_inc(s_x.h, 16)
                    o["sync"].append(ldx)
                    s_x.n += 16
                    v_ldx = s_x.n

                    v_stats = grouped_stats(xsrc, KT_D, 4, sqx, False, 0, v_ldx)
                    v_r1, v_g = ln_rows(v_stats, D, EPS5, EPS6)
                    v_mx = maxabs_rows(xsrc, KT_D, s_x, v_ldx, v_r1, v_g)
                    v_sq = squant_rows(v_mx, v_g, 0, 1.0 / QB)
                    v_sd = rowdup(0, row("s_"), v_sq)
                    v_msd = rowdup(1, row("t0"), v_sq)
                    v_ad = rowdup(2, row("a_"), v_sq)
                    v_xq = quantize(xsrc, KT_D, xq, v_sd, v_msd)

                    evac_vals = []
                    gl_vals = []
                    for m in range(MT_F):
                        view, wsem, v_w = wpanel(o, g_wfc, m, 0, KT_D, BF16,
                                                 wait_cc=(cc_done.get("wfc") if (tag == "f1" and ch == 0 and m == 0) else None))
                        wait_ev = evac_vals[m - 2] if m >= 2 else 0

                        def mm(e, view=view, wsem=wsem, v_w=v_w, m=m, wait_ev=wait_ev, v_xq=v_xq):
                            e.wait_ge(wsem.h, v_w)
                            e.wait_ge(s_dv.h, v_xq)
                            if wait_ev:
                                e.wait_ge(s_dv.h, wait_ev)
                            pm = p_mm[m % 2]
                            for k in range(KT_D):
                                ii = e.matmul(pm[:, :], view[:, k * 128:(k + 1) * 128],
                                              xq[:, k * FC:(k + 1) * FC], start=(k == 0), stop=(k == KT_D - 1))
                            ii.then_inc(s_pe.h, 1)
                        o["tensor"].append(mm)
                        s_pe.n += 1
                        wr_mark(1, s_pe.n)
                        v_mm = s_pe.n

                        wait_gl = gl_vals[m - 2] if m >= 2 else 0

                        def ev(e, m=m, v_mm=v_mm, v_ad=v_ad, wait_gl=wait_gl, prev_st=tuple(prev_st)):
                            e.wait_ge(s_pe.h, v_mm)
                            e.wait_ge(s_dv.h, v_ad)
                            if wait_gl:
                                e.wait_ge(s_ac.h, wait_gl)
                            if m < 2 and prev_st[m % 2]:
                                e.wait_ge(s_st[m % 2].h, prev_st[m % 2])
                            e.tensor_mul(out=evs[:, (m % 2) * FC:(m % 2 + 1) * FC], in0=p_mm[m % 2][:, :],
                                         in1=rdup[:, 2 * FC:3 * FC]).then_inc(s_dv.h, 1)
                        o["vector"].append(ev)
                        s_dv.n += 1
                        evac_vals.append(s_dv.n)
                        v_ev = s_dv.n

                        def gl(e, m=m, v_ev=v_ev):
                            e.wait_ge(s_dv.h, v_ev)
                            e.activation(out=hsl[:, m * FC:(m + 1) * FC],
                                         in_=evs[:, (m % 2) * FC:(m % 2 + 1) * FC], func=AF.Gelu).then_inc(s_ac.h, 1)
                        o["scalar"].append(gl)
                        s_ac.n += 1
                        gl_vals.append(s_ac.n)
                    v_gelu = s_ac.n

                    v_hstats = grouped_stats(hsl, MT_F, 8, sqh, True, v_gelu, 0)
                    v_hr1, v_hg = ln_rows(v_hstats, DFF, EPS6, None)
                    v_hmx = maxabs_rows(hsl, MT_F, s_ac, v_gelu, v_hr1, v_hg)
                    v_hs = squant_rows(v_hmx, v_hg, 1, 0.5 / QB)
                    v_hsd = rowdup(0, row("s_"), v_hs)
                    v_hmsd = rowdup(1, row("t0"), v_hs)
                    v_had = rowdup(2, row("a_"), v_hs)
                    v_hq = quantize(hsl, MT_F, hsl, v_hsd, v_hmsd)

                    if tag == "f1" and ch == 0 and "xq" in dbg:
                        def dqd(e, v_hq=v_hq, v_xq=v_xq):
                            e.wait_ge(s_dv.h, max(v_hq, v_xq))
                            e.dma_start(out=dbg["xq"].rearrange("(k p) t -> p k t", p=128),
                                        in_=xq[:, :].rearrange("p (k t) -> p k t", k=KT_D)).then_inc(s_dbg.h, 16)
                            e.dma_start(out=dbg["hq"].rearrange("(k p) t -> p k t", p=128),
                                        in_=hsl[:, :].rearrange("p (k t) -> p k t", k=MT_F)).then_inc(s_dbg.h, 16)
                            e.dma_start(out=dbg["rows"], in_=rows[0:1, :]).then_inc(s_dbg.h, 16)
                            e.dma_start(out=dbg["rdup"], in_=rdup[:, :]).then_inc(s_dbg.h, 16)
                        o["gpsimd"].append(dqd)
                        s_dbg.n += 64

                    ev2_vals = []
                    st_vals = [0, 0]
                    for m in range(MT_D):
                        wait_ev = ev2_vals[m - 2] if m >= 2 else 0
                        for q in range(4):
                            view, wsem, v_w = wpanel(o, g_wpj, m, 16 * q, 16, BF16,
                                                     wait_cc=(cc_done.get("wpj") if (tag == "f1" and ch == 0 and m == 0 and q == 0) else None))

                            def mm2q(e, m=m, q=q, view=view, wsem=wsem, v_w=v_w,
                                     wait_ev=wait_ev, v_hq=v_hq):
                                e.wait_ge(wsem.h, v_w)
                                if q == 0:
                                    e.wait_ge(s_dv.h, v_hq)
                                    if wait_ev:
                                        e.wait_ge(s_dv.h, wait_ev)
                                pm = p_mm[m % 2]
                                for k in range(16):
                                    ii = e.matmul(pm[:, :], view[:, k * 128:(k + 1) * 128],
                                                  hsl[:, (16 * q + k) * FC:(16 * q + k + 1) * FC],
                                                  start=(q == 0 and k == 0), stop=(q == 3 and k == 15))
                                ii.then_inc(s_pe.h, 1)
                            o["tensor"].append(mm2q)
                            s_pe.n += 1
                            wr_mark(1, s_pe.n)
                        v_mm2 = s_pe.n
                        wait_st = st_vals[m % 2]

                        def ev_p(e, m=m, v_mm2=v_mm2, v_had=v_had, wait_st=wait_st):
                            e.wait_ge(s_pe.h, v_mm2)
                            e.wait_ge(s_dv.h, v_had)
                            if wait_st:
                                e.wait_ge(s_st[m % 2].h, wait_st)
                            sl = evs[:, (m % 2) * FC:(m % 2 + 1) * FC]
                            ii = e.tensor_mul(out=sl, in0=p_mm[m % 2][:, :], in1=rdup[:, 2 * FC:3 * FC])
                            edge_dv(e, ii)
                            e.tensor_add(out=sl, in0=sl, in1=xsrc[:, m * FC:(m + 1) * FC]).then_inc(s_dv.h, 1)
                        o["vector"].append(ev_p)
                        s_dv.n += 1
                        ev2_vals.append(s_dv.n)
                        v_evp = s_dv.n

                        def stx(e, m=m, t0c=t0c, v_evp=v_evp):
                            e.wait_ge(s_dv.h, v_evp)
                            e.dma_start(out=dst_hbm[128 * m:128 * (m + 1), t0c:t0c + FC],
                                        in_=evs[:, (m % 2) * FC:(m % 2 + 1) * FC]).then_inc(s_st[m % 2].h, 16)
                        o["gpsimd"].append(stx)
                        s_st[m % 2].n += 16
                        st_vals[m % 2] = s_st[m % 2].n

                if tag == "f1" and "msc" in dbg:
                    v_last = s_dv.n

                    def dmp(e, v_last=v_last):
                        e.wait_ge(s_dv.h, v_last)
                        e.dma_start(out=dbg["msc"][0:1, :], in_=rows[0:1, :]).then_inc(s_out.h, 16)
                        e.dma_start(out=dbg["msc"][1:2, 0:2], in_=gsc[0:1, 0:2]).then_inc(s_out.h, 16)
                        e.dma_start(out=dbg["msc"][1:2, 8:10], in_=gamv[0:1, 0:2]).then_inc(s_out.h, 16)
                        e.dma_start(out=dbg["msc"][1:2, 10:12], in_=gpart_keep[0:1, 0:2]).then_inc(s_out.h, 16)
                    o["gpsimd"].append(dmp)
                    s_out.n += 64
                run_block(o)

        if _NPHASE >= 1:
            emit_ffn(x0, x1me, "f1")
            o = new_ops()
            vst0, vst1 = s_st[0].n, s_st[1].n

            def ag_x1(e):
                e.wait_ge(s_st[0].h, vst0)
                e.wait_ge(s_st[1].h, vst1)
                e.collective_compute("AllGather", ALU.bypass,
                                     replica_groups=[[2 * b, 2 * b + 1] for b in range(4)],
                                     ins=[x1me.opt()], outs=[x1g.opt()]).then_inc(s_ccl["x1"].h, 1)

            o["gpsimd"].append(ag_x1)
            s_ccl["x1"].n += 1
            cc_done["x1"] = (s_ccl["x1"], 1)

            if "x1g" in dbg:
                def dump_x1(e):
                    e.wait_ge(s_ccl["x1"].h, 1)
                    e.dma_start(out=dbg["x1g"], in_=x1g.rearrange("a d t -> (a d) t")).then_inc(s_out.h, 16)
                o["gpsimd"].append(dump_x1)
                s_out.n += 16
            run_block(o)

        def emit_mamba():
            with ExitStack() as stm:
                o = new_ops()

                def mgate(e, vo=s_out.n, v0=s_st[0].n, v1=s_st[1].n):
                    if vo:
                        e.wait_ge(s_out.h, vo)
                    if v0:
                        e.wait_ge(s_st[0].h, v0)
                    if v1:
                        e.wait_ge(s_st[1].h, v1)
                for _eng in ("sync", "tensor", "vector", "scalar"):
                    o[_eng].append(mgate)
                x1c = stm.enter_context(sb("m_x1c", [128, KT_D * TC], F32))
                sqm = stm.enter_context(sb("m_sqm", [128, 4 * TC], F32R))
                rows = stm.enter_context(sb("m_rows", [1, 4 * TC], F32))
                rdm = stm.enter_context(sb("m_rdm", [128, 2 * TC], F32))
                xs2 = [stm.enter_context(sb(f"m_xs{j}", [64, TC + 3], F32R)) for j in range(16)]
                pconv = stm.enter_context(sb("m_pconv", [64, KC * 16 * 128], F32R))
                bsrc = stm.enter_context(sb("m_bsrc", [128, MT_B * TC], BF16))
                csrc = stm.enter_context(sb("m_csrc", [128, MT_C * TC], BF16))
                rsl = stm.enter_context(sb("m_rsl", [128, TC], F32R))
                spn = stm.enter_context(sb("m_spn", [128, CTI * TC], F32R))
                wsl = stm.enter_context(sb("m_wsl", [128, 2 * TC], BF16))
                rg1 = stm.enter_context(sb("m_rg1", [128, 2 * TC], F32))
                rg2 = stm.enter_context(sb("m_rg2", [128, 2 * TC], F32))
                rg3 = stm.enter_context(sb("m_rg3", [128, 2 * TC], F32))
                xar = stm.enter_context(sb("m_xar", [128, 2 * TC], F32))
                dxe = stm.enter_context(sb("m_dxe", [128, 2 * TC], F32))
                Esl = stm.enter_context(sb("m_E", [128, 2 * TC], F32))
                wdr = stm.enter_context(sb("m_wd", [128, 2 * TC], F32))
                usl = stm.enter_context(sb("m_u", [128, 2 * TC], F32))
                Ssl = [stm.enter_context(sb(f"m_S{i}", [128, 8 * TC], F32)) for i in range(2)]
                SCr = stm.enter_context(sb("m_SC", [128, 2 * TC], F32R))
                carry = stm.enter_context(sb("m_carry", [128, 256], F32))
                yer = stm.enter_context(sb("m_ye", [128, 2 * TC], F32))
                masc = stm.enter_context(sb("m_asc", [128, 256], F32))
                p_sp = stm.enter_context(sb("m_psp", [128, 8 * 128], F32R))
                p_w = stm.enter_context(sb("m_pw", [128, 8 * 128], BF16))
                p_b = stm.enter_context(sb("m_pb", [128, 16 * 128], BF16))
                p_r = stm.enter_context(sb("m_pr", [128, 8 * 128], F32R))
                pb_ = [stm.enter_context(psal(f"m_pb{i}", [128, TC], F32)) for i in range(8)]

                def mld(e):
                    e.dma_start(out=masc[:, :], in_=alog_i).then_inc(s_cst.h, 16)
                    e.dma_start(out=pconv[:, :], in_=pconv_i).then_inc(s_cst.h, 16)
                    e.dma_start(out=p_sp[:, :], in_=p_sp_i).then_inc(s_cst.h, 16)
                    e.dma_start(out=p_r[:, :], in_=p_r_i).then_inc(s_cst.h, 16)
                    e.dma_start(out=p_w[:, :], in_=p_w_i).then_inc(s_cst.h, 16)
                    e.dma_start(out=p_b[:, :], in_=p_b_i).then_inc(s_cst.h, 16)
                o["sync"].append(mld)
                s_cst.n += 96
                v_mld = s_cst.n

                def mcst(e):
                    e.wait_ge(s_cst.h, v_mld)
                    ii = e.activation(out=masc[:, :], in_=masc[:, :], func=AF.Exp)
                    edge_ac(e, ii)
                    e.activation(out=masc[:, :], in_=masc[:, :], func=AF.Copy, scale=-1.0).then_inc(s_ac.h, 1)
                o["scalar"].append(mcst)
                s_ac.n += 1
                v_masc = s_ac.n

                def mpat(e):
                    e.wait_ge(s_cst.h, v_mld)
                    e.memset(carry[:, :], 0.0).then_inc(s_dv.h, 1)
                o["vector"].append(mpat)
                s_dv.n += 1
                v_mpat = s_dv.n

                carry_vals = [0] * 32
                silu_vals = [0] * 32
                wtt_vals = [0] * 32
                mmw7 = [0] * 32
                v_convmm_prev = 0
                v_ipmm_prev = 0
                v_scmm_prev = 0
                v_yev_prev = 0
                v_lnc_prev = 0

                for ch in range(NCH):
                    half, tc0 = ch // 4, (ch % 4) * TC

                    v_cm = v_convmm_prev

                    def halo(e, ch=ch, v_cm=v_cm):
                        if v_cm:
                            e.wait_ge(s_pe.h, v_cm)
                        if ch == 0:
                            for j in range(16):
                                ii = e.memset(xs2[j][:, 0:3].bitcast(F32), 0.0)
                        else:
                            for j in range(16):
                                ii = e.tensor_copy(out=xs2[j][:, 0:3], in_=xs2[j][:, TC:TC + 3])
                        ii.then_inc(s_dv.h, 1)
                    o["vector"].append(halo)
                    s_dv.n += 1
                    v_halo = s_dv.n

                    v_lp = v_lnc_prev
                    v_ip = v_ipmm_prev

                    ch_first_cc = cc_done.get("x1") if ch == 0 else None

                    def ldx1(e, half=half, tc0=tc0, v_lp=v_lp, v_ip=v_ip, cfc=ch_first_cc):
                        if v_lp:
                            e.wait_ge(s_dv.h, v_lp)
                        if v_ip:
                            e.wait_ge(s_pe.h, v_ip)
                        if cfc is not None:
                            e.wait_ge(cfc[0].h, cfc[1])
                        e.dma_start(out=x1c[:, :].rearrange("p (k t) -> p k t", k=KT_D),
                                    in_=x1g[half][:, tc0:tc0 + TC].rearrange("(k p) t -> p k t", p=128)
                                    ).then_inc(s_x.h, 16)
                    o["sync"].append(ldx1)
                    s_x.n += 16
                    v_lx1 = s_x.n

                    vstat_mm = 0
                    for g4 in range(4):
                        v_pe_prev = s_pe.n

                        def msq(e, g4=g4, v_pe_prev=v_pe_prev, v_lx1=v_lx1):
                            e.wait_ge(s_x.h, v_lx1)
                            if g4 > 0:
                                e.wait_ge(s_pe.h, v_pe_prev)
                            for i in range(4):
                                k = 4 * g4 + i
                                ii = e.activation(out=sqm[:, i * TC:(i + 1) * TC],
                                                  in_=x1c[:, k * TC:(k + 1) * TC], func=AF.Square)
                            ii.then_inc(s_ac.h, 1)
                        o["scalar"].append(msq)
                        s_ac.n += 1
                        v_msq = s_ac.n
                        v_rfree = v_lnc_prev

                        def mmst(e, g4=g4, v_msq=v_msq, v_lx1=v_lx1, mf=mmst_free[0], af=scanac_free[0]):
                            e.wait_ge(s_x.h, v_lx1)
                            e.wait_ge(s_ac.h, v_msq)
                            if g4 == 0 and mf:
                                e.wait_ge(s_dv.h, mf)
                            if g4 == 0 and af:
                                e.wait_ge(s_ac.h, af)
                            for i in range(4):
                                k = 4 * g4 + i
                                e.matmul(pb_[2][0:1, :], ones_f[:, 0:1], x1c[:, k * TC:(k + 1) * TC],
                                         start=(k == 0), stop=(k == KT_D - 1), skip_group_check=True)
                                ii = e.matmul(pb_[3][0:1, :], ones_r[:, 0:1], sqm[:, i * TC:(i + 1) * TC],
                                              start=(k == 0), stop=(k == KT_D - 1), skip_group_check=True)
                            ii.then_inc(s_pe.h, 1)
                        o["tensor"].append(mmst)
                        s_pe.n += 1
                    v_mst = s_pe.n

                    def mrows(e, v_mst=v_mst):
                        e.wait_ge(s_pe.h, v_mst)
                        ii = e.tensor_scalar(out=rows[0:1, 0:TC], in0=pb_[2][0:1, :], scalar1=1.0 / D,
                                             scalar2=None, op0=ALU.mult)
                        e.tensor_scalar(out=rows[0:1, TC:2 * TC], in0=pb_[3][0:1, :], scalar1=1.0 / D,
                                        scalar2=None, op0=ALU.mult)
                        edge_dv(e, ii)
                        ii = e.tensor_mul(out=rows[0:1, 2 * TC:3 * TC], in0=rows[0:1, 0:TC], in1=rows[0:1, 0:TC])
                        edge_dv(e, ii)
                        e.tensor_sub(out=rows[0:1, TC:2 * TC], in0=rows[0:1, TC:2 * TC],
                                     in1=rows[0:1, 2 * TC:3 * TC]).then_inc(s_dv.h, 1)
                    o["vector"].append(mrows)
                    s_dv.n += 1
                    v_mr1 = s_dv.n
                    mmst_free[0] = v_mr1

                    def mrow2(e, v_mr1=v_mr1):
                        e.wait_ge(s_dv.h, v_mr1)
                        ii = e.activation(out=rows[0:1, 2 * TC:3 * TC], in_=rows[0:1, TC:2 * TC], func=AF.Ln,
                                          bias=cst[0:1, 0:1])
                        edge_ac(e, ii)
                        e.activation(out=rows[0:1, 3 * TC:4 * TC], in_=rows[0:1, 2 * TC:3 * TC],
                                     func=AF.Exp, scale=-0.5).then_inc(s_ac.h, 1)
                    o["scalar"].append(mrow2)
                    s_ac.n += 1
                    v_mg = s_ac.n

                    v_mdupmm = s_pe.n

                    def mdup(e, v_mr1=v_mr1, v_mg=v_mg, mdw=mdup_free[0], af=scanac_free[0]):
                        e.wait_ge(s_dv.h, v_mr1)
                        if mdw:
                            e.wait_ge(s_dv.h, mdw)
                        if af:
                            e.wait_ge(s_ac.h, af)
                        e.matmul(pb_[4][:, :], ones_f[0:1, :], rows[0:1, 0:TC], start=True, stop=True).then_inc(s_pe.h, 1)
                    o["tensor"].append(mdup)
                    s_pe.n += 1
                    v_md1 = s_pe.n

                    def mcp1(e, v_md1=v_md1):
                        e.wait_ge(s_pe.h, v_md1)
                        e.tensor_copy(out=rdm[:, 0:TC], in_=pb_[4][:, :]).then_inc(s_dv.h, 1)
                    o["vector"].append(mcp1)
                    s_dv.n += 1
                    v_mc1 = s_dv.n

                    def mdup2(e, v_mg=v_mg, v_mc1=v_mc1):
                        e.wait_ge(s_ac.h, v_mg)
                        e.wait_ge(s_dv.h, v_mc1)
                        e.matmul(pb_[4][:, :], ones_f[0:1, :], rows[0:1, 3 * TC:4 * TC], start=True, stop=True).then_inc(s_pe.h, 1)
                    o["tensor"].append(mdup2)
                    s_pe.n += 1
                    v_md2 = s_pe.n

                    def mcp2(e, v_md2=v_md2):
                        e.wait_ge(s_pe.h, v_md2)
                        e.tensor_copy(out=rdm[:, TC:2 * TC], in_=pb_[4][:, :]).then_inc(s_dv.h, 1)
                    o["vector"].append(mcp2)
                    s_dv.n += 1
                    v_mc2 = s_dv.n
                    mdup_free[0] = v_mc2

                    lnc = x1c[:, :].bitcast(F32R)

                    def mln(e, v_mc2=v_mc2):
                        e.wait_ge(s_dv.h, v_mc2)
                        for k in range(KT_D):
                            ii = e.tensor_sub(out=lnc[:, k * TC:(k + 1) * TC], in0=x1c[:, k * TC:(k + 1) * TC],
                                              in1=rdm[:, 0:TC])
                        edge_dv(e, ii)
                        for k in range(KT_D):
                            ii = e.tensor_mul(out=lnc[:, k * TC:(k + 1) * TC],
                                              in0=lnc[:, k * TC:(k + 1) * TC], in1=rdm[:, TC:2 * TC])
                        ii.then_inc(s_dv.h, 1)
                    o["vector"].append(mln)
                    s_dv.n += 1
                    v_lnc = s_dv.n
                    v_lnc_prev = v_lnc

                    v_sc_prev = v_scmm_prev
                    v_scdv_prev = scandv_free[0]
                    evac_ac = {}
                    for m in range(28):
                        view, wsem, v_w = wpanel(o, g_wm, m, 0, 16, F32R,
                                                 wait_cc=(cc_done.get("wm") if (ch == 0 and m == 0) else None))
                        wait_ev = evac_ac.get(m - 2, 0)

                        def ipmm(e, view=view, wsem=wsem, v_w=v_w, m=m, wait_ev=wait_ev,
                                 v_lnc=v_lnc, v_sc_prev=v_sc_prev, v_scdv_prev=v_scdv_prev):
                            e.wait_ge(wsem.h, v_w)
                            e.wait_ge(s_dv.h, v_lnc)
                            if m < 2 and v_sc_prev:
                                e.wait_ge(s_pe.h, v_sc_prev)
                            if m < 2 and v_scdv_prev:
                                e.wait_ge(s_dv.h, v_scdv_prev)
                            if wait_ev:
                                e.wait_ge(s_ac.h, wait_ev)
                            pm = pb_[m % 2]
                            for k in range(KT_D):
                                ii = e.matmul(pm[:, :], view[:, k * 128:(k + 1) * 128],
                                              lnc[:, k * TC:(k + 1) * TC], start=(k == 0), stop=(k == KT_D - 1))
                            ii.then_inc(s_pe.h, 1)
                        o["tensor"].append(ipmm)
                        s_pe.n += 1
                        wr_mark(1, s_pe.n)
                        v_ipm = s_pe.n

                        if m < 8:
                            def evx(e, m=m, v_ipm=v_ipm, v_halo=v_halo):
                                e.wait_ge(s_pe.h, v_ipm)
                                e.wait_ge(s_dv.h, v_halo)
                                e.activation(out=xs2[2 * m][:, 3:3 + TC], in_=pb_[m % 2][0:64, :], func=AF.Copy)
                                e.activation(out=xs2[2 * m + 1][:, 3:3 + TC], in_=pb_[m % 2][64:128, :],
                                             func=AF.Copy).then_inc(s_ac.h, 1)
                            o["scalar"].append(evx)
                        elif m < 12:
                            def evx(e, m=m, v_ipm=v_ipm, v_sc_prev=v_sc_prev):
                                e.wait_ge(s_pe.h, v_ipm)
                                if v_sc_prev:
                                    e.wait_ge(s_pe.h, v_sc_prev)
                                e.activation(out=bsrc[:, (m - 8) * TC:(m - 7) * TC], in_=pb_[m % 2][:, :],
                                             func=AF.Copy).then_inc(s_ac.h, 1)
                            o["scalar"].append(evx)
                        else:
                            def evx(e, m=m, v_ipm=v_ipm, v_sc_prev=v_sc_prev):
                                e.wait_ge(s_pe.h, v_ipm)
                                if v_sc_prev:
                                    e.wait_ge(s_pe.h, v_sc_prev)
                                e.activation(out=csrc[:, (m - 12) * TC:(m - 11) * TC], in_=pb_[m % 2][:, :],
                                             func=AF.Copy).then_inc(s_ac.h, 1)
                            o["scalar"].append(evx)
                        s_ac.n += 1
                        evac_ac[m] = s_ac.n
                    v_bc = s_ac.n

                    view, wsem, v_w = wpanel(o, g_wdtin, 0, 0, 16, F32R)
                    wait_ev = evac_ac.get(26, 0)

                    def dt1mm(e, view=view, wsem=wsem, v_w=v_w, wait_ev=wait_ev, v_lnc=v_lnc):
                        e.wait_ge(wsem.h, v_w)
                        e.wait_ge(s_dv.h, v_lnc)
                        e.wait_ge(s_ac.h, wait_ev)
                        for k in range(KT_D):
                            ii = e.matmul(pb_[0][:, :], view[:, k * 128:(k + 1) * 128],
                                          lnc[:, k * TC:(k + 1) * TC], start=(k == 0), stop=(k == KT_D - 1))
                        ii.then_inc(s_pe.h, 1)
                    o["tensor"].append(dt1mm)
                    s_pe.n += 1
                    wr_mark(1, s_pe.n)
                    v_dt1 = s_pe.n

                    def dt1ev(e, v_dt1=v_dt1):
                        e.wait_ge(s_pe.h, v_dt1)
                        e.activation(out=rsl[:, :], in_=pb_[0][:, :], func=AF.Copy).then_inc(s_ac.h, 1)
                    o["scalar"].append(dt1ev)
                    s_ac.n += 1
                    v_rsl = s_ac.n

                    sp_vals = [0] * 32
                    relu_vals = [0] * 32
                    for ct in range(CTI):
                        view, wsem, v_w = wpanel(o, g_wdtp, ct, 0, 1, F32R)
                        wait_rd = relu_vals[ct - 2] if ct >= 2 else 0

                        def dt2mm(e, view=view, wsem=wsem, v_w=v_w, ct=ct, v_rsl=v_rsl, wait_rd=wait_rd):
                            e.wait_ge(wsem.h, v_w)
                            e.wait_ge(s_ac.h, max(v_rsl, wait_rd))
                            e.matmul(pb_[ct % 2][:, :], view[:, 0:128], rsl[:, :],
                                     start=True, stop=True).then_inc(s_pe.h, 1)
                        o["tensor"].append(dt2mm)
                        s_pe.n += 1
                        wr_mark(1, s_pe.n)
                        v_d2 = s_pe.n

                        sp_wdv = sp_vals[ct - 2] if ct >= 2 else 0
                        sp_wdbg = s_dbg.n if ch == 1 else 0

                        def spchain(e, ct=ct, v_d2=v_d2, wdv=sp_wdv):
                            e.wait_ge(s_pe.h, v_d2)
                            if wdv:
                                e.wait_ge(s_dv.h, wdv)
                            sl = ct % 2
                            ii = e.activation(out=rg1[:, sl * TC:(sl + 1) * TC], in_=pb_[ct % 2][:, :],
                                              func=AF.Abs, bias=dtbp[:, ct:ct + 1])
                            edge_ac(e, ii)
                            ii = e.activation(out=rg2[:, sl * TC:(sl + 1) * TC],
                                              in_=rg1[:, sl * TC:(sl + 1) * TC], func=AF.Exp, scale=-1.0)
                            edge_ac(e, ii)
                            e.activation(out=rg2[:, sl * TC:(sl + 1) * TC],
                                         in_=rg2[:, sl * TC:(sl + 1) * TC], func=AF.Ln, bias=cstp[:, 2:3])
                            e.activation(out=rg3[:, sl * TC:(sl + 1) * TC], in_=pb_[ct % 2][:, :],
                                         func=AF.Relu, bias=dtbp[:, ct:ct + 1]).then_inc(s_ac.h, 1)
                        o["scalar"].append(spchain)
                        s_ac.n += 1
                        relu_vals[ct] = s_ac.n
                        v_spc = s_ac.n

                        def spadd(e, ct=ct, v_spc=v_spc, wdbg=sp_wdbg):
                            e.wait_ge(s_ac.h, v_spc)
                            if wdbg:
                                e.wait_ge(s_dbg.h, wdbg)
                            sl = ct % 2
                            e.tensor_add(out=spn[:, ct * TC:(ct + 1) * TC], in0=rg2[:, sl * TC:(sl + 1) * TC],
                                         in1=rg3[:, sl * TC:(sl + 1) * TC]).then_inc(s_dv.h, 1)
                        o["vector"].append(spadd)
                        s_dv.n += 1
                        sp_vals[ct] = s_dv.n

                    v_dt2ac = s_ac.n

                    def emit_cv(g, ch=ch):
                        if g < 2:
                            wbank = ("ac", v_dt2ac)
                        elif g % 2 == 0:
                            wbank = ("dv", sc_vals[8 * (g - 2) + 7])
                        else:
                            wbank = ("dv", yev_vals[g - 2])

                        def cvmm(e, g=g, wbank=wbank, v_bc=v_bc):
                            e.wait_ge(s_ac.h, v_bc)
                            if wbank[0] == "ac":
                                e.wait_ge(s_ac.h, wbank[1])
                            else:
                                e.wait_ge(s_dv.h, wbank[1])
                            b0 = 32 * (g % 2)
                            for k in range(KC):
                                gg = k * 16 + g // 2
                                ii = e.matmul(pb_[g % 2][:, :],
                                              pconv[b0:b0 + 32, gg * 128:(gg + 1) * 128],
                                              xs2[g // 2][b0:b0 + 32, k:k + TC],
                                              start=(k == 0), stop=(k == KC - 1))
                            ii.then_inc(s_pe.h, 1)
                        o["tensor"].append(cvmm)
                        s_pe.n += 1
                        v_cv = s_pe.n

                        def silu(e, g=g, v_cv=v_cv, wdv=wtt_vals[g - 2]):
                            e.wait_ge(s_pe.h, v_cv)
                            if wdv:
                                e.wait_ge(s_dv.h, wdv)
                            e.activation(out=xar[:, (g % 2) * TC:(g % 2 + 1) * TC], in_=pb_[g % 2][:, :],
                                         func=AF.Silu, bias=cb_t[:, g:g + 1]).then_inc(s_ac.h, 1)
                        o["scalar"].append(silu)
                        s_ac.n += 1
                        silu_vals[g] = s_ac.n
                        v_si = s_ac.n

                        def wtt(e, g=g, hf=ch // 4, v_si=v_si, w_sp=sp_vals[g], wmw=mmw7[g - 2]):
                            e.wait_ge(s_ac.h, v_si)
                            e.wait_ge(s_dv.h, w_sp)
                            if wmw:
                                e.wait_ge(s_pe.h, wmw)
                            ii = e.tensor_mul(out=wsl[:, (g % 2) * TC:(g % 2 + 1) * TC],
                                              in0=spn[:, g * TC:(g + 1) * TC].bitcast(F32),
                                              in1=xar[:, (g % 2) * TC:(g % 2 + 1) * TC])
                            edge_dv(e, ii)
                            e.tensor_scalar(out=dxe[:, (g % 2) * TC:(g % 2 + 1) * TC],
                                            in0=xar[:, (g % 2) * TC:(g % 2 + 1) * TC],
                                            scalar1=dv_t[:, 32 * hf + g:32 * hf + g + 1],
                                            scalar2=None, op0=ALU.mult).then_inc(s_dv.h, 1)
                        o["vector"].append(wtt)
                        s_dv.n += 1
                        wtt_vals[g] = s_dv.n

                    emit_cv(0)
                    emit_cv(1)

                    E_vals = [0] * 256
                    wd_vals = [0] * 256
                    u_vals = [0] * 256
                    scan_vals = [0] * 256
                    sc_vals = [0] * 256
                    yev_vals = [0] * 32
                    new_carry = [0] * 32
                    for q in range(256):
                        g, qq = q // 8, q % 8
                        if qq == 0 and g > 0:
                            pass
                        w_free = E_vals[q - 2] if q >= 2 else 0

                        def mmS(e, q=q, g=g, w_free=w_free, w_sp=sp_vals[g], v_mpat=v_mpat):
                            e.wait_ge(s_dv.h, max(w_sp, v_mpat))
                            if w_free:
                                e.wait_ge(s_ac.h, w_free)
                            e.matmul(pb_[2 + q % 2][:, :], p_sp[:, (q % 8) * 128:(q % 8 + 1) * 128],
                                     spn[:, g * TC:(g + 1) * TC], start=True, stop=True).then_inc(s_pe.h, 1)
                        o["tensor"].append(mmS)
                        s_pe.n += 1
                        v_mmS = s_pe.n

                        def mE(e, q=q, v_mmS=v_mmS, v_masc=v_masc, wsc=scan_vals[q - 2] if q >= 2 else 0):
                            e.wait_ge(s_pe.h, v_mmS)
                            e.wait_ge(s_ac.h, v_masc)
                            if wsc:
                                e.wait_ge(s_dv.h, wsc)
                            e.activation(out=Esl[:, (q % 2) * TC:(q % 2 + 1) * TC], in_=pb_[2 + q % 2][:, :],
                                         func=AF.Exp, scale=masc[:, q:q + 1]).then_inc(s_ac.h, 1)
                        o["scalar"].append(mE)
                        s_ac.n += 1
                        E_vals[q] = s_ac.n

                        w_free = wd_vals[q - 2] if q >= 2 else 0

                        def mmW(e, q=q, g=g, w_free=w_free, w_w=wtt_vals[g]):
                            e.wait_ge(s_dv.h, w_w)
                            if w_free:
                                e.wait_ge(s_ac.h, w_free)
                            e.matmul(pb_[4 + q % 2][:, :], p_w[:, (q % 8) * 128:(q % 8 + 1) * 128],
                                     wsl[:, (g % 2) * TC:(g % 2 + 1) * TC],
                                     start=True, stop=True).then_inc(s_pe.h, 1)
                        o["tensor"].append(mmW)
                        s_pe.n += 1
                        v_mmW = s_pe.n
                        if qq == 7:
                            mmw7[g] = s_pe.n

                        def mwd(e, q=q, v_mmW=v_mmW, wu=u_vals[q - 2] if q >= 2 else 0):
                            e.wait_ge(s_pe.h, v_mmW)
                            if wu:
                                e.wait_ge(s_dv.h, wu)
                            e.activation(out=wdr[:, (q % 2) * TC:(q % 2 + 1) * TC], in_=pb_[4 + q % 2][:, :],
                                         func=AF.Copy).then_inc(s_ac.h, 1)
                        o["scalar"].append(mwd)
                        s_ac.n += 1
                        wd_vals[q] = s_ac.n

                        w_free = u_vals[q - 2] if q >= 2 else 0

                        def mmB(e, q=q, w_free=w_free, v_bc=v_bc, v_mpat=v_mpat):
                            e.wait_ge(s_ac.h, v_bc)
                            e.wait_ge(s_dv.h, v_mpat)
                            if w_free:
                                e.wait_ge(s_dv.h, w_free)
                            e.matmul(pb_[6 + q % 2][:, :], p_b[:, ((q // 4) % 16) * 128:((q // 4) % 16 + 1) * 128],
                                     bsrc[:, (q // 64) * TC:(q // 64 + 1) * TC], start=True, stop=True).then_inc(s_pe.h, 1)
                        o["tensor"].append(mmB)
                        s_pe.n += 1
                        v_mmB = s_pe.n

                        def mu(e, q=q, v_mmB=v_mmB, wwd=wd_vals[q]):
                            e.wait_ge(s_pe.h, v_mmB)
                            e.wait_ge(s_ac.h, wwd)
                            e.tensor_mul(out=usl[:, (q % 2) * TC:(q % 2 + 1) * TC], in0=pb_[6 + q % 2][:, :],
                                         in1=wdr[:, (q % 2) * TC:(q % 2 + 1) * TC]).then_inc(s_dv.h, 1)
                        o["vector"].append(mu)
                        s_dv.n += 1
                        u_vals[q] = s_dv.n

                        def mscan(e, q=q, g=g, qq=qq, ch=ch, wu=u_vals[q], wE=E_vals[q],
                                  wca=carry_vals[g], wsl2=new_carry[g - 2] if g >= 2 else 0,
                                  wsc2=sc_vals[q - 16] if q >= 16 else 0):
                            e.wait_ge(s_dv.h, wu)
                            e.wait_ge(s_ac.h, wE)
                            if wca:
                                e.wait_ge(s_dv.h, wca)
                            if wsl2:
                                e.wait_ge(s_dv.h, wsl2)
                            init = 0.0 if ch == 0 else carry[:, q:q + 1]
                            e.tensor_tensor_scan(out=Ssl[g % 2][:, qq * TC:(qq + 1) * TC],
                                                 data0=Esl[:, (q % 2) * TC:(q % 2 + 1) * TC],
                                                 data1=usl[:, (q % 2) * TC:(q % 2 + 1) * TC],
                                                 initial=init, op0=ALU.mult, op1=ALU.add).then_inc(s_dv.h, 1)
                        o["vector"].append(mscan)
                        s_dv.n += 1
                        scan_vals[q] = s_dv.n

                        w_free = sc_vals[q - 1] if q >= 1 else 0
                        if qq == 0 and g == 0:
                            w_cvac = silu_vals[0]
                        elif qq == 0 and g % 2 == 1 and g <= 30:
                            w_cvac = silu_vals[g + 1]
                        else:
                            w_cvac = 0

                        def mmC(e, q=q, w_free=w_free, v_bc=v_bc, v_mpat=v_mpat, w_cvac=w_cvac):
                            e.wait_ge(s_ac.h, v_bc)
                            e.wait_ge(s_dv.h, v_mpat)
                            if w_cvac:
                                e.wait_ge(s_ac.h, w_cvac)
                            if w_free:
                                e.wait_ge(s_dv.h, w_free)
                            e.matmul(pb_[0][:, :], p_b[:, (q % 16) * 128:(q % 16 + 1) * 128],
                                     csrc[:, (q // 16) * TC:(q // 16 + 1) * TC], start=True, stop=True).then_inc(s_pe.h, 1)
                        o["tensor"].append(mmC)
                        s_pe.n += 1
                        v_mmC = s_pe.n

                        def msc(e, q=q, g=g, qq=qq, v_mmC=v_mmC, wscan=scan_vals[q]):
                            e.wait_ge(s_pe.h, v_mmC)
                            e.wait_ge(s_dv.h, wscan)
                            e.tensor_mul(out=SCr[:, (q % 2) * TC:(q % 2 + 1) * TC],
                                         in0=pb_[0][:, :], in1=Ssl[g % 2][:, qq * TC:(qq + 1) * TC]
                                         ).then_inc(s_dv.h, 1)
                        o["vector"].append(msc)
                        s_dv.n += 1
                        sc_vals[q] = s_dv.n

                        def mmY(e, q=q, qq=qq, wsc=sc_vals[q],
                                wyev=yev_vals[g - 1] if (qq == 0 and g >= 1) else 0,
                                w_cvac=silu_vals[g + 1] if (qq == 0 and g % 2 == 0 and g <= 30) else 0):
                            e.wait_ge(s_dv.h, wsc)
                            if wyev:
                                e.wait_ge(s_dv.h, wyev)
                            if w_cvac:
                                e.wait_ge(s_ac.h, w_cvac)
                            e.matmul(pb_[1][:, :], p_r[:, (q % 8) * 128:(q % 8 + 1) * 128],
                                     SCr[:, (q % 2) * TC:(q % 2 + 1) * TC],
                                     start=(qq == 0), stop=(qq == 7)).then_inc(s_pe.h, 1)
                        o["tensor"].append(mmY)
                        s_pe.n += 1
                        v_mmY = s_pe.n

                        if qq == 7:
                            def ycar(e, g=g, ch=ch, v_mmY=v_mmY, wye=yest[g % 2]):
                                e.wait_ge(s_pe.h, v_mmY)
                                if wye:
                                    e.wait_ge(s_ys[g % 2].h, wye)
                                ii = e.tensor_add(out=yer[:, (g % 2) * TC:(g % 2 + 1) * TC],
                                                  in0=pb_[1][:, :], in1=dxe[:, (g % 2) * TC:(g % 2 + 1) * TC])
                                if ch < NCH - 1:
                                    ii = e.tensor_copy(out=carry[:, 8 * g:8 * g + 8],
                                                       in_=Ssl[g % 2][:, :].rearrange("p (a b) -> p a b", a=8)[:, :, TC - 1:TC].squeeze(2))
                                ii.then_inc(s_dv.h, 1)
                            o["vector"].append(ycar)
                            s_dv.n += 1
                            yev_vals[g] = s_dv.n
                            new_carry[g] = s_dv.n

                            def yst(e, g=g, ch=ch, half=half, tc0=tc0, wy=yev_vals[g]):
                                e.wait_ge(s_dv.h, wy)
                                e.dma_start(out=ybuf[half][128 * g:128 * (g + 1), tc0:tc0 + TC],
                                            in_=yer[:, (g % 2) * TC:(g % 2 + 1) * TC]).then_inc(s_ys[g % 2].h, 16)
                            o["gpsimd"].append(yst)
                            s_ys[g % 2].n += 16
                            yest[g % 2] = s_ys[g % 2].n

                            if g < 30:
                                emit_cv(g + 2)

                    carry_vals = list(new_carry)
                    v_scmm_prev = s_pe.n
                    v_convmm_prev = s_pe.n
                    v_ipmm_prev = s_pe.n
                    v_yev_prev = yev_vals[31]
                    scandv_free[0] = s_dv.n
                    scanac_free[0] = s_ac.n

                    if ch == 0 and "spn" in dbg:
                        v_dmp = s_dv.n

                        def dmps(e, v_dmp=v_dmp):
                            e.wait_ge(s_dv.h, v_dmp)
                            e.dma_start(out=dbg["spn"].rearrange("(g p) t -> p g t", p=128),
                                        in_=spn[:, :].bitcast(F32).rearrange("p (g t) -> p g t", g=CTI)
                                        ).then_inc(s_dbg.h, 16)
                        o["gpsimd"].append(dmps)
                        s_dbg.n += 16

                if "ypart" in dbg:
                    def dmpy(e, w0=s_ys[0].n, w1=s_ys[1].n):
                        e.wait_ge(s_ys[0].h, w0)
                        e.wait_ge(s_ys[1].h, w1)
                        e.dma_start(out=dbg["ypart"][:, 0:TH], in_=ybuf[0]).then_inc(s_out.h, 16)
                        e.dma_start(out=dbg["ypart"][:, TH:TF], in_=ybuf[1]).then_inc(s_out.h, 16)
                    o["gpsimd"].append(dmpy)
                    s_out.n += 32

                run_block(o)

        if _NPHASE >= 3:
            mmst_free = [0]
            mdup_free = [0]
            yest = [0, 0]
            scandv_free = [0]
            scanac_free = [0]
            emit_mamba()
            o = new_ops()
            def rs_y(e, w0=s_ys[0].n, w1=s_ys[1].n):
                e.wait_ge(s_ys[0].h, w0)
                e.wait_ge(s_ys[1].h, w1)
                e.collective_compute("ReduceScatter", ALU.add,
                                     replica_groups=[[2 * b, 2 * b + 1] for b in range(4)],
                                     ins=[ybuf.opt()], outs=[ysum_b.opt()]).then_inc(s_ccl["y"].h, 1)

            o["gpsimd"].append(rs_y)
            s_ccl["y"].n += 1
            cc_done["y"] = (s_ccl["y"], 1)
            run_block(o)

        def emit_zgate():
            with ExitStack() as stz:
                o = new_ops()

                def zgate0(e, vo=s_out.n, v0=s_st[0].n, v1=s_st[1].n,
                           y0=s_ys[0].n, y1=s_ys[1].n, vd=s_dbg.n):
                    for sh, vv in ((s_out.h, vo), (s_st[0].h, v0), (s_st[1].h, v1),
                                   (s_ys[0].h, y0), (s_ys[1].h, y1), (s_dbg.h, vd)):
                        if vv:
                            e.wait_ge(sh, vv)
                for _eng in ("sync", "tensor", "vector", "scalar"):
                    o[_eng].append(zgate0)

                zx = stz.enter_context(sb("z_x", [128, KT_D * ZC], F32))
                zln = stz.enter_context(sb("z_ln", [128, KT_D * ZC], F32R))
                zsq = stz.enter_context(sb("z_sq", [128, 4 * ZC], F32R))
                zrows = stz.enter_context(sb("z_rows", [1, 4 * ZC], F32))
                zrdm = stz.enter_context(sb("z_rdm", [128, 2 * ZC], F32))
                zsil = stz.enter_context(sb("z_sil", [128, 2 * ZC], F32))
                ysl = stz.enter_context(sb("z_y", [128, 2 * ZC], F32))
                gsl = stz.enter_context(sb("z_g", [128, CTI * ZC], F32R))
                x2s = stz.enter_context(sb("z_x2", [128, 2 * ZC], F32))
                pz = [stz.enter_context(psal(f"z_pz{i}", [128, ZC], F32)) for i in range(2)]
                po = [stz.enter_context(psal(f"z_po{i}", [128, ZC], F32)) for i in range(2)]
                pzs1 = stz.enter_context(psal("z_ps1", [1, ZC], F32))
                pzs2 = stz.enter_context(psal("z_ps2", [1, ZC], F32))
                pzdu = stz.enter_context(psal("z_pdu", [128, ZC], F32))

                v_zlnrd_prev = 0
                v_x2_prev = 0
                v_gslrd_prev = 0
                v_zr1_prev = 0
                zdu_prev = 0
                evz_sl = [0, 0]
                gm_sl = [0, 0]
                x2_sl = [0, 0]
                st2 = [s_st[0].n, s_st[1].n]

                for oc in range(NZC):
                    oc0 = oc * ZC

                    def ldzx(e, oc0=oc0, wx2=v_x2_prev):
                        if wx2:
                            e.wait_ge(s_dv.h, wx2)
                        e.dma_start(out=zx[:, :].rearrange("p (k t) -> p k t", k=KT_D),
                                    in_=x1me[:, oc0:oc0 + ZC].rearrange("(k p) t -> p k t", p=128)
                                    ).then_inc(s_x.h, 16)
                    o["sync"].append(ldzx)
                    s_x.n += 16
                    v_lzx = s_x.n

                    for g4 in range(4):
                        v_pe_prev = s_pe.n

                        def zsqa(e, g4=g4, v_pe_prev=v_pe_prev, v_lzx=v_lzx):
                            e.wait_ge(s_x.h, v_lzx)
                            if g4 > 0:
                                e.wait_ge(s_pe.h, v_pe_prev)
                            for i in range(4):
                                k = 4 * g4 + i
                                ii = e.activation(out=zsq[:, i * ZC:(i + 1) * ZC],
                                                  in_=zx[:, k * ZC:(k + 1) * ZC], func=AF.Square)
                            ii.then_inc(s_ac.h, 1)
                        o["scalar"].append(zsqa)
                        s_ac.n += 1
                        v_zsq = s_ac.n

                        def zmmst(e, g4=g4, v_zsq=v_zsq, v_lzx=v_lzx,
                                  wrp=(v_zr1_prev if g4 == 0 else 0)):
                            e.wait_ge(s_x.h, v_lzx)
                            e.wait_ge(s_ac.h, v_zsq)
                            if wrp:
                                e.wait_ge(s_dv.h, wrp)
                            for i in range(4):
                                k = 4 * g4 + i
                                e.matmul(pzs1[0:1, :], ones_f[:, 0:1], zx[:, k * ZC:(k + 1) * ZC],
                                         start=(k == 0), stop=(k == KT_D - 1), skip_group_check=True)
                                ii = e.matmul(pzs2[0:1, :], ones_r[:, 0:1], zsq[:, i * ZC:(i + 1) * ZC],
                                              start=(k == 0), stop=(k == KT_D - 1), skip_group_check=True)
                            ii.then_inc(s_pe.h, 1)
                        o["tensor"].append(zmmst)
                        s_pe.n += 1
                    v_zstat = s_pe.n

                    def zrows1(e, v_zstat=v_zstat):
                        e.wait_ge(s_pe.h, v_zstat)
                        ii = e.tensor_scalar(out=zrows[0:1, 0:ZC], in0=pzs1[0:1, :],
                                             scalar1=1.0 / D, scalar2=None, op0=ALU.mult)
                        e.tensor_scalar(out=zrows[0:1, ZC:2 * ZC], in0=pzs2[0:1, :],
                                        scalar1=1.0 / D, scalar2=None, op0=ALU.mult)
                        edge_dv(e, ii)
                        ii = e.tensor_mul(out=zrows[0:1, 2 * ZC:3 * ZC], in0=zrows[0:1, 0:ZC],
                                          in1=zrows[0:1, 0:ZC])
                        edge_dv(e, ii)
                        e.tensor_sub(out=zrows[0:1, ZC:2 * ZC], in0=zrows[0:1, ZC:2 * ZC],
                                     in1=zrows[0:1, 2 * ZC:3 * ZC]).then_inc(s_dv.h, 1)
                    o["vector"].append(zrows1)
                    s_dv.n += 1
                    v_zr1 = s_dv.n
                    v_zr1_prev = v_zr1

                    def zrow2(e, v_zr1=v_zr1):
                        e.wait_ge(s_dv.h, v_zr1)
                        ii = e.activation(out=zrows[0:1, 2 * ZC:3 * ZC], in_=zrows[0:1, ZC:2 * ZC],
                                          func=AF.Ln, bias=cst[0:1, 0:1])
                        edge_ac(e, ii)
                        e.activation(out=zrows[0:1, 3 * ZC:4 * ZC], in_=zrows[0:1, 2 * ZC:3 * ZC],
                                     func=AF.Exp, scale=-0.5).then_inc(s_ac.h, 1)
                    o["scalar"].append(zrow2)
                    s_ac.n += 1
                    v_zg = s_ac.n

                    def zdup(e, v_zr1=v_zr1, wdup=zdu_prev):
                        e.wait_ge(s_dv.h, v_zr1)
                        if wdup:
                            e.wait_ge(s_dv.h, wdup)
                        e.matmul(pzdu[:, :], ones_f[0:1, :], zrows[0:1, 0:ZC],
                                 start=True, stop=True).then_inc(s_pe.h, 1)
                    o["tensor"].append(zdup)
                    s_pe.n += 1
                    v_zd1 = s_pe.n

                    def zcp1(e, v_zd1=v_zd1):
                        e.wait_ge(s_pe.h, v_zd1)
                        e.tensor_copy(out=zrdm[:, 0:ZC], in_=pzdu[:, :]).then_inc(s_dv.h, 1)
                    o["vector"].append(zcp1)
                    s_dv.n += 1
                    v_zc1 = s_dv.n

                    def zdup2(e, v_zg=v_zg, v_zc1=v_zc1):
                        e.wait_ge(s_ac.h, v_zg)
                        e.wait_ge(s_dv.h, v_zc1)
                        e.matmul(pzdu[:, :], ones_f[0:1, :], zrows[0:1, 3 * ZC:4 * ZC],
                                 start=True, stop=True).then_inc(s_pe.h, 1)
                    o["tensor"].append(zdup2)
                    s_pe.n += 1
                    v_zd2 = s_pe.n

                    def zcp2(e, v_zd2=v_zd2):
                        e.wait_ge(s_pe.h, v_zd2)
                        e.tensor_copy(out=zrdm[:, ZC:2 * ZC], in_=pzdu[:, :]).then_inc(s_dv.h, 1)
                    o["vector"].append(zcp2)
                    s_dv.n += 1
                    v_zc2 = s_dv.n
                    zdu_prev = v_zc2

                    def zlnc(e, v_zc2=v_zc2, wrd=v_zlnrd_prev):
                        e.wait_ge(s_dv.h, v_zc2)
                        if wrd:
                            e.wait_ge(s_pe.h, wrd)
                        for k in range(KT_D):
                            ii = e.tensor_sub(out=zln[:, k * ZC:(k + 1) * ZC],
                                              in0=zx[:, k * ZC:(k + 1) * ZC], in1=zrdm[:, 0:ZC])
                        edge_dv(e, ii)
                        for k in range(KT_D):
                            ii = e.tensor_mul(out=zln[:, k * ZC:(k + 1) * ZC],
                                              in0=zln[:, k * ZC:(k + 1) * ZC], in1=zrdm[:, ZC:2 * ZC])
                        ii.then_inc(s_dv.h, 1)
                    o["vector"].append(zlnc)
                    s_dv.n += 1
                    v_zln = s_dv.n

                    for m in range(CTI):
                        view, wsem, v_w = wpanel(o, g_wz, m, 0, KT_D, F32R,
                                                 wait_cc=(cc_done.get("wz") if (oc == 0 and m == 0) else None))

                        def zmm(e, m=m, view=view, wsem=wsem, v_w=v_w, v_zln=v_zln, wev=evz_sl[m % 2]):
                            e.wait_ge(wsem.h, v_w)
                            e.wait_ge(s_dv.h, v_zln)
                            if wev:
                                e.wait_ge(s_ac.h, wev)
                            for k in range(KT_D):
                                ii = e.matmul(pz[m % 2][:, :], view[:, k * 128:(k + 1) * 128],
                                              zln[:, k * ZC:(k + 1) * ZC],
                                              start=(k == 0), stop=(k == KT_D - 1))
                            ii.then_inc(s_pe.h, 1)
                        o["tensor"].append(zmm)
                        s_pe.n += 1
                        wr_mark(1, s_pe.n)
                        v_zmm = s_pe.n

                        def ldy(e, m=m, oc0=oc0, wgm=gm_sl[m % 2],
                                wcc=(cc_done.get("y") if (oc == 0 and m == 0) else None)):
                            if wcc is not None:
                                e.wait_ge(wcc[0].h, wcc[1])
                            if wgm:
                                e.wait_ge(s_dv.h, wgm)
                            e.dma_start(out=ysl[:, (m % 2) * ZC:(m % 2 + 1) * ZC],
                                        in_=ysum_b[128 * m:128 * (m + 1), oc0:oc0 + ZC]
                                        ).then_inc(s_zy[m % 2].h, 16)
                        o["sync"].append(ldy)
                        s_zy[m % 2].n += 16
                        v_ly = s_zy[m % 2].n

                        def evz(e, m=m, v_zmm=v_zmm, wgm=gm_sl[m % 2]):
                            e.wait_ge(s_pe.h, v_zmm)
                            if wgm:
                                e.wait_ge(s_dv.h, wgm)
                            e.activation(out=zsil[:, (m % 2) * ZC:(m % 2 + 1) * ZC],
                                         in_=pz[m % 2][:, :], func=AF.Silu).then_inc(s_ac.h, 1)
                        o["scalar"].append(evz)
                        s_ac.n += 1
                        evz_sl[m % 2] = s_ac.n
                        v_evz = s_ac.n

                        def gmul(e, m=m, v_ly=v_ly, v_evz=v_evz,
                                 wgf=(v_gslrd_prev if oc > 0 else 0)):
                            e.wait_ge(s_zy[m % 2].h, v_ly)
                            e.wait_ge(s_ac.h, v_evz)
                            if wgf:
                                e.wait_ge(s_pe.h, wgf)
                            e.tensor_mul(out=gsl[:, m * ZC:(m + 1) * ZC],
                                         in0=ysl[:, (m % 2) * ZC:(m % 2 + 1) * ZC],
                                         in1=zsil[:, (m % 2) * ZC:(m % 2 + 1) * ZC]
                                         ).then_inc(s_dv.h, 1)
                        o["vector"].append(gmul)
                        s_dv.n += 1
                        gm_sl[m % 2] = s_dv.n
                    v_zlnrd_prev = s_pe.n
                    v_gsl_done = s_dv.n

                    for mo in range(MT_D):
                        pans = []
                        for kh in range(2):
                            view, wsem, v_w = wpanel(o, g_wop, mo, 16 * kh, 16, F32R,
                                                     wait_cc=(cc_done.get("wop")
                                                              if (oc == 0 and mo == 0 and kh == 0) else None))
                            pans.append((view, wsem, v_w))

                        def omm(e, pans=tuple(pans), mo=mo, v_gd=v_gsl_done, wx2=x2_sl[mo % 2]):
                            for _, wsem, v_w in pans:
                                e.wait_ge(wsem.h, v_w)
                            e.wait_ge(s_dv.h, v_gd)
                            if wx2:
                                e.wait_ge(s_dv.h, wx2)
                            for kh in range(2):
                                view = pans[kh][0]
                                for k in range(16):
                                    ii = e.matmul(po[mo % 2][:, :], view[:, k * 128:(k + 1) * 128],
                                                  gsl[:, (16 * kh + k) * ZC:(16 * kh + k + 1) * ZC],
                                                  start=(kh == 0 and k == 0), stop=(kh == 1 and k == 15))
                            ii.then_inc(s_pe.h, 1)
                        o["tensor"].append(omm)
                        s_pe.n += 1
                        wr_mark(2, s_pe.n)
                        v_omm = s_pe.n

                        def x2add(e, mo=mo, v_omm=v_omm, wst=st2[mo % 2]):
                            e.wait_ge(s_pe.h, v_omm)
                            if wst:
                                e.wait_ge(s_st[mo % 2].h, wst)
                            e.tensor_add(out=x2s[:, (mo % 2) * ZC:(mo % 2 + 1) * ZC],
                                         in0=po[mo % 2][:, :],
                                         in1=zx[:, mo * ZC:(mo + 1) * ZC]).then_inc(s_dv.h, 1)
                        o["vector"].append(x2add)
                        s_dv.n += 1
                        x2_sl[mo % 2] = s_dv.n
                        v_x2a = s_dv.n

                        def x2st(e, mo=mo, oc0=oc0, v_x2a=v_x2a):
                            e.wait_ge(s_dv.h, v_x2a)
                            e.dma_start(out=x2buf[128 * mo:128 * (mo + 1), oc0:oc0 + ZC],
                                        in_=x2s[:, (mo % 2) * ZC:(mo % 2 + 1) * ZC]
                                        ).then_inc(s_st[mo % 2].h, 16)
                        o["gpsimd"].append(x2st)
                        s_st[mo % 2].n += 16
                        st2[mo % 2] = s_st[mo % 2].n

                    v_gslrd_prev = s_pe.n
                    v_x2_prev = s_dv.n

                if "x2" in dbg:
                    def dmpx2(e, w0=s_st[0].n, w1=s_st[1].n):
                        e.wait_ge(s_st[0].h, w0)
                        e.wait_ge(s_st[1].h, w1)
                        e.dma_start(out=dbg["x2"], in_=x2buf).then_inc(s_dbg.h, 16)
                    o["gpsimd"].append(dmpx2)
                    s_dbg.n += 16
                run_block(o)

        if _NPHASE >= 4:
            emit_zgate()

        if _NPHASE >= 5:
            emit_ffn(x2buf, out, "f2",
                     gates=[(s_out.h, s_out.n), (s_st[0].h, s_st[0].n), (s_st[1].h, s_st[1].n),
                            (s_ys[0].h, s_ys[0].n), (s_ys[1].h, s_ys[1].n),
                            (s_zy[0].h, s_zy[0].n), (s_zy[1].h, s_zy[1].n), (s_dbg.h, s_dbg.n)])

        o = new_ops()
        if _NPHASE < 5:
            zro = st.enter_context(sb("zro", [128, TH], F32))

            def fin_v(e):
                e.memset(zro[:, :], 0.0).then_inc(s_dv.h, 1)
            o["vector"].append(fin_v)
            s_dv.n += 1
            vz = s_dv.n
            v0 = s_out.n

            def fin_g(e):
                e.wait_ge(s_dv.h, vz)
                for k in range(MT_D):
                    e.dma_start(out=out[128 * k:128 * (k + 1), :], in_=zro[:, :]).then_inc(s_out.h, 16)
                e.wait_ge(s_out.h, v0 + 16 * MT_D)
            o["gpsimd"].append(fin_g)
            s_out.n += 16 * MT_D
        else:
            def fin_g(e, v0=s_out.n, w0=s_st[0].n, w1=s_st[1].n):
                e.wait_ge(s_st[0].h, w0)
                e.wait_ge(s_st[1].h, w1)
                if v0:
                    e.wait_ge(s_out.h, v0)
            o["gpsimd"].append(fin_g)
        run_block(o)

    return nc


def _host_layout(inputs):
    import ml_dtypes
    _bf16 = ml_dtypes.bfloat16
    x_in = np.asarray(inputs["L0_attn_hidden_states"], np.float32)
    ipw = np.asarray(inputs["L0_attn_in_proj_w"], np.float32)
    dtin = np.asarray(inputs["L0_attn_dt_in_proj_w"], np.float32)
    dtp = np.asarray(inputs["L0_attn_dt_proj_w"], np.float32)
    dtb_v = np.asarray(inputs["L0_attn_dt_proj_b"], np.float32)
    cw = np.asarray(inputs["L0_attn_conv_w"], np.float32)
    cb = np.asarray(inputs["L0_attn_conv_b"], np.float32)
    alog_v = np.asarray(inputs["L0_attn_A_log"], np.float32)
    dv = np.asarray(inputs["L0_attn_D"], np.float32)
    opw = np.asarray(inputs["L0_attn_out_proj_w"], np.float32)
    wfc = np.asarray(inputs["L0_ffn_w_fc"], np.float32)
    wpj = np.asarray(inputs["L0_ffn_w_proj"], np.float32)

    wfcT = np.ascontiguousarray(wfc.T)
    wpjT = np.ascontiguousarray(wpj.T)
    wzT = np.ascontiguousarray(ipw[:DI, :].T)
    wopT = np.ascontiguousarray(opw.T)
    wdtinT = np.ascontiguousarray(dtin.T)
    wdtpT = np.ascontiguousarray(dtp.T)

    wmv = []
    for j in (0, 1):
        cols = np.empty([D, NPF], np.float32)
        cols[:, XO:XO + DXB] = ipw[DI:DI + DXB, :].T
        bidx = np.array([DI + DXB + bh * 16 + 8 * j + nl for bh in range(BH) for nl in range(NH)])
        cols[:, BO:BO + BH * NH] = ipw[bidx, :].T
        cidx = np.array([DI + 2 * DXB + h * 16 + 8 * j + nl for h in range(H) for nl in range(NH)])
        cols[:, CO:CO + H * NH] = ipw[cidx, :].T
        wmv.append(np.ascontiguousarray(cols))

    alogs = []
    p = np.arange(128)
    for j in (0, 1):
        a = np.empty([128, 256], np.float32)
        for q in range(256):
            a[:, q] = alog_v[16 * q + p // 8, 8 * j + p % 8]
        alogs.append(np.ascontiguousarray(a))

    def tile32(v):
        return np.ascontiguousarray(v.reshape(32, 128).T)

    dvmask = []
    for j in range(2):
        dm = np.zeros([128, 64], np.float32)
        dm[:, 32 * j:32 * j + 32] = tile32(dv)
        dvmask.append(dm)

    pc = np.zeros([64, KC * 16 * 128], np.float32)
    for k in range(KC):
        for ct in range(32):
            c = 128 * ct + p
            srcloc = (p // 64) * 16 + p % 16
            pc[32 * (ct % 2) + srcloc, (k * 16 + ct // 2) * 128 + p] = cw[c, k]

    psp = np.zeros([128, 8 * 128], np.float32)
    pwm = np.zeros([128, 8 * 128], np.float32)
    for kk in range(8):
        psp[16 * kk + p // 8, kk * 128 + p] = 1.0
        pwm[16 * kk + p // 8, kk * 128 + p] = 1.0
    pb = np.zeros([128, 16 * 128], np.float32)
    for kk in range(16):
        pb[8 * kk + p % 8, kk * 128 + p] = 1.0
    pr = np.zeros([128, 8 * 128], np.float32)
    for kk in range(8):
        pr[p, kk * 128 + 16 * kk + p // 8] = 1.0

    ident = np.eye(128, dtype=np.float32)
    onesm = np.ones([128, 128], np.float32)

    maps = []
    for core in range(8):
        b, j = core // 2, core % 2
        x0c = np.ascontiguousarray(x_in[b, j * TH:(j + 1) * TH, :].T)
        m = {
            "x0": x0c,
            "wm_sh": np.ascontiguousarray(wmv[j][512 * b:512 * (b + 1), :]),
            "wz_sh": np.ascontiguousarray(wzT[256 * core:256 * (core + 1), :]),
            "wop_sh": np.ascontiguousarray(wopT[512 * core:512 * (core + 1), :]),
            "wfc_sh": np.ascontiguousarray(wfcT[256 * core:256 * (core + 1), :]),
            "wpj_sh": np.ascontiguousarray(wpjT[1024 * core:1024 * (core + 1), :]),
            "wdtin": wdtinT, "wdtp": wdtpT,
            "alog": alogs[j],
            "pconv": pc, "convb": tile32(cb), "dvec": dvmask[j], "dtb": tile32(dtb_v),
            "p_sp": psp, "p_w": pwm.astype(_bf16), "p_b": pb.astype(_bf16), "p_r": pr,
            "ones_i": onesm, "ident": ident,
        }
        maps.append(m)
    return maps


def kernel(**inputs):
    maps = _host_layout(inputs)
    nc = build_program()
    res = run_bass_kernel_spmd(nc, maps, core_ids=list(range(8)))
    outa = np.empty([B_, TF, D], np.float32)
    for core in range(8):
        b, j = core // 2, core % 2
        outa[b, j * TH:(j + 1) * TH, :] = np.asarray(res.results[core]["out"]).T
    return outa
